# version 19
# speedup vs baseline: 1.0783x; 1.0189x over previous
import os
os.environ.setdefault("TRN_ALLOW_ICI_ALL_TO_ALL", "1")

import numpy as np
from contextlib import ExitStack
import concourse.bass as bass
import concourse.mybir as mybir
from concourse.bass_utils import run_bass_kernel_spmd

F32 = mybir.dt.float32
BF16 = mybir.dt.bfloat16
AF = mybir.ActivationFunctionType
OP = mybir.AluOpType
AX = mybir.AxisListType

NCORES = 8
D = 1024
T = 2048
KT = D // 128
TC = T // 512

ENGINES = ("tensor", "vector", "scalar", "gpsimd", "sync")
DMA_QUEUES = ("sync", "scalar", "gpsimd")
DMA_SLOTS = 8


class _Op:
    __slots__ = ("engine", "thunk", "waits", "incs", "tag", "scope")

    def __init__(self, engine, thunk, waits, incs, tag, scope=None):
        self.engine, self.thunk, self.waits, self.incs, self.tag = \
            engine, thunk, waits, incs, tag
        self.scope = scope


class FW:
    def __init__(self, nc, ctx: ExitStack):
        self.nc = nc
        self.ctx = ctx
        self.ops = []
        self.csem = {e: ctx.enter_context(nc.semaphore(f"c_{e}"))
                     for e in ENGINES}
        self.dsem = {}
        self.dcum = {}
        for q in DMA_QUEUES:
            for s in range(DMA_SLOTS):
                self.dsem[(q, s)] = ctx.enter_context(
                    nc.semaphore(f"d_{q}_{s}"))
                self.dcum[(q, s)] = 0
        self.dnext = {q: 0 for q in DMA_QUEUES}
        self.ccount = {e: 0 for e in ENGINES}
        self.last_writer = {}
        self.readers = {}
        self.waited = {e: {} for e in ENGINES}
        self.finalized = False
        self.scope = None

    def _wait_tok(self, eng, tok, waits):
        if isinstance(tok, list):
            for t in tok:
                self._wait_tok(eng, t, waits)
            return
        kind, key, cnt = tok
        if cnt <= self.waited[eng].get((kind, key), 0):
            return
        sem = self.csem[key] if kind == "c" else self.dsem[key]
        waits.append((sem, cnt))
        self.waited[eng][(kind, key)] = cnt

    def op(self, engine, thunk, reads=(), writes=(), dma=False, n_dma=1,
           tag="", extra_waits=()):
        waits = list(extra_waits)
        for k in reads:
            lw = self.last_writer.get(k)
            if lw is not None:
                self._wait_tok(engine, lw, waits)
        for k in writes:
            lw = self.last_writer.get(k)
            if lw is not None:
                self._wait_tok(engine, lw, waits)
            for tok in self.readers.get(k, {}).values():
                self._wait_tok(engine, tok, waits)
        if dma:
            assert engine in DMA_QUEUES
            incs, toks = [], []
            for _ in range(n_dma):
                s = self.dnext[engine]
                self.dnext[engine] = (s + 1) % DMA_SLOTS
                key = (engine, s)
                if self.dcum[key]:
                    self._wait_tok(engine, ("d", key, self.dcum[key]), waits)
                self.dcum[key] += 16
                incs.append((self.dsem[key], 16))
                toks.append(("d", key, self.dcum[key]))
            wtoks = toks
        else:
            self.ccount[engine] += 1
            wtoks = [("c", engine, self.ccount[engine])]
            incs = None
        for k in writes:
            self.last_writer[k] = wtoks[-1] if len(wtoks) == 1 \
                else list(wtoks)
            self.readers[k] = {}
        for k in reads:
            rd = self.readers.setdefault(k, {})
            for t in wtoks:
                rd[(t[0], t[1])] = t
        self.ops.append(_Op(engine, thunk, waits, incs, tag, self.scope))
        return wtoks[-1]

    def alias(self, newkeys, oldkeys, engine="vector"):
        self.op(engine, lambda e: e.engine_nop(),
                writes=list(newkeys) + list(oldkeys), tag="alias")

    def raw_op(self, engine, thunk, tag=""):
        self.ops.append(_Op(engine, thunk, [], "raw", tag, self.scope))

    def barrier(self):
        pre_c = dict(self.ccount)
        pre_d = dict(self.dcum)
        hub = "vector"
        waits = []
        for e2 in ENGINES:
            if e2 != hub and pre_c[e2]:
                self._wait_tok(hub, ("c", e2, pre_c[e2]), waits)
        for key, cum in pre_d.items():
            if cum:
                self._wait_tok(hub, ("d", key, cum), waits)
        self.ccount[hub] += 1
        self.ops.append(_Op(hub, lambda e: e.nop(), waits, None,
                            "bar_hub"))
        hubcnt = self.ccount[hub]
        for e2 in ENGINES:
            if e2 == hub:
                continue
            w2 = []
            self._wait_tok(e2, ("c", hub, hubcnt), w2)
            self.ccount[e2] += 1
            self.ops.append(_Op(e2, lambda e: e.nop(), w2, None,
                                "bar_spk"))
        for e2 in ENGINES:
            w = self.waited[e2]
            for e3 in ENGINES:
                w[("c", e3)] = max(w.get(("c", e3), 0), pre_c[e3])
            w[("c", hub)] = max(w.get(("c", hub), 0), hubcnt)
            for key, cum in pre_d.items():
                w[("d", key)] = max(w.get(("d", key), 0), cum)

    def sink(self, engine="sync"):
        def thunk(e):
            return e.nop()
        waits = []
        for key, cum in self.dcum.items():
            if cum:
                self._wait_tok(engine, ("d", key, cum), waits)
        self.ccount[engine] += 1
        self.ops.append(_Op(engine, thunk, waits, None, "sink"))

    def finalize(self):
        assert not self.finalized
        self.finalized = True
        per_engine = {e: [] for e in ENGINES}
        for op in self.ops:
            per_engine[op.engine].append(op)
        blk = self.ctx.enter_context(self.nc.Block())

        nc = self.nc

        def body(ops, ename):
            def run(e):
                from contextlib import nullcontext
                for op in ops:
                    sc = nc.named_scope(op.scope) if op.scope else \
                        nullcontext()
                    with sc:
                        for sem, val in op.waits:
                            e.wait_ge(sem, val)
                        if op.incs == "raw":
                            op.thunk(e)
                            continue
                        last = op.thunk(e)
                        if op.incs is not None:
                            assert isinstance(last, (list, tuple)) and \
                                len(last) == len(op.incs), \
                                f"dma op {op.tag}"
                            for ins, (sem, v) in zip(last, op.incs):
                                ins.then_inc(sem, v)
                        else:
                            assert last is not None, \
                                f"op {op.tag} returned None"
                            last.then_inc(self.csem[ename], 1)
            return run

        for ename, emeth in (("sync", blk.sync), ("tensor", blk.tensor),
                             ("vector", blk.vector), ("scalar", blk.scalar),
                             ("gpsimd", blk.gpsimd)):
            if per_engine[ename]:
                emeth(body(per_engine[ename], ename))


def _f(a):
    return np.ascontiguousarray(a, dtype=np.float32)


def lhsT_prep(w_out_in):
    din = w_out_in.shape[1]
    dout = w_out_in.shape[0]
    kt = din // 128
    return _f(w_out_in.T.reshape(kt, 128, dout).transpose(1, 0, 2))


def consts_host():
    return {
        "c_onescol": _f(np.ones((128, 1))),
        "c_onesrow": _f(np.ones((1, 128))),
        "c_onespair": _f(np.ones((64, 128))),
        "c_ident": _f(np.eye(128)),
    }


class Buf:
    def __init__(self, nc, ctx):
        self.nc, self.ctx, self.t = nc, ctx, {}
        self.scopes = []
        self.uid = 0
        self.dead = {}

    def push_scope(self):
        self.uid += 1
        self.scopes.append(([], ExitStack(), f"s{self.uid}_"))

    def pop_scope(self):
        names, stack, _ = self.scopes.pop()
        stack.close()
        for n in names:
            self.dead[n] = self.t.pop(n)

    def S(self, name, shape, dt=F32):
        if name not in self.t:
            if self.scopes:
                names, stack, pre = self.scopes[-1]
                self.t[name] = stack.enter_context(
                    self.nc.sbuf_tensor("sb_" + pre + name, shape, dt))
                names.append(name)
            else:
                self.t[name] = self.ctx.enter_context(
                    self.nc.sbuf_tensor("sb_" + name, shape, dt))
        return self.t[name]

    def P(self, name, shape, dt=F32):
        if name not in self.t:
            self.t[name] = self.ctx.enter_context(
                self.nc.psum_tensor("ps_" + name, shape, dt))
        return self.t[name]

    def has(self, name):
        return name in self.t

    def __getitem__(self, name):
        if name in self.t:
            return self.t[name]
        return self.dead[name]


class Prog:

    def __init__(self):
        self.nc = bass.Bass("TRN2", target_bir_lowering=False)
        self.ctx = ExitStack()
        self.fw = FW(self.nc, self.ctx)
        self.b = Buf(self.nc, self.ctx)
        self.dram = {}
        self.psA = self.b.P("A", [128, 2048], F32)
        self.psB = self.b.P("B", [128, 2048], F32)
        self.x_src_ap = None
        self.x_dst_ap = None

    def begin_block(self, name=None):
        self.fw.scope = name
        self.b.push_scope()

    def end_block(self):
        self.fw.barrier()
        self.b.pop_scope()
        self.fw.scope = "glue"

    def store_x_to(self, name):
        ap = self.dscratch(name, [D, T]).rearrange("(k p) t -> p k t", p=128)
        sbx = self.b["x"]
        h = KT // 2
        self.fw.op("sync",
                   lambda e: [e.dma_start(out=ap[:, 0:h], in_=sbx[:, 0:h])],
                   reads=[f"x:{k}" for k in range(h)], dma=True,
                   tag="st_" + name)
        self.fw.op("gpsimd",
                   lambda e: [e.dma_start(out=ap[:, h:KT],
                                          in_=sbx[:, h:KT])],
                   reads=[f"x:{k}" for k in range(h, KT)], dma=True,
                   tag="st2_" + name)

    def reload_x_from(self, ap, extra_waits=()):
        sbx = self.b["x"]
        h = KT // 2
        if len(ap.shape) == 4:
            def th(e, lo, hi):
                return [e.dma_start(
                    out=sbx[:, k, :].rearrange("p (r c) -> p r c",
                                               r=ap.shape[2]),
                    in_=ap[:, k]) for k in range(lo, hi)]
            self.fw.op("sync", lambda e: th(e, 0, h),
                       writes=[f"x:{k}" for k in range(h)], dma=True,
                       n_dma=h, extra_waits=list(extra_waits),
                       tag="reload_x")
            self.fw.op("gpsimd", lambda e: th(e, h, KT),
                       writes=[f"x:{k}" for k in range(h, KT)], dma=True,
                       n_dma=KT - h, extra_waits=list(extra_waits),
                       tag="reload_x2")
            return
        self.fw.op("sync",
                   lambda e: [e.dma_start(out=sbx[:, 0:h],
                                          in_=ap[:, 0:h])],
                   writes=[f"x:{k}" for k in range(h)], dma=True,
                   extra_waits=list(extra_waits), tag="reload_x")
        self.fw.op("gpsimd",
                   lambda e: [e.dma_start(out=sbx[:, h:KT],
                                          in_=ap[:, h:KT])],
                   writes=[f"x:{k}" for k in range(h, KT)], dma=True,
                   extra_waits=list(extra_waits), tag="reload_x2")

    def set_xio(self, src, dst):
        if isinstance(src, str):
            src = (self.din(src, [D, T]) if src == "x_in"
                   else self.dscratch(src, [D, T])) \
                .rearrange("(k p) t -> p k t", p=128)
        if isinstance(dst, str):
            dst = (self.dout(dst, [D, T]) if dst == "x_out"
                   else self.dscratch(dst, [D, T])) \
                .rearrange("(k p) t -> p k t", p=128)
        self.x_src_ap = src
        self.x_dst_ap = dst

    def din(self, name, shape):
        if name not in self.dram:
            self.dram[name] = self.nc.dram_tensor(
                name, list(shape), F32, kind="ExternalInput").ap()
        return self.dram[name]

    def dout(self, name, shape, dt=F32):
        if name not in self.dram:
            self.dram[name] = self.nc.dram_tensor(
                name, list(shape), dt, kind="ExternalOutput").ap()
        return self.dram[name]

    def dscratch(self, name, shape, dt=F32):
        if name not in self.dram:
            self.dram[name] = self.nc.dram_tensor(
                name, list(shape), dt, kind="Internal").ap()
        return self.dram[name]

    def load(self, name, dram_shape, sb_shape=None, dt=BF16, queue="gpsimd",
             rearr=None, sb_name=None):
        ap = self.din(name, dram_shape)
        if rearr is not None:
            ap = ap.rearrange(rearr[0], **rearr[1])
        sb_name = sb_name or name
        dst = self.b.S(sb_name, sb_shape or list(dram_shape), dt)
        q = queue if dt != F32 else "sync"
        self.fw.op(q, lambda e: [e.dma_start(out=dst[:], in_=ap)],
                   writes=[sb_name], dma=True, tag="ld_" + sb_name)
        return dst

    def load_consts(self):
        self.load("c_onescol", [128, 1], dt=F32)
        self.load("c_onesrow", [1, 128], dt=F32)
        self.load("c_onespair", [64, 128], dt=F32)
        self.load("c_ident", [128, 128], dt=F32)
        ap = self.din("c_onescol", [128, 1])
        dst = self.b.S("c_onescolb", [128, 1], BF16)
        self.fw.op("gpsimd", lambda e: [e.dma_start(out=dst[:], in_=ap)],
                   writes=["c_onescolb"], dma=True, tag="ld_onescolb")
        ap2 = self.din("c_ident", [128, 128])
        dst2 = self.b.S("c_identb", [128, 128], BF16)
        self.fw.op("gpsimd", lambda e: [e.dma_start(out=dst2[:], in_=ap2)],
                   writes=["c_identb"], dma=True, tag="ld_identb")
        ap3 = self.din("c_onesrow", [1, 128])
        dst3 = self.b.S("c_onesrowb", [1, 128], BF16)
        self.fw.op("gpsimd", lambda e: [e.dma_start(out=dst3[:], in_=ap3)],
                   writes=["c_onesrowb"], dma=True, tag="ld_onesrowb")

    def load_x(self):
        ap = self.din("x_in", [D, T]).rearrange("(k p) t -> p k t", p=128)
        sbx = self.b.S("x", [128, KT, T], F32)
        self.fw.op("sync", lambda e: [e.dma_start(out=sbx[:], in_=ap)],
                   writes=[f"x:{k}" for k in range(KT)], dma=True,
                   tag="ld_x")
        return sbx

    def store_x(self):
        ap = self.dout("x_out", [D, T]).rearrange("(k p) t -> p k t", p=128)
        sbx = self.b["x"]
        self.fw.op("sync", lambda e: [e.dma_start(out=ap, in_=sbx[:])],
                   reads=[f"x:{k}" for k in range(KT)], dma=True,
                   tag="st_x")

    def finish(self):
        self.fw.sink()
        self.fw.finalize()
        self.ctx.close()
        return self.nc


def _akeys(lo=0, hi=4):
    return [f"A{i}" for i in range(lo, hi)]


def _bkeys(lo=0, hi=4):
    return [f"B{i}" for i in range(lo, hi)]


def emit_ln(p, uid, out="xln", gamma=None, beta=None, eps=1e-5):
    fw, b = p.fw, p.b
    psA, psB = p.psA, p.psB
    sbx = b["x"]
    xln = b.S(out, [128, KT, T], BF16)
    xsq = b.S("lnscr", [128, KT, T], BF16)
    tmid = xsq
    rowsA = b.S("rowsA", [128, T], F32)
    rowsB = b.S("rowsB", [128, T], F32)
    murow = rowsA[0:1, :]
    sumrow = rowsB[0:1, :]
    varrow = rowsB[0:1, :]
    rsrow = rowsB[0:1, :]
    onescol, onescolb, onesrow = b["c_onescol"], b["c_onescolb"], \
        b["c_onesrow"]

    for k in range(KT):
        fw.op("scalar",
              lambda e, k=k: e.activation(xsq[:, k, :], sbx[:, k, :],
                                          AF.Square),
              reads=[f"x:{k}"], writes=[f"lnscr:{k}"], tag=f"{uid}sq{k}")

    def sums(e):
        last = None
        for k in range(KT):
            for c in range(TC):
                sl = slice(c * 512, (c + 1) * 512)
                e.matmul(psA[0:1, sl], onescol[:, :], sbx[:, k, sl],
                         start=(k == 0), stop=(k == KT - 1))
                last = e.matmul(psB[0:1, sl], onescolb[:, :], xsq[:, k, sl],
                                start=(k == 0), stop=(k == KT - 1))
        return last
    fw.op("tensor", sums,
          reads=[f"x:{k}" for k in range(KT)] +
                [f"lnscr:{k}" for k in range(KT)] +
                ["c_onescol", "c_onescolb"],
          writes=_akeys() + _bkeys(), tag=uid + "sums")

    fw.op("vector",
          lambda e: e.tensor_scalar(murow, psA[0:1, :],
                                    1.0 / 1024.0, None, OP.mult),
          reads=_akeys(), writes=["rowA0"], tag=uid + "mu")
    fw.op("vector",
          lambda e: e.tensor_tensor(sumrow, murow, murow, OP.mult),
          reads=["rowA0"], writes=["rowB0"], tag=uid + "mu2")
    fw.op("vector",
          lambda e: e.scalar_tensor_tensor(varrow, psB[0:1, :],
                                           1.0 / 1024.0, varrow,
                                           OP.mult, OP.subtract),
          reads=_bkeys() + ["rowB0"], writes=["rowB0"], tag=uid + "var")
    fw.op("vector",
          lambda e: e.tensor_scalar(varrow, varrow, float(eps), None,
                                    OP.add),
          reads=["rowB0"], writes=["rowB0"], tag=uid + "eps")
    fw.op("scalar",
          lambda e: e.activation(rsrow, varrow, AF.Ln),
          reads=["rowB0"], writes=["rowB0"], tag=uid + "sd")
    fw.op("scalar",
          lambda e: e.activation(rsrow, rsrow, AF.Exp, scale=-0.5),
          reads=["rowB0"], writes=["rowB0"], tag=uid + "rs")

    onespair = b["c_onespair"]

    def bcast(srcap, srckey, lhs):
        def f(e):
            last = None
            for c in range(TC):
                sl = slice(c * 512, (c + 1) * 512)
                last = e.matmul(psB[:, sl], lhs, srcap[:, sl],
                                start=True, stop=True)
            return last
        fw.op("tensor", f, reads=[srckey, "c_onespair", "c_onesrow"],
              writes=_bkeys(), tag=uid + "bc_" + srckey)

    bcast(murow, "rowA0", onesrow[:, :])
    for k in range(KT):
        fw.op("vector",
              lambda e, k=k: e.tensor_tensor(tmid[:, k, :], sbx[:, k, :],
                                             psB[:, :], OP.subtract),
              reads=[f"x:{k}"] + _bkeys(), writes=[f"lnscr:{k}"],
              tag=f"{uid}sub{k}")
    bcast(rsrow, "rowB0", onesrow[:, :])
    for k in range(KT):
        fw.op("vector",
              lambda e, k=k: e.tensor_tensor(xln[:, k, :], tmid[:, k, :],
                                             psB[:, :], OP.mult),
              reads=[f"lnscr:{k}"] + _bkeys(), writes=[f"{out}:{k}"],
              tag=f"{uid}mul{k}")
    if gamma is not None:
        g, bt = b[gamma], b[beta]
        for k in range(KT):
            fw.op("vector",
                  lambda e, k=k: e.tensor_scalar(xln[:, k, :], xln[:, k, :],
                                                 g[:, k:k + 1],
                                                 bt[:, k:k + 1],
                                                 OP.mult, OP.add),
                  reads=[f"{out}:{k}", gamma, beta],
                  writes=[f"{out}:{k}"], tag=f"{uid}aff{k}")


def host_prep_l1ffn(inp):
    out = {}
    out["w_l1f_router"] = lhsT_prep(inp["L1_ffn_router_w"])
    out["w_l1f_rbias"] = _f(inp["L1_ffn_router_bias"].reshape(32, 1))
    gw, uw, dw = (inp["L1_ffn_gate_w"], inp["L1_ffn_up_w"],
                  inp["L1_ffn_down_w"])
    sg, su, sd = (inp["L1_ffn_sgate_w"], inp["L1_ffn_sup_w"],
                  inp["L1_ffn_sdown_w"])
    gus = []
    for e in range(32):
        g = gw[e].T.reshape(KT, 128, 128)
        u = uw[e].T.reshape(KT, 128, 128)
        gus.append(np.stack([g, u], axis=0))
    for o4 in range(4):
        g = sg[o4 * 128:(o4 + 1) * 128].T.reshape(KT, 128, 128)
        u = su[o4 * 128:(o4 + 1) * 128].T.reshape(KT, 128, 128)
        gus.append(np.stack([g, u], axis=0))
    gus = np.stack(gus, axis=0)
    out["w_l1f_gu"] = _f(gus.transpose(0, 3, 1, 2, 4))
    downs = [dw[e].T for e in range(32)]
    sdT = sd.T
    for o4 in range(4):
        downs.append(sdT[o4 * 128:(o4 + 1) * 128])
    big = np.stack(downs, axis=0)
    out["w_l1f_d"] = _f(big.reshape(36, 128, KT, 128)
                        .transpose(2, 1, 0, 3))
    return out


def emit_l1ffn(p):
    fw, b = p.fw, p.b
    psA, psB = p.psA, p.psB
    xln = b["xln"]
    xkeys = [f"xln:{k}" for k in range(KT)]
    p.load("w_l1f_router", [128, KT, 32], dt=F32)
    p.load("w_l1f_rbias", [32, 1], dt=F32)
    gu_dram = p.din("w_l1f_gu", [36, 128, 2, KT, 128])
    d_dram = p.din("w_l1f_d", [KT, 128, 36, 128])
    hwstage = p.dscratch("l1f_hwstage", [36, 128, T], BF16)
    cwstage = p.dscratch("l1f_cwstage", [32, T], BF16)
    wrt, rbias = b["w_l1f_router"], b["w_l1f_rbias"]
    ident = b["c_ident"]
    sbx = b["x"]
    xfkeys = [f"x:{k}" for k in range(KT)]
    murow = b["rowsA"][0:1, :]

    nrw = b.S("l1f_nrw", [1, 32], F32)

    def rsum(e):
        last = None
        for k in range(KT):
            last = e.matmul(psB[0:1, 0:32], b["c_onescol"][:, :],
                            wrt[:, k, :], start=(k == 0),
                            stop=(k == KT - 1))
        return last
    fw.op("tensor", rsum, reads=["w_l1f_router", "c_onescol"],
          writes=["B0"], tag="l1f_rws")
    fw.op("vector",
          lambda e: e.tensor_scalar(nrw[:, :], psB[0:1, 0:32], -1.0, None,
                                    OP.mult),
          reads=["B0"], writes=["l1f_nrw"], tag="l1f_nrw")
    onespair = b["c_onespair"]

    _rowsB = b["rowsB"]
    _onesrow = b["c_onesrow"]

    def rebc(e):
        last = None
        for c in range(TC):
            sl = slice(c * 512, (c + 1) * 512)
            last = e.matmul(psB[:, sl], _onesrow[:, :],
                            _rowsB[0:1, sl], start=True, stop=True)
        return last
    fw.op("tensor", rebc, reads=["rowB0", "c_onesrow"], writes=_bkeys(),
          tag="l1f_rebc")

    def rmm(e):
        last = None
        for k in range(KT):
            for c in range(TC):
                sl = slice(c * 512, (c + 1) * 512)
                e.matmul(psA[0:32, sl], wrt[:, k, :], sbx[:, k, sl],
                         start=(k == 0), stop=False)
        for c in range(TC):
            sl = slice(c * 512, (c + 1) * 512)
            last = e.matmul(psA[0:32, sl], nrw[:, :], murow[:, sl],
                            start=False, stop=True)
        return last
    fw.op("tensor", rmm,
          reads=["w_l1f_router", "l1f_nrw", "rowA0"] + xfkeys,
          writes=_akeys(), tag="l1f_rmm")
    lgT = b.S("l1f_lgT", [32, T], F32)
    fw.op("scalar", lambda e: e.copy(lgT[:, :], psA[0:32, :]),
          reads=_akeys(), writes=["l1f_lgT"], tag="l1f_lg0")
    fw.op("vector",
          lambda e: e.tensor_tensor(lgT[:, :], lgT[:, :], psB[0:32, :],
                                    OP.mult),
          reads=["l1f_lgT"] + _bkeys(), writes=["l1f_lgT"], tag="l1f_lgm")
    fw.op("vector",
          lambda e: e.tensor_scalar(lgT[:, :], lgT[:, :], rbias[:, :],
                                    None, OP.add),
          reads=["l1f_lgT", "w_l1f_rbias"], writes=["l1f_lgT"],
          tag="l1f_lgb")

    lg = b.S("l1f_lg", [128, 16, 32], F32)
    for half in range(2):
        def trs(e, half=half):
            last = None
            for c in range(8):
                cc = half * 8 + c
                last = e.transpose(psB[:, c * 32:(c + 1) * 32],
                                   lgT[:, cc * 128:(cc + 1) * 128],
                                   ident[0:32, 0:32])
            return last
        fw.op("tensor", trs, reads=["l1f_lgT", "c_ident"], writes=["B0"],
              tag=f"l1f_tr{half}")
        fw.op("vector",
              lambda e, half=half: e.tensor_copy(
                  lg[:, half * 8:(half + 1) * 8, :],
                  psB[:, 0:256].rearrange("p (c e) -> p c e", c=8)),
              reads=["B0"], writes=["l1f_lg"], tag=f"l1f_lgcp{half}")

    ex = b.S("l1f_ex", [128, 16, 32], F32)
    sm = b.S("l1f_sm", [128, 16], F32)
    t8 = b.S("l1f_t8", [128, 16, 8], F32)
    cw = b.S("l1f_cw", [128, 16, 32], F32)
    t4s = b.S("l1f_t4s", [128, 16], F32)
    msk = lg
    fw.op("scalar", lambda e: e.activation(ex[:, :, :], lg[:, :, :], AF.Exp),
          reads=["l1f_lg"], writes=["l1f_ex"], tag="l1f_exp")
    fw.op("vector", lambda e: e.tensor_reduce(sm[:, :], ex[:, :, :], AX.X,
                                              OP.add),
          reads=["l1f_ex"], writes=["l1f_sm"], tag="l1f_sum")
    fw.op("vector", lambda e: e.reciprocal(sm[:, :], sm[:, :]),
          reads=["l1f_sm"], writes=["l1f_sm"], tag="l1f_rc")
    fw.op("vector",
          lambda e: e.tensor_tensor(ex[:, :, :], ex[:, :, :],
                                    sm[:, :].unsqueeze(2)
                                    .to_broadcast([128, 16, 32]), OP.mult),
          reads=["l1f_ex", "l1f_sm"], writes=["l1f_ex"], tag="l1f_sc")

    def max8s(e):
        last = None
        for c in range(16):
            last = e.max(out=t8[:, c, :], in_=ex[:, c, :])
            e.drain()
        return last
    fw.op("vector", max8s, reads=["l1f_ex"], writes=["l1f_t8"],
          tag="l1f_max8")
    fw.op("vector",
          lambda e: e.tensor_reduce(t4s[:, :], t8[:, :, 0:4], AX.X, OP.add),
          reads=["l1f_t8"], writes=["l1f_t4s"], tag="l1f_t4s")
    fw.op("vector",
          lambda e: e.tensor_scalar(t4s[:, :], t4s[:, :], 1e-12, None,
                                    OP.max),
          reads=["l1f_t4s"], writes=["l1f_t4s"], tag="l1f_clip")
    fw.op("vector", lambda e: e.reciprocal(t4s[:, :], t4s[:, :]),
          reads=["l1f_t4s"], writes=["l1f_t4s"], tag="l1f_rc2")
    fw.op("vector",
          lambda e: e.tensor_tensor(msk[:, :, :], ex[:, :, :],
                                    t8[:, :, 3:4]
                                    .to_broadcast([128, 16, 32]), OP.is_ge),
          reads=["l1f_ex", "l1f_t8"], writes=["l1f_lg"], tag="l1f_lg")
    fw.op("vector",
          lambda e: e.tensor_tensor(cw[:, :, :], ex[:, :, :], msk[:, :, :],
                                    OP.mult),
          reads=["l1f_ex", "l1f_lg"], writes=["l1f_cw"], tag="l1f_cw1")
    fw.op("vector",
          lambda e: e.tensor_tensor(cw[:, :, :], cw[:, :, :],
                                    t4s[:, :].unsqueeze(2)
                                    .to_broadcast([128, 16, 32]), OP.mult),
          reads=["l1f_cw", "l1f_t4s"], writes=["l1f_cw"], tag="l1f_cw2")

    def trc(e):
        last = None
        for c in range(16):
            last = e.transpose(psA[0:32, c * 128:(c + 1) * 128],
                               cw[:, c, :], ident[:, :])
        return last
    fw.op("tensor", trc, reads=["l1f_cw", "c_ident"], writes=_akeys(),
          tag="l1f_trc")
    cwT = lgT
    fw.op("vector", lambda e: e.tensor_copy(cwT[0:32, :], psA[0:32, :]),
          reads=_akeys(), writes=["l1f_lgT"], tag="l1f_cwTcp")
    fw.op("gpsimd", lambda e: [e.dma_start(out=cwstage, in_=cwT[0:32, :])],
          reads=["l1f_lgT"], writes=["cwstage"], dma=True,
          tag="l1f_cwst")

    emit_moe_passes(p, "l1f", 36, 32,
                    lambda e_i: e_i if e_i < 32 else None, gu_dram, d_dram,
                    cwstage, hwstage)


def emit_moe_passes(p, pre, n_virt, n_route, cw_row, gu_dram, d_dram,
                    cwstage, hwstage, tcw=512, pool_elems=23040,
                    wd_in_xln=False):
    fw, b = p.fw, p.b
    psA, psB = p.psA, p.psB
    xln = b["xln"]
    sbx = b["x"]
    xkeys = [f"xln:{k}" for k in range(KT)]
    pool = b.S(pre + "pool", [128, pool_elems], BF16)
    d_bf = p.dscratch(pre + "_dbf", [KT, 128, n_virt, 128], BF16)
    fw.op("gpsimd",
          lambda e: [e.dma_start(out=d_bf, in_=d_dram)],
          writes=[pre + "_dbf"], dma=True, tag=pre + "_stgd")
    wgu = pool[:, 0:4096].rearrange("p (d g k h) -> p d g k h",
                                    d=2, g=2, k=KT, h=128)
    gt = pool[:, 4096:6144]
    hwe = pool[:, 6144:10240].rearrange("p (d t) -> p d t", d=2)
    cwbe = pool[:, 10240:14336].rearrange("p (d t) -> p d t", d=2)
    for e_i in range(n_virt):
        db = e_i % 2
        crow = cw_row(e_i)
        if crow is not None:
            fw.op("gpsimd",
                  lambda e, crow=crow: [e.dma_start(
                      out=cwbe[:, 0, :],
                      in_=cwstage[crow:crow + 1, :]
                      .to_broadcast([128, T]))],
                  reads=["cwstage"], writes=[pre + "_cwbe:0"], dma=True,
                  tag=f"{pre}_ldcw{e_i}")
        fw.op("gpsimd",
              lambda e, e_i=e_i, db=db: [e.dma_start(
                  out=wgu[:, db, :, :, :], in_=gu_dram[e_i])],
              writes=[f"{pre}_wgu:{db}"], dma=True, tag=f"{pre}_ldgu{e_i}")

        def gmm(e, db=db):
            last = None
            for k in range(KT):
                for c in range(TC):
                    sl = slice(c * 512, (c + 1) * 512)
                    last = e.matmul(psA[:, sl], wgu[:, db, 0, k, :],
                                    xln[:, k, sl],
                                    start=(k == 0), stop=(k == KT - 1))
            return last
        fw.op("tensor", gmm, reads=[f"{pre}_wgu:{db}"] + xkeys,
              writes=_akeys(), tag=f"{pre}_g{e_i}")

        def umm(e, db=db):
            last = None
            for k in range(KT):
                for c in range(TC):
                    sl = slice(c * 512, (c + 1) * 512)
                    last = e.matmul(psB[:, sl], wgu[:, db, 1, k, :],
                                    xln[:, k, sl],
                                    start=(k == 0), stop=(k == KT - 1))
            return last
        fw.op("tensor", umm, reads=[f"{pre}_wgu:{db}"] + xkeys,
              writes=_bkeys(), tag=f"{pre}_u{e_i}")
        fw.op("scalar",
              lambda e: e.activation(gt, psA[:, :], AF.Silu),
              reads=_akeys(), writes=[pre + "_gts"], tag=f"{pre}_si{e_i}")
        fw.op("vector",
              lambda e: e.tensor_tensor(hwe[:, 0, :], psB[:, :],
                                        gt, OP.mult),
              reads=_bkeys() + [pre + "_gts"], writes=[pre + "_hwe:0"],
              tag=f"{pre}_hw{e_i}")
        if crow is not None:
            fw.op("vector",
                  lambda e: e.tensor_tensor(hwe[:, 0, :], hwe[:, 0, :],
                                            cwbe[:, 0, :], OP.mult),
                  reads=[pre + "_hwe:0", pre + "_cwbe:0"],
                  writes=[pre + "_hwe:0"], tag=f"{pre}_hwc{e_i}")
        fw.op("sync",
              lambda e, e_i=e_i: [e.dma_start(
                  out=hwstage[e_i], in_=hwe[:, 0, :])],
              reads=[pre + "_hwe:0"], writes=[f"hws:{e_i}"], dma=True,
              tag=f"{pre}_sthw{e_i}")

    if wd_in_xln:
        n1 = min(n_virt, pool_elems // tcw)
        n2 = n_virt - n1
        assert n2 * tcw <= KT * T, "xln carve overflow"
        assert n_virt * 128 <= KT * T, "lnscr carve overflow"
        hwbuf = pool[:, 0:n1 * tcw].rearrange("p (e t) -> p e t", e=n1)
        hwbuf2 = b["xln"][:, :, :].rearrange("p k t -> p (k t)") \
            [:, 0:n2 * tcw].rearrange("p (e t) -> p e t", e=n2)
        fw.alias([pre + "_hwb"],
                 [f"{pre}_wgu:0", f"{pre}_wgu:1", pre + "_gts",
                  pre + "_hwe:0", pre + "_cwbe:0"])
        fw.alias([pre + "_hwb2"], [f"xln:{k}" for k in range(KT)])
        fw.alias([pre + "_wd"], [f"lnscr:{k}" for k in range(KT)])
        wd = b["lnscr"][:, :, :].rearrange("p k t -> p (k t)") \
            [:, 0:n_virt * 128].rearrange("p (e h) -> p e h", e=n_virt)
        wd2 = None
    else:
        hwbuf = pool[:, 0:n_virt * tcw].rearrange("p (e t) -> p e t",
                                                  e=n_virt)
        hwbuf2 = None
        n1 = n_virt
        fw.alias([pre + "_hwb"],
                 [f"{pre}_wgu:0", f"{pre}_wgu:1", pre + "_gts",
                  pre + "_hwe:0", pre + "_cwbe:0"])
        assert 2 * n_virt * 128 <= KT * T, "lnscr wd overflow"
        fw.alias([pre + "_wd", pre + "_wd2"],
                 [f"lnscr:{k}" for k in range(KT)])
        lnf = b["lnscr"][:, :, :].rearrange("p k t -> p (k t)")
        wd = lnf[:, 0:n_virt * 128].rearrange("p (e h) -> p e h",
                                              e=n_virt)
        wd2 = lnf[:, n_virt * 128:2 * n_virt * 128] \
            .rearrange("p (e h) -> p e h", e=n_virt)
        assert n_virt * tcw <= pool_elems, "moe pool overflow"
    ntc = T // tcw
    for c in range(ntc):
        sl = slice(c * tcw, (c + 1) * tcw)
        if wd_in_xln:
            fw.op("gpsimd",
                  lambda e, c=c: [e.dma_start(
                      out=hwbuf[:],
                      in_=hwstage[0:n1, :, c * tcw:(c + 1) * tcw]
                      .rearrange("e p t -> p e t"))],
                  reads=[f"hws:{i}" for i in range(n1)],
                  writes=[pre + "_hwb"], dma=True, tag=f"{pre}_ldhw{c}")
            fw.op("sync",
                  lambda e, c=c: [e.dma_start(
                      out=hwbuf2[:],
                      in_=hwstage[n1:n_virt, :, c * tcw:(c + 1) * tcw]
                      .rearrange("e p t -> p e t"))],
                  reads=[f"hws:{i}" for i in range(n1, n_virt)],
                  writes=[pre + "_hwb2"], dma=True,
                  tag=f"{pre}_ldhw2_{c}")
        else:
            fw.op("gpsimd",
                  lambda e, c=c: [e.dma_start(
                      out=hwbuf[:],
                      in_=hwstage[:, :, c * tcw:(c + 1) * tcw]
                      .rearrange("e p t -> p e t"))],
                  reads=[f"hws:{i}" for i in range(n_virt)],
                  writes=[pre + "_hwb"], dma=True, tag=f"{pre}_ldhw{c}")
        for o in range(KT):
            db = o % 2
            wdo = wd if (wd2 is None or o % 2 == 0) else wd2
            wdk = pre + ("_wd" if (wd2 is None or o % 2 == 0) else "_wd2")
            fw.op("gpsimd",
                  lambda e, o=o, wdo=wdo: [e.dma_start(out=wdo,
                                                       in_=d_bf[o])],
                  reads=[pre + "_dbf"], writes=[wdk], dma=True,
                  tag=f"{pre}_ldd{c}_{o}")
            po = psB[:, db * 512:db * 512 + tcw]
            ko = f"B{db}"

            def dmm(e, po=po, wdo=wdo):
                last = None
                for e_i in range(n_virt):
                    if wd_in_xln and e_i >= n1:
                        hslc = hwbuf2[:, e_i - n1, :]
                    else:
                        hslc = hwbuf[:, e_i, :]
                    last = e.matmul(po, wdo[:, e_i, :], hslc,
                                    start=(e_i == 0),
                                    stop=(e_i == n_virt - 1))
                return last
            rd = [wdk, pre + "_hwb"]
            if wd_in_xln:
                rd.append(pre + "_hwb2")
            fw.op("tensor", dmm, reads=rd, writes=[ko],
                  tag=f"{pre}_d{c}_{o}")
            fw.op("vector",
                  lambda e, o=o, po=po, sl=sl: e.tensor_tensor(
                      sbx[:, o, sl], sbx[:, o, sl], po, OP.add),
                  reads=[ko, f"x:{o}"], writes=[f"x:{o}"],
                  tag=f"{pre}_add{c}_{o}")


def host_prep_l2ffn(inp):
    out = {}
    out["w_l2f_gate"] = lhsT_prep(inp["L2_ffn_w_gate"])
    w1, w3, w2 = inp["L2_ffn_w1_e"], inp["L2_ffn_w3_e"], inp["L2_ffn_w2_e"]
    s1, s3, s2 = inp["L2_ffn_w1_s"], inp["L2_ffn_w3_s"], inp["L2_ffn_w2_s"]
    gus, downs = [], []
    for ex in range(16):
        for j in range(4):
            g = w1[ex][j * 128:(j + 1) * 128].T.reshape(KT, 128, 128)
            u = w3[ex][j * 128:(j + 1) * 128].T.reshape(KT, 128, 128)
            gus.append(np.stack([g, u], axis=0))
            downs.append(w2[ex].T[j * 128:(j + 1) * 128])
    for j in range(8):
        g = s1[j * 128:(j + 1) * 128].T.reshape(KT, 128, 128)
        u = s3[j * 128:(j + 1) * 128].T.reshape(KT, 128, 128)
        gus.append(np.stack([g, u], axis=0))
        downs.append(s2.T[j * 128:(j + 1) * 128])
    gus = np.stack(gus, axis=0)
    out["w_l2f_gu"] = _f(gus.transpose(0, 3, 1, 2, 4))
    big = np.stack(downs, axis=0)
    out["w_l2f_d"] = _f(big.reshape(72, 128, KT, 128)
                        .transpose(2, 1, 0, 3))
    return out


def emit_l2ffn(p):
    fw, b = p.fw, p.b
    psA, psB = p.psA, p.psB
    sbx = b["x"]
    xfkeys = [f"x:{k}" for k in range(KT)]
    p.load("w_l2f_gate", [128, KT, 16], dt=F32)
    gu_dram = p.din("w_l2f_gu", [72, 128, 2, KT, 128])
    d_dram = p.din("w_l2f_d", [KT, 128, 72, 128])
    hwstage = p.dscratch("l2f_hwstage", [72, 128, T], BF16)
    cwstage = p.dscratch("l2f_cwstage", [16, T], BF16)
    wg = b["w_l2f_gate"]
    ident = b["c_ident"]
    murow = b["rowsA"][0:1, :]
    rsrow = b["rowsB"][0:1, :]

    ngw = b.S("l2f_ngw", [1, 16], F32)

    def gsum(e):
        last = None
        for k in range(KT):
            last = e.matmul(psB[0:1, 0:16], b["c_onescol"][:, :],
                            wg[:, k, :], start=(k == 0),
                            stop=(k == KT - 1))
        return last
    fw.op("tensor", gsum, reads=["w_l2f_gate", "c_onescol"],
          writes=["B0"], tag="l2f_gws")
    fw.op("vector",
          lambda e: e.tensor_scalar(ngw[:, :], psB[0:1, 0:16], -1.0, None,
                                    OP.mult),
          reads=["B0"], writes=["l2f_ngw"], tag="l2f_ngw")

    def rebc(e):
        last = None
        for c in range(TC):
            sl = slice(c * 512, (c + 1) * 512)
            last = e.matmul(psB[:, sl], b["c_onesrow"][:, :],
                            rsrow[:, sl], start=True, stop=True)
        return last
    fw.op("tensor", rebc, reads=["rowB0", "c_onesrow"], writes=_bkeys(),
          tag="l2f_rebc")

    def rmm(e):
        last = None
        for k in range(KT):
            for c in range(TC):
                sl = slice(c * 512, (c + 1) * 512)
                e.matmul(psA[0:16, sl], wg[:, k, :], sbx[:, k, sl],
                         start=(k == 0), stop=False)
        for c in range(TC):
            sl = slice(c * 512, (c + 1) * 512)
            last = e.matmul(psA[0:16, sl], ngw[:, :], murow[:, sl],
                            start=False, stop=True)
        return last
    fw.op("tensor", rmm,
          reads=["w_l2f_gate", "l2f_ngw", "rowA0"] + xfkeys,
          writes=_akeys(), tag="l2f_rmm")
    lgT = b.S("l2f_lgT", [16, T], F32)
    fw.op("scalar", lambda e: e.copy(lgT[:, :], psA[0:16, :]),
          reads=_akeys(), writes=["l2f_lgT"], tag="l2f_lg0")
    fw.op("vector",
          lambda e: e.tensor_tensor(lgT[:, :], lgT[:, :], psB[0:16, :],
                                    OP.mult),
          reads=["l2f_lgT"] + _bkeys(), writes=["l2f_lgT"], tag="l2f_lgm")

    lg = b.S("l2f_lg", [128, 16, 16], F32)
    for half in range(2):
        def trs(e, half=half):
            last = None
            for c in range(8):
                cc = half * 8 + c
                last = e.transpose(psB[:, c * 16:(c + 1) * 16],
                                   lgT[:, cc * 128:(cc + 1) * 128],
                                   ident[0:16, 0:16])
            return last
        fw.op("tensor", trs, reads=["l2f_lgT", "c_ident"], writes=["B0"],
              tag=f"l2f_tr{half}")
        fw.op("vector",
              lambda e, half=half: e.tensor_copy(
                  lg[:, half * 8:(half + 1) * 8, :],
                  psB[:, 0:128].rearrange("p (c e) -> p c e", c=8)),
              reads=["B0"], writes=["l2f_lg"], tag=f"l2f_lgcp{half}")
    s = b.S("l2f_s", [128, 16, 16], F32)
    fw.op("scalar", lambda e: e.activation(s[:, :, :], lg[:, :, :],
                                           AF.Sigmoid),
          reads=["l2f_lg"], writes=["l2f_s"], tag="l2f_sig")

    sv = s[:, :, :].rearrange("p c (g j) -> p c g j", g=4)
    m1 = b.S("l2f_m1", [128, 16, 4], F32)
    mk1 = b.S("l2f_mk1", [128, 16, 4, 4], F32)
    gs = b.S("l2f_gs", [128, 16, 4], F32)
    gm = b.S("l2f_gm", [128, 16], F32)
    am = b.S("l2f_am", [128, 16, 4], F32)
    smk = b.S("l2f_smk", [128, 16, 16], F32)
    t8 = b.S("l2f_t8", [128, 16, 8], F32)
    cw = b.S("l2f_cw", [128, 16, 16], F32)
    den = b.S("l2f_den", [128, 16], F32)
    fw.op("vector", lambda e: e.tensor_reduce(m1[:, :, :], sv, AX.X, OP.max),
          reads=["l2f_s"], writes=["l2f_m1"], tag="l2f_m1")
    fw.op("vector",
          lambda e: e.tensor_tensor(mk1[:, :, :, :], sv,
                                    m1[:, :, :].unsqueeze(3)
                                    .to_broadcast([128, 16, 4, 4]),
                                    OP.is_ge),
          reads=["l2f_s", "l2f_m1"], writes=["l2f_mk1"], tag="l2f_mk1")
    fw.op("vector",
          lambda e: e.scalar_tensor_tensor(mk1[:, :, :, :], mk1[:, :, :, :],
                                           -1e9, sv, OP.mult, OP.add),
          reads=["l2f_mk1", "l2f_s"], writes=["l2f_mk1"], tag="l2f_sv2")
    fw.op("vector",
          lambda e: e.tensor_reduce(gs[:, :, :], mk1[:, :, :, :], AX.X,
                                    OP.max),
          reads=["l2f_mk1"], writes=["l2f_gs"], tag="l2f_m2")
    fw.op("vector",
          lambda e: e.tensor_tensor(gs[:, :, :], gs[:, :, :], m1[:, :, :],
                                    OP.add),
          reads=["l2f_gs", "l2f_m1"], writes=["l2f_gs"], tag="l2f_gs")
    fw.op("vector", lambda e: e.tensor_reduce(gm[:, :], gs[:, :, :], AX.X,
                                              OP.max),
          reads=["l2f_gs"], writes=["l2f_gm"], tag="l2f_gm1")
    fw.op("vector",
          lambda e: e.tensor_tensor(am[:, :, :], gs[:, :, :],
                                    gm[:, :].unsqueeze(2)
                                    .to_broadcast([128, 16, 4]), OP.is_ge),
          reads=["l2f_gs", "l2f_gm"], writes=["l2f_am"], tag="l2f_amk")
    fw.op("vector",
          lambda e: e.scalar_tensor_tensor(am[:, :, :], am[:, :, :], -1e9,
                                           gs[:, :, :], OP.mult, OP.add),
          reads=["l2f_am", "l2f_gs"], writes=["l2f_am"], tag="l2f_gs2")
    fw.op("vector", lambda e: e.tensor_reduce(gm[:, :], am[:, :, :], AX.X,
                                              OP.max),
          reads=["l2f_am"], writes=["l2f_gm"], tag="l2f_gm2")
    fw.op("vector",
          lambda e: e.tensor_tensor(am[:, :, :], gs[:, :, :],
                                    gm[:, :].unsqueeze(2)
                                    .to_broadcast([128, 16, 4]), OP.is_ge),
          reads=["l2f_gs", "l2f_gm"], writes=["l2f_am"], tag="l2f_am")
    fw.op("vector",
          lambda e: e.tensor_tensor(
              smk[:, :, :].rearrange("p c (g j) -> p c g j", g=4), sv,
              am[:, :, :].unsqueeze(3).to_broadcast([128, 16, 4, 4]),
              OP.mult),
          reads=["l2f_am", "l2f_s"], writes=["l2f_smk"], tag="l2f_smk")
    mbig = b.S("l2f_mbig", [128, 16, 4], F32)
    fw.op("vector",
          lambda e: e.tensor_scalar(mbig[:, :, :], am[:, :, :], 1e9, -1e9,
                                    OP.mult, OP.add),
          reads=["l2f_am"], writes=["l2f_mbig"], tag="l2f_mbig")
    fw.op("vector",
          lambda e: e.tensor_tensor(
              smk[:, :, :].rearrange("p c (g j) -> p c g j", g=4),
              smk[:, :, :].rearrange("p c (g j) -> p c g j", g=4),
              mbig[:, :, :].unsqueeze(3).to_broadcast([128, 16, 4, 4]),
              OP.add),
          reads=["l2f_smk", "l2f_mbig"], writes=["l2f_smk"],
          tag="l2f_smk2")

    def max8s(e):
        last = None
        for c in range(16):
            last = e.max(out=t8[:, c, :], in_=smk[:, c, :])
            e.drain()
        return last
    fw.op("vector", max8s, reads=["l2f_smk"], writes=["l2f_t8"],
          tag="l2f_max8")
    fw.op("vector",
          lambda e: e.tensor_tensor(cw[:, :, :], smk[:, :, :],
                                    t8[:, :, 3:4]
                                    .to_broadcast([128, 16, 16]), OP.is_ge),
          reads=["l2f_smk", "l2f_t8"], writes=["l2f_cw"], tag="l2f_mk4")
    fw.op("vector",
          lambda e: e.tensor_tensor(cw[:, :, :], cw[:, :, :], s[:, :, :],
                                    OP.mult),
          reads=["l2f_cw", "l2f_s"], writes=["l2f_cw"], tag="l2f_w")
    fw.op("vector",
          lambda e: e.tensor_reduce(den[:, :], cw[:, :, :], AX.X, OP.add),
          reads=["l2f_cw"], writes=["l2f_den"], tag="l2f_den")
    fw.op("vector",
          lambda e: e.tensor_scalar(den[:, :], den[:, :], 1e-6, None,
                                    OP.add),
          reads=["l2f_den"], writes=["l2f_den"], tag="l2f_den2")
    fw.op("vector", lambda e: e.reciprocal(den[:, :], den[:, :]),
          reads=["l2f_den"], writes=["l2f_den"], tag="l2f_rc")
    fw.op("vector",
          lambda e: e.tensor_tensor(cw[:, :, :], cw[:, :, :],
                                    den[:, :].unsqueeze(2)
                                    .to_broadcast([128, 16, 16]), OP.mult),
          reads=["l2f_cw", "l2f_den"], writes=["l2f_cw"], tag="l2f_cwn")

    def trc(e):
        last = None
        for c in range(16):
            last = e.transpose(psA[0:16, c * 128:(c + 1) * 128],
                               cw[:, c, :], ident[:, :])
        return last
    fw.op("tensor", trc, reads=["l2f_cw", "c_ident"], writes=_akeys(),
          tag="l2f_trc")
    cwT = lgT
    fw.op("vector", lambda e: e.tensor_copy(cwT[0:16, :], psA[0:16, :]),
          reads=_akeys(), writes=["l2f_lgT"], tag="l2f_cwTcp")
    fw.op("gpsimd", lambda e: [e.dma_start(out=cwstage, in_=cwT[0:16, :])],
          reads=["l2f_lgT"], writes=["cwstage"], dma=True, tag="l2f_cwst")

    emit_moe_passes(p, "l2f", 72, 64,
                    lambda e_i: e_i // 4 if e_i < 64 else None,
                    gu_dram, d_dram, cwstage, hwstage, tcw=512,
                    pool_elems=20480, wd_in_xln=True)


def host_prep_l0ffn(inp):
    out = {}
    Wg = inp["L0_ffn_Wg"]
    out["w_l0f_gate"] = _f(Wg.reshape(KT, 128, 8).transpose(1, 0, 2))
    A = inp["L0_ffn_A"]
    Astack = np.concatenate([A[e] for e in range(8)], axis=1)
    out["w_l0f_A"] = _f(Astack.reshape(KT, 128, 128).transpose(1, 0, 2))
    B = inp["L0_ffn_B"]
    out["w_l0f_B"] = _f(B.reshape(128, 1024))
    return out


def emit_l0ffn(p):
    fw, b = p.fw, p.b
    psA, psB = p.psA, p.psB
    sbx = b["x"]
    xfkeys = [f"x:{k}" for k in range(KT)]
    xkeys = [f"xln:{k}" for k in range(KT)]
    xln = b["xln"]
    p.load("w_l0f_gate", [128, KT, 8], dt=F32)
    p.load("w_l0f_A", [128, KT, 128])
    p.load("w_l0f_B", [128, 1024])
    cwstage = p.dscratch("l0f_cwstage", [8, T], BF16)
    wg, wa, wb = b["w_l0f_gate"], b["w_l0f_A"], b["w_l0f_B"]
    ident = b["c_ident"]
    murow = b["rowsA"][0:1, :]
    rsrow = b["rowsB"][0:1, :]

    ngw = b.S("l0f_ngw", [1, 8], F32)

    def gsum(e):
        last = None
        for k in range(KT):
            last = e.matmul(psB[0:1, 0:8], b["c_onescol"][:, :],
                            wg[:, k, :], start=(k == 0),
                            stop=(k == KT - 1))
        return last
    fw.op("tensor", gsum, reads=["w_l0f_gate", "c_onescol"],
          writes=["B0"], tag="l0f_gws")
    fw.op("vector",
          lambda e: e.tensor_scalar(ngw[:, :], psB[0:1, 0:8], -1.0, None,
                                    OP.mult),
          reads=["B0"], writes=["l0f_ngw"], tag="l0f_ngw")

    def rebc(e):
        last = None
        for c in range(TC):
            sl = slice(c * 512, (c + 1) * 512)
            last = e.matmul(psB[:, sl], b["c_onesrow"][:, :],
                            rsrow[:, sl], start=True, stop=True)
        return last
    fw.op("tensor", rebc, reads=["rowB0", "c_onesrow"], writes=_bkeys(),
          tag="l0f_rebc")

    def rmm(e):
        last = None
        for k in range(KT):
            for c in range(TC):
                sl = slice(c * 512, (c + 1) * 512)
                e.matmul(psA[0:8, sl], wg[:, k, :], sbx[:, k, sl],
                         start=(k == 0), stop=False)
        for c in range(TC):
            sl = slice(c * 512, (c + 1) * 512)
            last = e.matmul(psA[0:8, sl], ngw[:, :], murow[:, sl],
                            start=False, stop=True)
        return last
    fw.op("tensor", rmm,
          reads=["w_l0f_gate", "l0f_ngw", "rowA0"] + xfkeys,
          writes=_akeys(), tag="l0f_rmm")
    lgT = b.S("l0f_lgT", [8, T], F32)
    fw.op("scalar", lambda e: e.copy(lgT[:, :], psA[0:8, :]),
          reads=_akeys(), writes=["l0f_lgT"], tag="l0f_lg0")
    fw.op("vector",
          lambda e: e.tensor_tensor(lgT[:, :], lgT[:, :], psB[0:8, :],
                                    OP.mult),
          reads=["l0f_lgT"] + _bkeys(), writes=["l0f_lgT"], tag="l0f_lgm")

    lg = b.S("l0f_lg", [128, 16, 8], F32)
    for half in range(2):
        def trs(e, half=half):
            last = None
            for c in range(8):
                cc = half * 8 + c
                last = e.transpose(psB[:, c * 8:(c + 1) * 8],
                                   lgT[:, cc * 128:(cc + 1) * 128],
                                   ident[0:8, 0:8])
            return last
        fw.op("tensor", trs, reads=["l0f_lgT", "c_ident"], writes=["B0"],
              tag=f"l0f_tr{half}")
        fw.op("vector",
              lambda e, half=half: e.tensor_copy(
                  lg[:, half * 8:(half + 1) * 8, :],
                  psB[:, 0:64].rearrange("p (c e) -> p c e", c=8)),
              reads=["B0"], writes=["l0f_lg"], tag=f"l0f_lgcp{half}")
    m1 = b.S("l0f_m1", [128, 16], F32)
    m2 = b.S("l0f_m2", [128, 16], F32)
    mk1 = b.S("l0f_mk1", [128, 16, 8], F32)
    mk2 = b.S("l0f_mk2", [128, 16, 8], F32)
    dm = b.S("l0f_dm", [128, 16], F32)
    s1 = b.S("l0f_s1", [128, 16], F32)
    cw = b.S("l0f_cw", [128, 16, 8], F32)
    fw.op("vector", lambda e: e.tensor_reduce(m1[:, :], lg[:, :, :], AX.X,
                                              OP.max),
          reads=["l0f_lg"], writes=["l0f_m1"], tag="l0f_m1")
    fw.op("vector",
          lambda e: e.tensor_tensor(mk1[:, :, :], lg[:, :, :],
                                    m1[:, :].unsqueeze(2)
                                    .to_broadcast([128, 16, 8]), OP.is_ge),
          reads=["l0f_lg", "l0f_m1"], writes=["l0f_mk1"], tag="l0f_mk1")
    fw.op("vector",
          lambda e: e.scalar_tensor_tensor(mk2[:, :, :], mk1[:, :, :],
                                           -1e9, lg[:, :, :], OP.mult,
                                           OP.add),
          reads=["l0f_mk1", "l0f_lg"], writes=["l0f_mk2"], tag="l0f_lg2")
    fw.op("vector", lambda e: e.tensor_reduce(m2[:, :], mk2[:, :, :], AX.X,
                                              OP.max),
          reads=["l0f_mk2"], writes=["l0f_m2"], tag="l0f_m2")
    fw.op("vector",
          lambda e: e.tensor_tensor(mk2[:, :, :], mk2[:, :, :],
                                    m2[:, :].unsqueeze(2)
                                    .to_broadcast([128, 16, 8]), OP.is_ge),
          reads=["l0f_mk2", "l0f_m2"], writes=["l0f_mk2"], tag="l0f_msk2")
    fw.op("vector", lambda e: e.tensor_tensor(dm[:, :], m2[:, :], m1[:, :],
                                              OP.subtract),
          reads=["l0f_m2", "l0f_m1"], writes=["l0f_dm"], tag="l0f_dm")
    fw.op("scalar", lambda e: e.activation(dm[:, :], dm[:, :], AF.Exp),
          reads=["l0f_dm"], writes=["l0f_dm"], tag="l0f_dexp")
    fw.op("vector",
          lambda e: e.tensor_scalar(s1[:, :], dm[:, :], 1.0, None, OP.add),
          reads=["l0f_dm"], writes=["l0f_s1"], tag="l0f_s1a")
    fw.op("vector", lambda e: e.reciprocal(s1[:, :], s1[:, :]),
          reads=["l0f_s1"], writes=["l0f_s1"], tag="l0f_s1r")
    fw.op("vector", lambda e: e.tensor_tensor(dm[:, :], dm[:, :], s1[:, :],
                                              OP.mult),
          reads=["l0f_dm", "l0f_s1"], writes=["l0f_dm"], tag="l0f_w2")
    fw.op("vector",
          lambda e: e.tensor_tensor(cw[:, :, :], mk1[:, :, :],
                                    s1[:, :].unsqueeze(2)
                                    .to_broadcast([128, 16, 8]), OP.mult),
          reads=["l0f_mk1", "l0f_s1"], writes=["l0f_cw"], tag="l0f_cw1")
    fw.op("vector",
          lambda e: e.scalar_tensor_tensor(mk2[:, :, :], mk2[:, :, :], 1.0,
                                           dm[:, :].unsqueeze(2)
                                           .to_broadcast([128, 16, 8]),
                                           OP.mult, OP.mult),
          reads=["l0f_mk2", "l0f_dm"], writes=["l0f_mk2"], tag="l0f_cw2")
    fw.op("vector",
          lambda e: e.tensor_tensor(cw[:, :, :], cw[:, :, :],
                                    mk2[:, :, :], OP.add),
          reads=["l0f_cw", "l0f_mk2"], writes=["l0f_cw"], tag="l0f_cw3")

    def trc(e):
        last = None
        for c in range(16):
            last = e.transpose(psA[0:8, c * 128:(c + 1) * 128],
                               cw[:, c, :], ident[:, :])
        return last
    fw.op("tensor", trc, reads=["l0f_cw", "c_ident"], writes=_akeys(),
          tag="l0f_trc")
    cwT = lgT
    fw.op("vector", lambda e: e.tensor_copy(cwT[0:8, :], psA[0:8, :]),
          reads=_akeys(), writes=["l0f_lgT"], tag="l0f_cwTcp")
    fw.op("gpsimd", lambda e: [e.dma_start(out=cwstage, in_=cwT[0:8, :])],
          reads=["l0f_lgT"], writes=["cwstage"], dma=True, tag="l0f_cwst")
    cwbc = b.S("l0f_cwbc", [128, T], BF16)
    fw.op("gpsimd",
          lambda e: [e.dma_start(
              out=cwbc[:],
              in_=cwstage.unsqueeze(1).to_broadcast([8, 16, T]))],
          reads=["cwstage"], writes=["l0f_cwbc"], dma=True, tag="l0f_bc")

    def hmm(e):
        last = None
        for k in range(KT):
            for c in range(TC):
                sl = slice(c * 512, (c + 1) * 512)
                last = e.matmul(psA[:, sl], wa[:, k, :], xln[:, k, sl],
                                start=(k == 0), stop=(k == KT - 1))
        return last
    fw.op("tensor", hmm, reads=["w_l0f_A"] + xkeys, writes=_akeys(),
          tag="l0f_hmm")
    h8 = b.S("l0f_h8", [128, T], BF16)
    fw.op("vector",
          lambda e: e.tensor_tensor(h8[:, :], psA[:, :], cwbc[:, :],
                                    OP.mult),
          reads=_akeys() + ["l0f_cwbc"], writes=["l0f_h8"], tag="l0f_hw")
    for o in range(KT):
        db = o % 2
        po = psB[:, db * 512:(db + 1) * 512]
        ko = f"B{db}"
        for c in range(TC):
            sl = slice(c * 512, (c + 1) * 512)
            fw.op("tensor",
                  lambda e, o=o, po=po, sl=sl: e.matmul(
                      po, wb[:, o * 128:(o + 1) * 128], h8[:, sl],
                      start=True, stop=True),
                  reads=["w_l0f_B", "l0f_h8"], writes=[ko],
                  tag=f"l0f_d{o}_{c}")
            fw.op("vector",
                  lambda e, o=o, po=po, sl=sl: e.tensor_tensor(
                      sbx[:, o, sl], sbx[:, o, sl], po, OP.add),
                  reads=[ko, f"x:{o}"], writes=[f"x:{o}"],
                  tag=f"l0f_add{o}_{c}")


def host_prep_l1attn(inp):
    out = {}
    for i in (0, 1):
        out[f"w_l1a_wz{i}"] = lhsT_prep(inp[f"L1_attn_Wz{i}"])
        out[f"w_l1a_wh{i}"] = lhsT_prep(inp[f"L1_attn_Wh{i}"])
        bz = inp[f"L1_attn_bz{i}"]
        bh = inp[f"L1_attn_bh{i}"]
        out[f"w_l1a_bz{i}"] = _f(bz.reshape(KT, 128).T)
        out[f"w_l1a_nbz{i}"] = _f(-bz.reshape(KT, 128).T)
        out[f"w_l1a_bh{i}"] = _f(bh.reshape(KT, 128).T)
    return out


def emit_mingru(p, uid, srcget, srckeys, wzd, whd, bz, bzk, nbz, nbzk,
                bh, bhk, hout, houtkey):
    fw, b = p.fw, p.b
    psA, psB = p.psA, p.psB
    wz = b.S(uid + "wz", [128, 2, KT, 128], BF16)
    wh = b.S(uid + "wh", [128, 2, KT, 128], BF16)
    cbuf = b.S("gru_c", [128, T], F32)
    ubuf = b.S("gru_u", [128, T], F32)
    thb = b.S("gru_th", [128, T], F32)
    zbuf = b.S("gru_z", [128, T], BF16)
    sbuf_ = b.S("gru_s", [128, T], BF16)
    mbuf = b.S("gru_m", [128, T], mybir.dt.uint8)
    for o in range(KT):
        db = o % 2
        fw.op("gpsimd",
              lambda e, o=o, db=db: [
                  e.dma_start(out=wz[:, db],
                              in_=wzd[:, :, o * 128:(o + 1) * 128]),
                  e.dma_start(out=wh[:, db],
                              in_=whd[:, :, o * 128:(o + 1) * 128])],
              writes=[f"{uid}wz:{db}", f"{uid}wh:{db}"], dma=True, n_dma=2,
              tag=f"{uid}ldw{o}")

        def zmm(e, db=db):
            last = None
            for k in range(KT):
                for c in range(TC):
                    sl = slice(c * 512, (c + 1) * 512)
                    last = e.matmul(psA[:, sl], wz[:, db, k, :], srcget(k, sl),
                                    start=(k == 0), stop=(k == KT - 1))
            return last
        fw.op("tensor", zmm, reads=[f"{uid}wz:{db}"] + srckeys,
              writes=_akeys(), tag=f"{uid}zmm{o}")

        def hmm(e, db=db):
            last = None
            for k in range(KT):
                for c in range(TC):
                    sl = slice(c * 512, (c + 1) * 512)
                    last = e.matmul(psB[:, sl], wh[:, db, k, :], srcget(k, sl),
                                    start=(k == 0), stop=(k == KT - 1))
            return last
        fw.op("tensor", hmm, reads=[f"{uid}wh:{db}"] + srckeys,
              writes=_bkeys(), tag=f"{uid}hmm{o}")

        fw.op("scalar",
              lambda e, o=o: e.activation(cbuf[:, :], psA[:, :], AF.Sigmoid,
                                          bias=nbz[:, o:o + 1], scale=-1.0),
              reads=_akeys() + [nbzk], writes=["gru_c"],
              tag=f"{uid}c{o}")
        fw.op("scalar",
              lambda e, o=o: e.activation(zbuf[:, :], psA[:, :], AF.Sigmoid,
                                          bias=bz[:, o:o + 1]),
              reads=_akeys() + [bzk], writes=["gru_z"],
              tag=f"{uid}z{o}")
        fw.op("scalar",
              lambda e, o=o: e.activation(sbuf_[:, :], psB[:, :], AF.Sigmoid,
                                          bias=bh[:, o:o + 1]),
              reads=_bkeys() + [bhk], writes=["gru_s"],
              tag=f"{uid}s{o}")
        fw.op("scalar",
              lambda e, o=o: e.activation(thb[:, :], psB[:, :], AF.Identity,
                                          bias=bh[:, o:o + 1]),
              reads=_bkeys() + [bhk], writes=["gru_th"],
              tag=f"{uid}th{o}")
        fw.op("vector",
              lambda e: e.tensor_scalar(mbuf[:, :], thb[:, :], 0.0, None,
                                        OP.is_ge),
              reads=["gru_th"], writes=["gru_m"], tag=f"{uid}m{o}")
        fw.op("vector",
              lambda e: e.tensor_scalar(thb[:, :], thb[:, :], 0.0, 0.5,
                                        OP.max, OP.add),
              reads=["gru_th"], writes=["gru_th"], tag=f"{uid}p{o}")
        fw.op("vector",
              lambda e: e.select(ubuf[:, :], mbuf[:, :], thb[:, :],
                                 sbuf_[:, :], add_drain=True),
              reads=["gru_m", "gru_th", "gru_s"], writes=["gru_u"],
              tag=f"{uid}sel{o}")
        fw.op("vector",
              lambda e: e.tensor_tensor(ubuf[:, :], ubuf[:, :], zbuf[:, :],
                                        OP.mult),
              reads=["gru_u", "gru_z"], writes=["gru_u"], tag=f"{uid}u{o}")
        fw.op("vector",
              lambda e, o=o: e.tensor_tensor_scan(hout[:, o, :], cbuf[:, :],
                                                  ubuf[:, :], 0.5,
                                                  OP.mult, OP.add),
              reads=["gru_c", "gru_u"], writes=[f"{houtkey}:{o}"],
              tag=f"{uid}scan{o}")


def emit_l1attn(p):
    fw, b = p.fw, p.b
    for i in (0, 1):
        for nm in ("bz", "nbz", "bh"):
            p.load(f"w_l1a_{nm}{i}", [128, KT], dt=F32)
    wz0 = p.din("w_l1a_wz0", [128, KT, 1024])
    wh0 = p.din("w_l1a_wh0", [128, KT, 1024])
    wz1 = p.din("w_l1a_wz1", [128, KT, 1024])
    wh1 = p.din("w_l1a_wh1", [128, KT, 1024])
    xln = b["xln"]
    h1 = b["lnscr"]
    fw.alias(["l1a_h1:" + str(k) for k in range(KT)],
             ["lnscr:" + str(k) for k in range(KT)])
    emit_mingru(p, "g0", lambda k, sl: xln[:, k, sl],
                [f"xln:{k}" for k in range(KT)], wz0, wh0,
                b["w_l1a_bz0"], "w_l1a_bz0", b["w_l1a_nbz0"], "w_l1a_nbz0",
                b["w_l1a_bh0"], "w_l1a_bh0", h1, "l1a_h1")
    fw.alias(["l1a_h2:" + str(k) for k in range(KT)],
             ["xln:" + str(k) for k in range(KT)])
    emit_mingru(p, "g1", lambda k, sl: h1[:, k, sl],
                [f"l1a_h1:{k}" for k in range(KT)], wz1, wh1,
                b["w_l1a_bz1"], "w_l1a_bz1", b["w_l1a_nbz1"], "w_l1a_nbz1",
                b["w_l1a_bh1"], "w_l1a_bh1", xln, "l1a_h2")
    sbx = b["x"]
    for o in range(KT):
        fw.op("vector",
              lambda e, o=o: e.tensor_tensor(sbx[:, o, :], sbx[:, o, :],
                                             xln[:, o, :], OP.add),
              reads=[f"l1a_h2:{o}", f"x:{o}"], writes=[f"x:{o}"],
              tag=f"l1a_add{o}")


def host_prep_l3ffn(inp):
    out = {}
    for nm in ("wf", "uf", "wi", "ui", "wc", "uc", "wo", "uo"):
        out[f"w_l3f_{nm}"] = lhsT_prep(inp[f"L3_ffn_{nm}"])
    for nm in ("bf", "bi", "bc", "bo"):
        out[f"w_l3f_{nm}"] = _f(inp[f"L3_ffn_{nm}"].reshape(KT, 128).T)
    c0 = inp["L3_ffn_in_c"].reshape(KT, 128).T
    out["w_l3f_c0"] = _f(np.repeat(c0[:, :, None], 256, axis=2))
    return out


def emit_l3ffn(p):
    fw, b = p.fw, p.b
    psA, psB = p.psA, p.psB
    xln = b["xln"]
    sbx = b["x"]
    lnscr = b["lnscr"]
    for nm in ("bf", "bi", "bc", "bo"):
        p.load(f"w_l3f_{nm}", [128, KT], dt=F32)
    cst = p.load("w_l3f_c0", [128, KT, 256], dt=F32, sb_name="l3f_c")
    xv = sbx[:, :, :].bitcast(BF16)
    sv = lnscr[:, :, :].rearrange("p k t -> p (k t)")
    wviews = {
        "wf": xv[:, :, 0:1024].rearrange("p k o -> p k o"),
        "uf": xv[:, :, 1024:2048],
        "wi": xv[:, :, 2048:3072],
        "ui": xv[:, :, 3072:4096],
        "wc": sv[:, 0:8192].rearrange("p (k o) -> p k o", k=KT),
        "uc": sv[:, 8192:16384].rearrange("p (k o) -> p k o", k=KT),
    }
    xkeys = [f"x:{k}" for k in range(KT)]
    lnkeys = [f"lnscr:{k}" for k in range(KT)]
    fw.alias(["l3f_w"], xkeys + lnkeys)
    for nm, view in wviews.items():
        ap = p.din(f"w_l3f_{nm}", [128, KT, 1024])
        fw.op("gpsimd", lambda e, v=view, ap=ap: [e.dma_start(out=v, in_=ap)],
              reads=["l3f_w"], writes=[f"l3f_{nm}"], dma=True,
              tag=f"l3f_ld{nm}")
    cb = b.S("l3f_cb", [128, KT, 256], BF16)
    fb = b.S("l3f_fb", [128, KT, 256], BF16)
    ib = b.S("l3f_ib", [128, KT, 256], BF16)
    gb = b.S("l3f_gb", [128, KT, 256], BF16)
    wkeys = lambda *nms: [f"l3f_{n}" for n in nms]
    gates = (("f", "wf", "uf", "bf", AF.Sigmoid, fb),
             ("i", "wi", "ui", "bi", AF.Sigmoid, ib),
             ("g", "wc", "uc", "bc", AF.Tanh, gb))
    for step in range(8):
        tsl = slice(step * 256, (step + 1) * 256)
        fw.op("vector",
              lambda e: e.tensor_copy(cb[:, :, :], cst[:, :, :]),
              reads=["l3f_c"], writes=["l3f_cbf"], tag=f"l3f_cb{step}")
        for gnm, wn, un, bn, af, dst in gates:
            wv, uv = wviews[wn], wviews[un]
            bcol = b[f"w_l3f_{bn}"]

            def gmm(e, wv=wv, uv=uv, tsl=tsl):
                last = None
                for o in range(KT):
                    osl = slice(o * 256, (o + 1) * 256)
                    for k in range(KT):
                        e.matmul(psA[:, osl],
                                 wv[:, k, o * 128:(o + 1) * 128],
                                 xln[:, k, tsl], start=(k == 0), stop=False)
                    for k in range(KT):
                        last = e.matmul(psA[:, osl],
                                        uv[:, k, o * 128:(o + 1) * 128],
                                        cb[:, k, :], start=False,
                                        stop=(k == KT - 1))
                return last
            fw.op("tensor", gmm,
                  reads=wkeys(wn, un) + ["l3f_cbf"] +
                        [f"xln:{k}" for k in range(KT)],
                  writes=_akeys(), tag=f"l3f_{gnm}mm{step}")

            def act(e, af=af, bcol=bcol, dst=dst):
                last = None
                for o in range(KT):
                    osl = slice(o * 256, (o + 1) * 256)
                    last = e.activation(dst[:, o, :], psA[:, osl], af,
                                        bias=bcol[:, o:o + 1])
                return last
            fw.op("scalar", act, reads=_akeys() + [f"w_l3f_{bn}"],
                  writes=[f"l3f_{gnm}b"], tag=f"l3f_{gnm}a{step}")
        fw.op("vector",
              lambda e: e.tensor_tensor(cst[:, :, :], cst[:, :, :],
                                        fb[:, :, :], OP.mult),
              reads=["l3f_c", "l3f_fb"], writes=["l3f_c"],
              tag=f"l3f_cf{step}")
        fw.op("vector",
              lambda e: e.tensor_tensor(ib[:, :, :], ib[:, :, :],
                                        gb[:, :, :], OP.mult),
              reads=["l3f_ib", "l3f_gb"], writes=["l3f_ib"],
              tag=f"l3f_ig{step}")
        fw.op("vector",
              lambda e: e.tensor_tensor(cst[:, :, :], cst[:, :, :],
                                        ib[:, :, :], OP.add),
              reads=["l3f_c", "l3f_ib"], writes=["l3f_c"],
              tag=f"l3f_cu{step}")
    fw.op("vector", lambda e: e.tensor_copy(cb[:, :, :], cst[:, :, :]),
          reads=["l3f_c"], writes=["l3f_cbf"], tag="l3f_cbF")
    wo_d = p.din("w_l3f_wo", [128, KT, 1024])
    uo_d = p.din("w_l3f_uo", [128, KT, 1024])
    wob = b.S("l3f_wo", [128, 2, KT, 128], BF16)
    uob = b.S("l3f_uo", [128, 2, KT, 128], BF16)
    up = b.S("l3f_up", [128, KT, 256], F32)
    hb = b.S("l3f_hb", [128, T], BF16)
    xi = p.x_src_ap
    xo = p.x_dst_ap
    bo = b["w_l3f_bo"]
    for o in range(KT):
        db = o % 2
        fw.op("gpsimd",
              lambda e, o=o, db=db: [
                  e.dma_start(out=wob[:, db],
                              in_=wo_d[:, :, o * 128:(o + 1) * 128]),
                  e.dma_start(out=uob[:, db],
                              in_=uo_d[:, :, o * 128:(o + 1) * 128])],
              writes=[f"l3f_wo:{db}", f"l3f_uo:{db}"], dma=True, n_dma=2,
              tag=f"l3f_ldo{o}")

        def umm(e, db=db):
            last = None
            for k in range(KT):
                last = e.matmul(psB[:, 0:256], uob[:, db, k, :],
                                cb[:, k, :], start=(k == 0),
                                stop=(k == KT - 1))
            return last
        fw.op("tensor", umm, reads=[f"l3f_uo:{db}", "l3f_cbf"],
              writes=["B0"], tag=f"l3f_umm{o}")
        fw.op("vector",
              lambda e, o=o: e.tensor_copy(up[:, o, :], psB[:, 0:256]),
              reads=["B0"], writes=["l3f_up"], tag=f"l3f_upc{o}")

        def omm(e, db=db):
            last = None
            for k in range(KT):
                for c in range(TC):
                    sl = slice(c * 512, (c + 1) * 512)
                    last = e.matmul(psA[:, sl], wob[:, db, k, :],
                                    xln[:, k, sl], start=(k == 0),
                                    stop=(k == KT - 1))
            return last
        fw.op("tensor", omm,
              reads=[f"l3f_wo:{db}"] + [f"xln:{k}" for k in range(KT)],
              writes=_akeys(), tag=f"l3f_omm{o}")
        fw.op("vector",
              lambda e, o=o: e.tensor_tensor(
                  hb[:, :].rearrange("p (s t) -> p s t", s=8),
                  psA[:, :].rearrange("p (s t) -> p s t", s=8),
                  up[:, o, :].unsqueeze(1).to_broadcast([128, 8, 256]),
                  OP.add),
              reads=_akeys() + ["l3f_up"], writes=["l3f_hb"],
              tag=f"l3f_hs{o}")
        fw.op("scalar",
              lambda e, o=o: e.activation(hb[:, :], hb[:, :], AF.Tanh,
                                          bias=bo[:, o:o + 1]),
              reads=["l3f_hb", "w_l3f_bo"], writes=["l3f_hb"],
              tag=f"l3f_ht{o}")
        xrow = b.S("l3f_xr", [128, 1, T], F32)
        if len(xi.shape) == 4:
            fw.op("sync",
                  lambda e, o=o: [e.dma_start(
                      out=xrow[:, 0, :].rearrange("p (r c) -> p r c", r=8),
                      in_=xi[:, o])],
                  writes=["l3f_xr"], dma=True, tag=f"l3f_ldx{o}")
        else:
            fw.op("sync",
                  lambda e, o=o: [e.dma_start(out=xrow[:, 0, :],
                                              in_=xi[:, o, :])],
                  writes=["l3f_xr"], dma=True, tag=f"l3f_ldx{o}")
        fw.op("vector",
              lambda e: e.tensor_tensor(xrow[:, 0, :], xrow[:, 0, :],
                                        hb[:, :], OP.add),
              reads=["l3f_xr", "l3f_hb"], writes=["l3f_xr"],
              tag=f"l3f_add{o}")
        fw.op("sync",
              lambda e, o=o: [e.dma_start(out=xo[:, o, :],
                                          in_=xrow[:, 0, :])],
              reads=["l3f_xr"], dma=True, tag=f"l3f_stx{o}")


def build_l2attn_block():
    p = Prog()
    p.load_consts()
    p.load_x()
    p.set_xio("x_in", "x_out")
    p.load("w_l2a_gam", [128, KT], dt=F32)
    p.load("w_l2a_bet", [128, KT], dt=F32)
    emit_ln(p, "ln2a", gamma="w_l2a_gam", beta="w_l2a_bet")
    emit_l2attn(p)
    return p.finish()


def build_l3ffn_block():
    p = Prog()
    p.load_consts()
    p.load_x()
    p.set_xio("x_in", "x_out")
    emit_ln(p, "ln3f")
    emit_l3ffn(p)
    return p.finish()


def host_prep_l2attn(inp):
    import math
    out = {}
    Wq, Wk, Wv = inp["L2_attn_Wq"], inp["L2_attn_Wk"], inp["L2_attn_Wv"]
    qkv = np.stack([lhsT_prep(Wq), lhsT_prep(Wk), lhsT_prep(Wv)],
                   axis=0)
    qkv = qkv.reshape(3, 128, KT, 16, 64)
    out["w_l2a_qkv"] = _f(qkv.transpose(3, 1, 0, 2, 4))
    Wpos = inp["L2_attn_Wpos"]
    pe_blocks = []
    for h in range(16):
        blk = Wpos[h * 64:(h + 1) * 64, :]
        pe_blocks.append(np.concatenate([blk[:, 0::2], blk[:, 1::2]],
                                        axis=1))
    out["w_l2a_pe"] = _f(np.stack(pe_blocks, axis=0))
    div = np.exp(np.arange(0, D, 2, dtype=np.float64) *
                 (-math.log(10000.0) / D))
    u = np.arange(T, dtype=np.float64)
    ang = u[None, :] * div[:, None]
    out["c_l2a_sin"] = _f(np.sin(ang).reshape(4, 128, T).transpose(1, 0, 2))
    out["c_l2a_cos"] = _f(np.cos(ang).reshape(4, 128, T).transpose(1, 0, 2))
    bq, bk, bv = inp["L2_attn_bq"], inp["L2_attn_bk"], inp["L2_attn_bv"]
    ub, vb = inp["L2_attn_u_bias"], inp["L2_attn_v_bias"]
    out["w_l2a_bqu"] = _f((bq.reshape(16, 64) + ub).T)
    out["w_l2a_bqv"] = _f((bq.reshape(16, 64) + vb).T)
    out["w_l2a_bk"] = _f(bk.reshape(16, 64).T)
    out["w_l2a_bv"] = _f(bv.reshape(16, 64).T)
    out["w_l2a_wout"] = lhsT_prep(inp["L2_attn_Wout"])
    out["w_l2a_bout"] = _f(inp["L2_attn_bout"].reshape(KT, 128).T)
    out["w_l2a_gam"] = _f(inp["L2_attn_ln_gamma"].reshape(KT, 128).T)
    out["w_l2a_bet"] = _f(inp["L2_attn_ln_beta"].reshape(KT, 128).T)
    out["c_negten"] = _f(np.full((128, 1), -10.0))
    return out


def emit_l2attn(p):
    fw, b = p.fw, p.b
    psA, psB = p.psA, p.psB
    xln = b["xln"]
    sbx = b["x"]
    lnscr = b["lnscr"]
    for nm in ("bqu", "bqv", "bk", "bv"):
        p.load(f"w_l2a_{nm}", [64, 16], dt=F32)
    p.load("w_l2a_bout", [128, KT], dt=F32)
    p.load("c_negten", [128, 1], dt=F32)
    qkv_d = p.din("w_l2a_qkv", [16, 128, 3, KT, 64])
    pe_d = p.din("w_l2a_pe", [16, 64, 1024])
    sin_d = p.din("c_l2a_sin", [128, 4, T])
    cos_d = p.din("c_l2a_cos", [128, 4, T])
    wout_d = p.din("w_l2a_wout", [128, KT, 1024])
    ctxstage = p.dscratch("l2a_ctx", [16, 64, T], BF16)
    onesrow = b["c_onesrow"]
    negten = b["c_negten"]

    lv = lnscr[:, :, :].rearrange("p k t -> p (k t)")
    SST = lv[:, 0:8192].rearrange("p (m t) -> p m t", m=4)
    CCT = lv[:, 8192:16384].rearrange("p (m t) -> p m t", m=4)
    fw.alias(["l2a_SS", "l2a_CC"], [f"lnscr:{k}" for k in range(KT)])
    fw.op("gpsimd", lambda e: [e.dma_start(out=SST, in_=sin_d)],
          reads=["l2a_SS"], writes=["l2a_SS"], dma=True, tag="l2a_ldS")
    fw.op("gpsimd", lambda e: [e.dma_start(out=CCT, in_=cos_d)],
          reads=["l2a_CC"], writes=["l2a_CC"], dma=True, tag="l2a_ldC")

    xv = sbx[:, :, :].bitcast(BF16).rearrange("p k t -> p (k t)")
    Ubuf = xv[:, 0:8192].rearrange("p (m t) -> p m t", m=4)
    Vbuf = xv[:, 8192:16384].rearrange("p (m t) -> p m t", m=4)
    gbuf = xv[:, 16384:18432]
    quB = xv[0:64, 18432:20480]
    qvB = xv[0:64, 20480:22528]
    kkB = xv[0:64, 22528:24576]
    vvB = xv[0:64, 24576:26624]
    ctxn = xv[0:64, 26624:28672]
    Vtok = xv[:, 28672:29824].rearrange("p (c d) -> p c d", c=16)
    at2 = xv[:, 29824:30848].rearrange("p (d t) -> p d t", d=2)
    tsc = b["rowsA"].ap().bitcast(BF16)[:, 0:2048]
    fw.alias(["l2a_carve"], [f"x:{k}" for k in range(KT)])
    fw.alias(["l2a_ts"], ["rowA0"])
    fw.op("vector", lambda e: e.memset(Vtok[:, :, 64:65], 1.0),
          reads=["l2a_carve"], writes=["l2a_vt1"], tag="l2a_vt1")

    wqkv = b.S("l2a_wqkv", [128, 2, 3, KT, 64], BF16)
    wpe = b.S("l2a_wpe", [64, 2, 1024], BF16)
    rsr = b["rowsB"][0:1, :]

    for h in range(16):
        db = h % 2
        fw.op("gpsimd",
              lambda e, h=h, db=db: [
                  e.dma_start(out=wqkv[:, db], in_=qkv_d[h]),
                  e.dma_start(out=wpe[:, db], in_=pe_d[h])],
              writes=[f"l2a_wqkv:{db}", f"l2a_wpe:{db}"], dma=True, n_dma=2,
              tag=f"l2a_ldw{h}")

        def qkvmm(e, which, db):
            last = None
            for k in range(KT):
                for c in range(TC):
                    sl = slice(c * 512, (c + 1) * 512)
                    last = e.matmul(psA[0:64, sl], wqkv[:, db, which, k, :],
                                    xln[:, k, sl],
                                    start=(k == 0), stop=(k == KT - 1))
            return last
        xkeys = [f"xln:{k}" for k in range(KT)]
        fw.op("tensor", lambda e, db=db: qkvmm(e, 0, db),
              reads=[f"l2a_wqkv:{db}"] + xkeys, writes=_akeys(),
              tag=f"l2a_qmm{h}")
        fw.op("scalar",
              lambda e, h=h: e.activation(quB, psA[0:64, :], AF.Identity,
                                          bias=b["w_l2a_bqu"][:, h:h + 1]),
              reads=_akeys() + ["w_l2a_bqu", "l2a_carve"], writes=["l2a_qu"],
              tag=f"l2a_qu{h}")
        fw.op("scalar",
              lambda e, h=h: e.activation(qvB, psA[0:64, :], AF.Identity,
                                          bias=b["w_l2a_bqv"][:, h:h + 1]),
              reads=_akeys() + ["w_l2a_bqv", "l2a_carve"], writes=["l2a_qv"],
              tag=f"l2a_qv{h}")
        fw.op("tensor", lambda e, db=db: qkvmm(e, 1, db),
              reads=[f"l2a_wqkv:{db}", "l2a_qu", "l2a_qv"] + xkeys,
              writes=_akeys(), tag=f"l2a_kmm{h}")
        fw.op("scalar",
              lambda e, h=h: e.activation(kkB, psA[0:64, :], AF.Identity,
                                          bias=b["w_l2a_bk"][:, h:h + 1]),
              reads=_akeys() + ["w_l2a_bk", "l2a_carve"], writes=["l2a_kk"],
              tag=f"l2a_kk{h}")
        fw.op("tensor", lambda e, db=db: qkvmm(e, 2, db),
              reads=[f"l2a_wqkv:{db}", "l2a_kk"] + xkeys,
              writes=_akeys(), tag=f"l2a_vmm{h}")
        fw.op("scalar",
              lambda e, h=h: e.activation(vvB, psA[0:64, :], AF.Identity,
                                          bias=b["w_l2a_bv"][:, h:h + 1]),
              reads=_akeys() + ["w_l2a_bv", "l2a_carve"], writes=["l2a_vv"],
              tag=f"l2a_vv{h}")
        pvb = psB.bitcast(BF16)

        def vtr(e):
            last = None
            for c in range(16):
                last = e.transpose(pvb[:, c * 64:(c + 1) * 64],
                                   vvB[:, c * 128:(c + 1) * 128],
                                   b["c_identb"][0:64, 0:64])
            return last
        fw.op("tensor", vtr, reads=["l2a_vv", "c_identb"], writes=["B0"],
              tag=f"l2a_vtr{h}")
        fw.op("vector",
              lambda e: e.tensor_copy(Vtok[:, :, 0:64],
                                      pvb[:, 0:1024]
                                      .rearrange("p (c d) -> p c d", c=16)),
              reads=["B0", "l2a_vt1"], writes=["l2a_vtok"],
              tag=f"l2a_vtc{h}")

        for m in range(4):
            def gemm(e, m=m, db=db, which=0):
                last = None
                for c in range(TC):
                    sl = slice(c * 512, (c + 1) * 512)
                    last = e.matmul(psA[:, sl],
                                    wpe[:, db, m * 128 + which * 512:
                                        m * 128 + which * 512 + 128],
                                    qvB[:, sl], start=True, stop=True)
                return last
            fw.op("tensor", lambda e, m=m, db=db: gemm(e, m, db, 0),
                  reads=[f"l2a_wpe:{db}", "l2a_qv"], writes=_akeys(),
                  tag=f"l2a_ge{h}_{m}")
            fw.op("vector",
                  lambda e: e.tensor_copy(gbuf, psA[:, :]),
                  reads=_akeys(), writes=["l2a_gb"], tag=f"l2a_gec{h}_{m}")
            fw.op("tensor", lambda e, m=m, db=db: gemm(e, m, db, 1),
                  reads=[f"l2a_wpe:{db}", "l2a_qv"], writes=_akeys(),
                  tag=f"l2a_go{h}_{m}")
            fw.op("vector",
                  lambda e, m=m: e.tensor_tensor(Ubuf[:, m, :], psA[:, :],
                                                 SST[:, m, :], OP.mult),
                  reads=_akeys() + ["l2a_SS"], writes=["l2a_U"],
                  tag=f"l2a_u1{h}_{m}")
            fw.op("vector",
                  lambda e, m=m: e.tensor_tensor(tsc[:, 0:T], gbuf,
                                                 CCT[:, m, :], OP.mult),
                  reads=["l2a_gb", "l2a_CC"], writes=["l2a_ts"],
                  tag=f"l2a_u2{h}_{m}")
            fw.op("vector",
                  lambda e, m=m: e.tensor_tensor(Ubuf[:, m, :],
                                                 Ubuf[:, m, :],
                                                 tsc[:, 0:T], OP.subtract),
                  reads=["l2a_U", "l2a_ts"], writes=["l2a_U"],
                  tag=f"l2a_u3{h}_{m}")
            fw.op("vector",
                  lambda e, m=m: e.tensor_tensor(Vbuf[:, m, :], psA[:, :],
                                                 CCT[:, m, :], OP.mult),
                  reads=_akeys() + ["l2a_CC"], writes=["l2a_V"],
                  tag=f"l2a_v1{h}_{m}")
            fw.op("vector",
                  lambda e, m=m: e.tensor_tensor(tsc[:, 0:T], gbuf,
                                                 SST[:, m, :], OP.mult),
                  reads=["l2a_gb", "l2a_SS"], writes=["l2a_ts"],
                  tag=f"l2a_v2{h}_{m}")
            fw.op("vector",
                  lambda e, m=m: e.tensor_tensor(Vbuf[:, m, :],
                                                 Vbuf[:, m, :],
                                                 tsc[:, 0:T], OP.add),
                  reads=["l2a_V", "l2a_ts"], writes=["l2a_V"],
                  tag=f"l2a_v3{h}_{m}")

        for tg in range(TC):
            tsl = slice(tg * 512, (tg + 1) * 512)
            avsl = slice(512, 1024)
            smsl = slice(1536, 2048)
            for jt in range(16):
                ssl = slice((jt % 2) * 512, (jt % 2) * 512 + 512)
                kjt = f"A{jt % 2}"

                def smm(e, jt=jt, ssl=ssl, tsl=tsl):
                    e.matmul(psA[:, ssl], kkB[:, jt * 128:(jt + 1) * 128],
                             quB[:, tsl], start=True, stop=False)
                    for m in range(4):
                        e.matmul(psA[:, ssl],
                                 SST[:, m, jt * 128:(jt + 1) * 128],
                                 Ubuf[:, m, tsl], start=False, stop=False)
                    for m in range(4):
                        last = e.matmul(psA[:, ssl],
                                        CCT[:, m, jt * 128:(jt + 1) * 128],
                                        Vbuf[:, m, tsl], start=False,
                                        stop=(m == 3))
                    return last
                fw.op("tensor", smm,
                      reads=["l2a_kk", "l2a_qu", "l2a_U", "l2a_V",
                             "l2a_SS", "l2a_CC"],
                      writes=[kjt], tag=f"l2a_s{h}_{tg}_{jt}")
                fw.op("scalar",
                      lambda e, jt=jt, ssl=ssl: e.activation(
                          at2[:, jt % 2, :], psA[:, ssl], AF.Exp,
                          bias=negten[:, :], scale=0.125),
                      reads=[kjt, "c_negten"], writes=[f"l2a_at:{jt % 2}"],
                      tag=f"l2a_e{h}_{tg}_{jt}")

                def avmm(e, jt=jt, avsl=avsl, smsl=smsl):
                    e.matmul(psB[0:65, avsl], Vtok[:, jt, 0:65],
                             at2[:, jt % 2, :], start=(jt == 0),
                             stop=(jt == 15))
                    return e.matmul(psB[0:1, smsl],
                                    b["c_onescolb"][:, :],
                                    at2[:, jt % 2, :], start=(jt == 0),
                                    stop=(jt == 15))
                fw.op("tensor", avmm,
                      reads=[f"l2a_at:{jt % 2}", "l2a_vtok", "c_onescolb"],
                      writes=["B1", "B3"], tag=f"l2a_av{h}_{tg}_{jt}")
            fw.op("vector",
                  lambda e: e.reciprocal(rsr[:, 0:512], psB[0:1, smsl]),
                  reads=["B3"], writes=["rowB0"], tag=f"l2a_rc{h}_{tg}")

            def rbc(e):
                return e.matmul(psA[0:64, 0:512], onesrow[0:1, 0:64],
                                rsr[:, 0:512], start=True, stop=True)
            fw.op("tensor", rbc, reads=["rowB0", "c_onesrow"],
                  writes=["A0"], tag=f"l2a_rb{h}_{tg}")
            fw.op("scalar",
                  lambda e: e.copy(tsc[0:64, 0:512], psB[0:64, avsl]),
                  reads=["B1"], writes=["l2a_ts"], tag=f"l2a_cc{h}_{tg}")
            fw.op("vector",
                  lambda e, tsl=tsl: e.tensor_tensor(
                      ctxn[:, tsl], tsc[0:64, 0:512], psA[0:64, 0:512],
                      OP.mult),
                  reads=["l2a_ts", "A0"], writes=["l2a_ctxn"],
                  tag=f"l2a_cn{h}_{tg}")
        fw.op("sync",
              lambda e, h=h: [e.dma_start(out=ctxstage[h], in_=ctxn)],
              reads=["l2a_ctxn"], writes=[f"l2a_cst:{h}"], dma=True,
              tag=f"l2a_stc{h}")

    ctxall = xln
    fw.alias(["l2a_call"], [f"xln:{k}" for k in range(KT)] +
             [f"l2a_cst:{h}" for h in range(16)])
    fw.op("gpsimd",
          lambda e: [e.dma_start(
              out=ctxall[:, :, :],
              in_=ctxstage.rearrange("(a c) d t -> (c d) a t", a=KT))],
          reads=["l2a_call"], writes=["l2a_call"], dma=True,
          tag="l2a_ldctx")
    wo = b.S("l2a_wo", [128, 2, KT, 128], BF16)
    xrow = b.S("l2a_xr", [128, 1, T], F32)
    hbuf = b.S("l2a_hbuf", [128, T], BF16)
    xi = p.x_src_ap
    xo = p.x_dst_ap
    bout = b["w_l2a_bout"]
    for o in range(KT):
        db = o % 2
        fw.op("gpsimd",
              lambda e, o=o, db=db: [e.dma_start(
                  out=wo[:, db], in_=wout_d[:, :, o * 128:(o + 1) * 128])],
              writes=[f"l2a_wo:{db}"], dma=True, tag=f"l2a_ldwo{o}")

        def omm(e, db=db):
            last = None
            for k in range(KT):
                for c in range(TC):
                    sl = slice(c * 512, (c + 1) * 512)
                    last = e.matmul(psA[:, sl], wo[:, db, k, :],
                                    ctxall[:, k, sl], start=(k == 0),
                                    stop=(k == KT - 1))
            return last
        fw.op("tensor", omm, reads=[f"l2a_wo:{db}", "l2a_call"],
              writes=_akeys(), tag=f"l2a_omm{o}")
        fw.op("sync",
              lambda e, o=o: [e.dma_start(out=xrow[:, 0, :],
                                          in_=xi[:, o, :])],
              writes=["l2a_xr"], dma=True, tag=f"l2a_ldx{o}")
        fw.op("scalar",
              lambda e, o=o: e.activation(hbuf[:, :], psA[:, :],
                                          AF.Identity,
                                          bias=bout[:, o:o + 1]),
              reads=_akeys() + ["w_l2a_bout"], writes=["l2a_hb"],
              tag=f"l2a_hb{o}")
        fw.op("vector",
              lambda e: e.tensor_tensor(xrow[:, 0, :], xrow[:, 0, :],
                                        hbuf[:, :], OP.add),
              reads=["l2a_xr", "l2a_hb"], writes=["l2a_xr"],
              tag=f"l2a_add{o}")
        fw.op("sync",
              lambda e, o=o: [e.dma_start(out=xo[:, o, :],
                                          in_=xrow[:, 0, :])],
              reads=["l2a_xr"], dma=True, tag=f"l2a_stx{o}")


def host_prep_l3attn(inp):
    out = {}
    Win = inp["L3_attn_W_in"]
    out["w_l3a_win"] = lhsT_prep(Win)
    cw = inp["L3_attn_conv_w"][:, 0, :]
    out["w_l3a_cwx"] = _f(cw[0:2048].reshape(16, 128, 4).transpose(1, 0, 2))
    out["w_l3a_cwB"] = _f(cw[2048:2064].reshape(16, 4))
    out["w_l3a_cwC"] = _f(cw[2064:2080].reshape(16, 4))
    cb = inp["L3_attn_conv_b"]
    out["w_l3a_cbx"] = _f(cb[0:2048].reshape(16, 128).T)
    out["w_l3a_cbB"] = _f(cb[2048:2064].reshape(16, 1))
    out["w_l3a_cbC"] = _f(cb[2064:2080].reshape(16, 1))
    out["w_l3a_dtb"] = _f(inp["L3_attn_dt_bias"].reshape(64, 1))
    out["w_l3a_alog"] = _f(inp["L3_attn_A_log"].reshape(64, 1))
    Dch = np.repeat(inp["L3_attn_D"], 32)
    out["w_l3a_Dch"] = _f(Dch.reshape(16, 128).T)
    nw = np.asarray(inp["L3_attn_norm_w"])
    out["w_l3a_nw"] = _f(nw.reshape(16, 128).T)
    out["w_l3a_wout"] = lhsT_prep(inp["L3_attn_W_out"])
    out["c_l3a_mask"] = _f(np.triu(np.ones((128, 128), np.float32)))
    return out


def emit_l3attn(p):
    fw, b = p.fw, p.b
    psA, psB = p.psA, p.psB
    xln = b["xln"]
    sbx = b["x"]
    NCH, Cc_ = 16, 128
    for nm, shp in (("cwx", [128, 16, 4]), ("cbx", [128, 16]),
                    ("cwB", [16, 4]), ("cwC", [16, 4]),
                    ("cbB", [16, 1]), ("cbC", [16, 1]),
                    ("dtb", [64, 1]), ("alog", [64, 1]),
                    ("Dch", [128, 16]), ("nw", [128, 16])):
        p.load(f"w_l3a_{nm}", shp, dt=F32)
    p.load("c_l3a_mask", [128, 128], dt=F32)
    win_d = p.din("w_l3a_win", [128, KT, 4192])
    wout_d = p.din("w_l3a_wout", [128, 16, 1024])
    zstage = p.dscratch("l3a_z", [16, 128, T], BF16)
    xcstage = p.dscratch("l3a_xcs", [16, 128, T], BF16)
    acstage = p.dscratch("l3a_acs", [64, T], F32)

    xrawstage = p.dscratch("l3a_xrs", [16, 128, T], BF16)
    Bf = b.S("l3a_B", [16, T], BF16)
    Cf = b.S("l3a_C", [16, T], BF16)
    dt_ = b.S("l3a_dt", [64, T], F32)
    Acum = b.S("l3a_Acum", [64, T], F32)
    AcT = b.S("l3a_AcT", [128, NCH, 64], F32)
    xlnB = b["xln"][:, :, :].rearrange("p k t -> p (k t)")
    xlnF = b["xln"][:, :, :].bitcast(F32).rearrange("p k t -> p (k t)")
    prevT = xlnF[0:16, 4608:6656]
    prevTb = xlnB[0:16, 7168:9216]
    nA = b.S("l3a_nA", [64, 1], F32)
    lnscr = b["lnscr"]
    lnF = lnscr[:, :, :].bitcast(F32).rearrange("p k t -> p (k t)")
    lnB = lnscr[:, :, :].rearrange("p k t -> p (k t)")
    accv = lnF[:, 0:T]
    fw.alias(["l3a_acc", "l3a_zw:0", "l3a_zw:1", "l3a_xcw:0",
              "l3a_xcw:1", "l3a_ygc", "l3a_sqc"],
             [f"lnscr:{k}" for k in range(KT)])
    AcumBC = sbx[:, 4:8, :].rearrange("p k t -> p (k t)").rearrange(
        "p (h i) -> p h i", h=64)
    xvb = sbx[:, 0:4, :].bitcast(BF16).rearrange("p k t -> p (k t)")
    Mbuf = xvb[:, 0:8192].rearrange("p (h i) -> p h i", h=64)
    CcE = xvb[0:16, 8192:16384].rearrange("p (h i) -> p h i", h=64)
    woutv = sbx[:, 4:8, :].bitcast(BF16).rearrange("p k t -> p (k t)") \
        [:, 0:16384].rearrange("p (k o) -> p k o", k=16)
    xcc = xlnB[:, 0:2048].rearrange("p (k t) -> p k t", k=16)
    xcT = xlnB[:, 2048:4096]
    zc = xlnB[:, 4096:6144]
    ygc = lnB[:, 12288:14336]
    sqc = lnB[:, 14336:16384]
    dtT = b.S("l3a_dtT", [128, 64], BF16)
    Btok = b.S("l3a_Bt", [128, 16], BF16)
    dterm = b.S("l3a_dl", [64, Cc_], F32)
    dlT = b.S("l3a_dlT", [128, 64], BF16)
    Bd = xlnB[:, 6144:7168]
    gcb = xlnB[0:16, 13312:15360]
    CBm = b.S("l3a_CBm", [128, Cc_], BF16)
    dtbfc = b.S("l3a_dtbf", [64, Cc_], BF16)
    dlbf = b.S("l3a_dlbf", [64, Cc_], BF16)
    Bbfc = b.S("l3a_Bbf", [16, Cc_], BF16)
    xrow = b.S("l3a_xr", [128, KT, Cc_], F32)
    ident, identb = b["c_ident"], b["c_identb"]
    xkeys = [f"xln:{k}" for k in range(KT)]

    wbuf = b.S("l3a_wb", [128, 2, KT, 128], BF16)
    zw = lnB[:, 4096:8192].rearrange("p (d t) -> p d t", d=2)
    ogroups = [(o, o * 128, 128) for o in range(32)] + \
              [(32, 4128, 64), (33, 4096, 16), (34, 4112, 16)]
    for oi, (o, col0, ncol) in enumerate(ogroups):
        db = oi % 2
        fw.op("gpsimd",
              lambda e, col0=col0, ncol=ncol, db=db: [e.dma_start(
                  out=wbuf[:, db, :, 0:ncol],
                  in_=win_d[:, :, col0:col0 + ncol])],
              writes=[f"l3a_wb:{db}"], dma=True, tag=f"l3a_ldw{oi}")

        def pmm(e, db=db, ncol=ncol):
            last = None
            for k in range(KT):
                for c in range(TC):
                    sl = slice(c * 512, (c + 1) * 512)
                    last = e.matmul(psA[0:ncol, sl],
                                    wbuf[:, db, k, 0:ncol], xln[:, k, sl],
                                    start=(k == 0), stop=(k == KT - 1))
            return last
        fw.op("tensor", pmm, reads=[f"l3a_wb:{db}"] + xkeys,
              writes=_akeys(), tag=f"l3a_pmm{oi}")
        if o < 16:
            fw.op("scalar",
                  lambda e, db=db: e.copy(zw[:, db, :], psA[:, :]),
                  reads=_akeys(), writes=[f"l3a_zw:{db}"],
                  tag=f"l3a_zc{oi}")
            fw.op("sync",
                  lambda e, o=o, db=db: [e.dma_start(out=zstage[o],
                                                     in_=zw[:, db, :])],
                  reads=[f"l3a_zw:{db}"], writes=[f"zst:{o}"], dma=True,
                  tag=f"l3a_zst{oi}")
        elif o < 32:
            fw.op("scalar",
                  lambda e, db=db: e.copy(zw[:, db, :], psA[:, :]),
                  reads=_akeys(), writes=[f"l3a_zw:{db}"],
                  tag=f"l3a_xrw{oi}")
            fw.op("sync",
                  lambda e, o=o, db=db: [e.dma_start(
                      out=xrawstage[o - 16], in_=zw[:, db, :])],
                  reads=[f"l3a_zw:{db}"], writes=[f"xrst:{o - 16}"],
                  dma=True, tag=f"l3a_xrst{oi}")
        elif o == 32:
            dtb = b["w_l3a_dtb"]
            fw.op("scalar",
                  lambda e: e.activation(dt_[:, :], psA[0:64, :],
                                         AF.Identity, bias=dtb[:, :]),
                  reads=_akeys() + ["w_l3a_dtb"], writes=["l3a_dt"],
                  tag="l3a_dtr")
            fw.op("scalar",
                  lambda e: e.activation(accv[0:64, :], dt_[:, :], AF.Abs),
                  reads=["l3a_dt"], writes=["l3a_acc"], tag="l3a_sp1")
            fw.op("scalar",
                  lambda e: e.activation(accv[0:64, :], accv[0:64, :],
                                         AF.Exp, scale=-1.0),
                  reads=["l3a_acc"], writes=["l3a_acc"], tag="l3a_sp2")
            fw.op("scalar",
                  lambda e: e.activation(accv[0:64, :], accv[0:64, :],
                                         AF.Ln,
                                         bias=b["c_onescol"][0:64, :]),
                  reads=["l3a_acc", "c_onescol"], writes=["l3a_acc"],
                  tag="l3a_sp3")
            fw.op("vector",
                  lambda e: e.tensor_scalar(dt_[:, :], dt_[:, :], 0.0,
                                            None, OP.max),
                  reads=["l3a_dt"], writes=["l3a_dt"], tag="l3a_sp4")
            fw.op("vector",
                  lambda e: e.tensor_tensor(dt_[:, :], dt_[:, :],
                                            accv[0:64, :], OP.add),
                  reads=["l3a_dt", "l3a_acc"], writes=["l3a_dt"],
                  tag="l3a_sp5")
        else:
            dst = Bf if o == 33 else Cf
            key = "l3a_B" if o == 33 else "l3a_C"
            fw.op("scalar",
                  lambda e, dst=dst: e.copy(dst[:, :], psA[0:16, :]),
                  reads=_akeys(), writes=[key], tag=f"l3a_bc{oi}")

    fw.alias(["l3a_xcc", "l3a_xcT", "l3a_zc", "l3a_Bd", "l3a_pv",
              "l3a_pvb"], [f"xln:{k}" for k in range(KT)])

    fw.op("scalar", lambda e: e.activation(nA[:, :], b["w_l3a_alog"][:, :],
                                           AF.Exp),
          reads=["w_l3a_alog"], writes=["l3a_nA"], tag="l3a_eA")
    fw.op("vector", lambda e: e.tensor_scalar(nA[:, :], nA[:, :], -1.0,
                                              None, OP.mult),
          reads=["l3a_nA"], writes=["l3a_nA"], tag="l3a_negA")
    fw.op("vector",
          lambda e: e.tensor_scalar(accv[0:64, :], dt_[:, :], nA[:, :],
                                    None, OP.mult),
          reads=["l3a_dt", "l3a_nA"], writes=["l3a_acc"], tag="l3a_dtA")

    def scans(e):
        last = None
        for c in range(NCH):
            sl = slice(c * Cc_, (c + 1) * Cc_)
            last = e.tensor_tensor_scan(Acum[:, sl], accv[0:64, sl],
                                        accv[0:64, sl], 0.0, OP.add,
                                        OP.bypass)
        return last
    fw.op("vector", scans, reads=["l3a_acc"], writes=["l3a_Acum"],
          tag="l3a_scan")
    fw.op("sync", lambda e: [e.dma_start(out=acstage, in_=Acum[:, :])],
          reads=["l3a_Acum"], writes=["acst"], dma=True, tag="l3a_acst")
    for c in range(NCH):
        fw.op("tensor",
              lambda e, c=c: e.transpose(psB[0:128, 0:64],
                                         Acum[:, c * Cc_:(c + 1) * Cc_],
                                         ident[0:64, 0:64]),
              reads=["l3a_Acum", "c_ident"], writes=["B0"],
              tag=f"l3a_actr{c}")
        fw.op("vector",
              lambda e, c=c: e.tensor_copy(AcT[:, c, :], psB[:, 0:64]),
              reads=["B0"], writes=["l3a_AcT"], tag=f"l3a_actc{c}")

    cwx, cbx = b["w_l3a_cwx"], b["w_l3a_cbx"]
    xcw = lnB[:, 8192:12288].rearrange("p (d t) -> p d t", d=2)
    xrb = zw
    for k in range(16):
        db = k % 2
        fw.op("gpsimd",
              lambda e, k=k, db=db: [e.dma_start(out=xrb[:, db, :],
                                                 in_=xrawstage[k])],
              reads=[f"xrst:{k}"], writes=[f"l3a_zw:{db}"], dma=True,
              tag=f"l3a_ldxr{k}")

        def conv(e, k=k, db=db):
            e.tensor_scalar(accv[:, :], xrb[:, db, :], cwx[:, k, 3:4],
                            None, OP.mult)
            e.drain()
            for tau in range(3):
                off = 3 - tau
                e.scalar_tensor_tensor(
                    accv[:, off:T], xrb[:, db, 0:T - off],
                    cwx[:, k, tau:tau + 1], accv[:, off:T],
                    OP.mult, OP.add)
                e.drain()
            return e.engine_nop()
        fw.op("vector", conv,
              reads=[f"l3a_zw:{db}", "w_l3a_cwx"], writes=["l3a_acc"],
              tag=f"l3a_cv{k}")
        fw.op("scalar",
              lambda e, k=k, db=db: e.activation(xcw[:, db, :], accv[:, :],
                                                 AF.Silu,
                                                 bias=cbx[:, k:k + 1]),
              reads=["l3a_acc", "w_l3a_cbx"], writes=[f"l3a_xcw:{db}"],
              tag=f"l3a_cs{k}")
        fw.op("sync",
              lambda e, k=k, db=db: [e.dma_start(out=xcstage[k],
                                                 in_=xcw[:, db, :])],
              reads=[f"l3a_xcw:{db}"], writes=[f"xcst:{k}"], dma=True,
              tag=f"l3a_xst{k}")
    for which, buf, key in ((0, Bf, "l3a_B"), (1, Cf, "l3a_C")):
        cwW = b["w_l3a_cwB" if which == 0 else "w_l3a_cwC"]
        cbW = b["w_l3a_cbB" if which == 0 else "w_l3a_cbC"]
        cwk = "w_l3a_cwB" if which == 0 else "w_l3a_cwC"
        cbk = "w_l3a_cbB" if which == 0 else "w_l3a_cbC"

        def convbc(e, buf=buf, cwW=cwW):
            e.tensor_scalar(accv[0:16, :], buf[:, :], cwW[:, 3:4],
                            None, OP.mult)
            e.drain()
            for tau in range(3):
                off = 3 - tau
                e.scalar_tensor_tensor(
                    accv[0:16, off:T], buf[:, 0:T - off],
                    cwW[:, tau:tau + 1], accv[0:16, off:T],
                    OP.mult, OP.add)
                e.drain()
            return e.engine_nop()
        fw.op("vector", convbc, reads=[key, cwk],
              writes=["l3a_acc"], tag=f"l3a_cvbc{which}")
        fw.op("scalar",
              lambda e, buf=buf, cbW=cbW: e.activation(
                  buf[:, :], accv[0:16, :], AF.Silu, bias=cbW[:, :]),
              reads=["l3a_acc", cbk], writes=[key],
              tag=f"l3a_csbc{which}")

    fw.op("vector", lambda e: e.memset(prevT[:, :], 0.0),
          writes=["l3a_pv"], tag="l3a_pv0")

    maskc = b["c_l3a_mask"]
    Dch, nw = b["w_l3a_Dch"], b["w_l3a_nw"]
    onesrow = b["c_onesrow"]
    onescolb = b["c_onescolb"]
    rsr = b["rowsB"][0:1, :]
    xi = p.x_src_ap
    xo = p.x_dst_ap
    wov = b.S("l3a_wov", [128, 2, 16, 128], BF16)
    wo_bfst = p.dscratch("l3a_wobf", [128, 16, 1024], BF16)
    fw.op("gpsimd",
          lambda e: [e.dma_start(out=wo_bfst, in_=wout_d)],
          writes=["l3a_wobf"], dma=True, tag="l3a_stgwo")

    def emit_tail(c):
        fw.op("sync",
              lambda e, c=c: [e.dma_start(
                  out=xrow[:],
                  in_=xi[:, :, c * Cc_:(c + 1) * Cc_])],
              writes=["l3a_xr"], dma=True, tag=f"l3a_ldx{c}")
        fw.op("vector",
              lambda e: e.tensor_tensor(
                  xrow[:].rearrange("p k t -> p (k t)"),
                  xrow[:].rearrange("p k t -> p (k t)"),
                  psB[:, 1024:2048], OP.add),
              reads=["l3a_xr", "B2", "B3"], writes=["l3a_xr"],
              tag=f"l3a_xa{c}")
        if len(xo.shape) == 4:
            fw.op("sync",
                  lambda e, c=c: [e.dma_start(
                      out=xo[:, :, c // 2,
                             (c % 2) * Cc_:(c % 2) * Cc_ + Cc_],
                      in_=xrow[:])],
                  reads=["l3a_xr"], dma=True, tag=f"l3a_stx{c}")
        else:
            fw.op("sync",
                  lambda e, c=c: [e.dma_start(
                      out=xo[:, :, c * Cc_:(c + 1) * Cc_], in_=xrow[:])],
                  reads=["l3a_xr"], dma=True, tag=f"l3a_stx{c}")

    def emit_ldab(c):
        if c == 0:
            fw.alias(["l3a_abc", "l3a_abc2"],
                     [f"x:{k}" for k in range(KT)])
        fw.op("sync",
              lambda e, c=c: [e.dma_start(
                  out=AcumBC[0:64],
                  in_=acstage[:, c * Cc_:(c + 1) * Cc_].unsqueeze(0)
                  .to_broadcast([64, 64, Cc_]))],
              reads=["acst"], writes=["l3a_abc"], dma=True,
              tag=f"l3a_ldab{c}")
        fw.op("gpsimd",
              lambda e, c=c: [e.dma_start(
                  out=AcumBC[64:128],
                  in_=acstage[:, c * Cc_:(c + 1) * Cc_].unsqueeze(0)
                  .to_broadcast([64, 64, Cc_]))],
              reads=["acst"], writes=["l3a_abc2"], dma=True,
              tag=f"l3a_ldab2_{c}")

    emit_ldab(0)
    for c in range(NCH):
        csl = slice(c * Cc_, (c + 1) * Cc_)
        fw.op("gpsimd",
              lambda e, c=c: [e.dma_start(
                  out=xcc[:],
                  in_=xcstage[:, :, c * Cc_:(c + 1) * Cc_]
                  .rearrange("k p t -> p k t"))],
              reads=[f"xcst:{k}" for k in range(16)], writes=["l3a_xcc"],
              dma=True, tag=f"l3a_ldxc{c}")
        fw.op("gpsimd",
              lambda e, c=c: [e.dma_start(
                  out=zc[:].rearrange("p (k t) -> p k t", k=16),
                  in_=zstage[:, :, c * Cc_:(c + 1) * Cc_]
                  .rearrange("k p t -> p k t"))],
              reads=[f"zst:{k}" for k in range(16)], writes=["l3a_zc"],
              dma=True, tag=f"l3a_ldz{c}")
        fw.op("scalar",
              lambda e: e.activation(
                  gcb[0:16, 0:64],
                  AcumBC[0:16, :, Cc_ - 1], AF.Exp),
              reads=["l3a_abc"], writes=["l3a_gcb"], tag=f"l3a_gcb{c}")
        fw.op("scalar",
              lambda e: e.activation(CcE[:, :, :], AcumBC[0:16, :, :],
                                     AF.Exp),
              reads=["l3a_abc"], writes=["l3a_CcE"], tag=f"l3a_cce1{c}")
        fw.op("vector",
              lambda e, csl=csl: e.tensor_tensor(
                  CcE[:, :, :], CcE[:, :, :],
                  Cf[:, csl].unsqueeze(1).to_broadcast([16, 64, Cc_]),
                  OP.mult),
              reads=["l3a_CcE", "l3a_C"], writes=["l3a_CcE"],
              tag=f"l3a_cce2{c}")
        fw.op("tensor",
              lambda e, csl=csl: e.matmul(psB[:, 0:Cc_], Bf[:, csl],
                                          Cf[:, csl], start=True,
                                          stop=True),
              reads=["l3a_B", "l3a_C"], writes=["B0"], tag=f"l3a_cb{c}")
        fw.op("vector",
              lambda e: e.tensor_tensor(CBm[:, :], maskc[:, :],
                                        psB[:, 0:Cc_], OP.mult),
              reads=["c_l3a_mask", "B0"], writes=["l3a_CBm"],
              tag=f"l3a_cbm{c}")
        fw.op("vector",
              lambda e, c=c: e.scalar_tensor_tensor(
                  AcumBC[:, :, :],
                  AcT[:, c, :].unsqueeze(2).to_broadcast([128, 64, Cc_]),
                  -1.0, AcumBC[:, :, :], OP.mult, OP.add),
              reads=["l3a_AcT", "l3a_abc", "l3a_abc2"],
              writes=["l3a_abc", "l3a_abc2"],
              tag=f"l3a_seg{c}")
        fw.op("vector",
              lambda e: e.tensor_scalar(AcumBC[:, :, :], AcumBC[:, :, :],
                                        0.0, None, OP.min),
              reads=["l3a_abc", "l3a_abc2"],
              writes=["l3a_abc", "l3a_abc2"], tag=f"l3a_cl{c}")
        fw.op("scalar",
              lambda e: e.activation(Mbuf[:, :, :], AcumBC[:, :, :],
                                     AF.Exp),
              reads=["l3a_abc", "l3a_abc2"], writes=["l3a_M"],
              tag=f"l3a_me{c}")
        if c + 1 < NCH:
            emit_ldab(c + 1)
        fw.op("vector",
              lambda e: e.tensor_tensor(
                  Mbuf[:, :, :], Mbuf[:, :, :],
                  CBm[:, :].unsqueeze(1).to_broadcast([128, 64, Cc_]),
                  OP.mult),
              reads=["l3a_M", "l3a_CBm"], writes=["l3a_M"],
              tag=f"l3a_mc{c}")
        def xtr2(e):
            last = None
            pab = psA.bitcast(BF16)
            for k in range(16):
                last = e.transpose(pab[:, k * 128:(k + 1) * 128],
                                   xcc[:, k, :], identb[:, :])
            return last
        fw.op("tensor", xtr2, reads=["l3a_xcc", "c_identb"],
              writes=["A0", "A1"], tag=f"l3a_xtr{c}")
        fw.op("vector",
              lambda e: e.tensor_copy(xcT[:, :],
                                      psA.bitcast(BF16)[:, 0:2048]),
              reads=["A0", "A1"], writes=["l3a_xcT"], tag=f"l3a_xtc{c}")
        fw.op("vector",
              lambda e, csl=csl: e.tensor_copy(dtbfc[:, :], dt_[:, csl]),
              reads=["l3a_dt"], writes=["l3a_dtbf"], tag=f"l3a_dtb{c}")
        fw.op("scalar",
              lambda e, c=c, csl=csl: e.activation(
                  dterm[:, :], Acum[:, csl], AF.Exp, scale=-1.0,
                  bias=Acum[:, c * Cc_ + Cc_ - 1:c * Cc_ + Cc_]),
              reads=["l3a_Acum"], writes=["l3a_dl"], tag=f"l3a_dle{c}")
        fw.op("vector", lambda e: e.tensor_copy(dlbf[:, :], dterm[:, :]),
              reads=["l3a_dl"], writes=["l3a_dlbf"], tag=f"l3a_dlb{c}")
        fw.op("vector",
              lambda e, csl=csl: e.tensor_copy(Bbfc[:, :], Bf[:, csl]),
              reads=["l3a_B"], writes=["l3a_Bbf"], tag=f"l3a_bbf{c}")
        pbb = psB.bitcast(BF16)

        def smalltr(e):
            e.transpose(pbb[:, 1024:1088], dtbfc[:, :], identb[0:64, 0:64])
            e.transpose(pbb[:, 1088:1152], dlbf[:, :], identb[0:64, 0:64])
            return e.transpose(pbb[:, 1152:1168], Bbfc[:, :],
                               identb[0:16, 0:16])
        fw.op("tensor", smalltr,
              reads=["l3a_dtbf", "l3a_dlbf", "l3a_Bbf", "c_identb"],
              writes=["B1"], tag=f"l3a_str{c}")
        fw.op("vector",
              lambda e: e.tensor_copy(dtT[:, :], pbb[:, 1024:1088]),
              reads=["B1"], writes=["l3a_dtT"], tag=f"l3a_dttc{c}")
        fw.op("vector",
              lambda e: e.tensor_copy(dlT[:, :], pbb[:, 1088:1152]),
              reads=["B1"], writes=["l3a_dlT"], tag=f"l3a_dltc{c}")
        fw.op("vector",
              lambda e: e.tensor_copy(Btok[:, :], pbb[:, 1152:1168]),
              reads=["B1"], writes=["l3a_Btok"], tag=f"l3a_btc{c}")
        fw.op("vector",
              lambda e: e.tensor_tensor(
                  xcT[:, :].rearrange("p (h q) -> p h q", h=64),
                  xcT[:, :].rearrange("p (h q) -> p h q", h=64),
                  dtT[:, :].unsqueeze(2).to_broadcast([128, 64, 32]),
                  OP.mult),
              reads=["l3a_xcT", "l3a_dtT"], writes=["l3a_xcT"],
              tag=f"l3a_xdt{c}")
        fw.op("vector",
              lambda e: e.tensor_tensor(
                  Bd[:, :].rearrange("p (h n) -> p h n", h=64),
                  Btok[:, :].unsqueeze(1).to_broadcast([128, 64, 16]),
                  dlT[:, :].unsqueeze(2).to_broadcast([128, 64, 16]),
                  OP.mult),
              reads=["l3a_Btok", "l3a_dlT"], writes=["l3a_Bd"],
              tag=f"l3a_bd{c}")
        fw.op("vector", lambda e: e.tensor_copy(prevTb[:, :], prevT[:, :]),
              reads=["l3a_pv"], writes=["l3a_pvb"], tag=f"l3a_pvb{c}")
        def ymm(e):
            last = None
            for g in range(16):
                gs = slice(g * 128, (g + 1) * 128)
                for j in range(4):
                    h = g * 4 + j
                    tp = (0, 96) if j == 3 else None
                    e.matmul(psA[32 * j:32 * j + 32, gs],
                             xcT[:, h * 32:(h + 1) * 32],
                             Mbuf[:, h, :], start=True, stop=False,
                             tile_position=tp)
                    last = e.matmul(psA[32 * j:32 * j + 32, gs],
                                    prevTb[:, h * 32:(h + 1) * 32],
                                    CcE[:, h, :], start=False, stop=True,
                                    tile_position=tp)
            return last
        fw.op("tensor", ymm,
              reads=["l3a_xcT", "l3a_M", "l3a_pvb", "l3a_CcE"],
              writes=_akeys(), tag=f"l3a_ymm{c}")
        def stmm(e):
            last = None
            for h in range(64):
                last = e.matmul(psB[0:16, h * 32:(h + 1) * 32],
                                Bd[:, h * 16:(h + 1) * 16],
                                xcT[:, h * 32:(h + 1) * 32],
                                start=True, stop=True)
            return last
        fw.op("tensor", stmm, reads=["l3a_Bd", "l3a_xcT"],
              writes=_bkeys(), tag=f"l3a_st{c}")
        fw.op("scalar", lambda e: e.activation(zc[:, :], zc[:, :], AF.Silu),
              reads=["l3a_zc"], writes=["l3a_zc"], tag=f"l3a_zs{c}")
        fw.op("vector",
              lambda e: e.tensor_tensor(
                  ygc[:, :].rearrange("p (k t) -> p k t", k=16),
                  xcc[:, :, :],
                  Dch[:, :].unsqueeze(2).to_broadcast([128, 16, Cc_]),
                  OP.mult),
              reads=["l3a_xcc", "w_l3a_Dch"], writes=["l3a_ygc"],
              tag=f"l3a_yd{c}")
        fw.op("vector",
              lambda e: e.tensor_tensor(ygc[:, :], ygc[:, :], psA[:, :],
                                        OP.add),
              reads=["l3a_ygc"] + _akeys(), writes=["l3a_ygc"],
              tag=f"l3a_ya{c}")
        fw.op("vector",
              lambda e: e.tensor_tensor(ygc[:, :], ygc[:, :], zc[:, :],
                                        OP.mult),
              reads=["l3a_ygc", "l3a_zc"], writes=["l3a_ygc"],
              tag=f"l3a_yz{c}")
        fw.op("vector",
              lambda e: e.tensor_tensor(
                  prevT[:, :].rearrange("p (h q) -> p h q", h=64),
                  prevT[:, :].rearrange("p (h q) -> p h q", h=64),
                  gcb[0:16, 0:64].unsqueeze(2).to_broadcast([16, 64, 32]),
                  OP.mult),
              reads=["l3a_pv", "l3a_gcb"], writes=["l3a_pv"],
              tag=f"l3a_pu1{c}")
        fw.op("vector",
              lambda e: e.tensor_tensor(prevT[:, :], prevT[:, :],
                                        psB[0:16, :], OP.add),
              reads=["l3a_pv"] + _bkeys(), writes=["l3a_pv"],
              tag=f"l3a_pu2{c}")
        fw.op("scalar", lambda e: e.activation(sqc[:, :], ygc[:, :],
                                               AF.Square),
              reads=["l3a_ygc"], writes=["l3a_sqc"], tag=f"l3a_sq{c}")

        def ssum(e):
            last = None
            for k in range(16):
                last = e.matmul(psB[0:1, 0:Cc_], onescolb[:, :],
                                sqc[:, k * 128:(k + 1) * 128],
                                start=(k == 0), stop=(k == 15))
            return last
        fw.op("tensor", ssum, reads=["l3a_sqc", "c_onescolb"],
              writes=["B0"], tag=f"l3a_ss{c}")
        fw.op("vector",
              lambda e: e.tensor_scalar(rsr[:, 0:Cc_], psB[0:1, 0:Cc_],
                                        1.0 / 2048.0, 1e-5, OP.mult,
                                        OP.add),
              reads=["B0"], writes=["rowB0"], tag=f"l3a_ms{c}")
        fw.op("scalar",
              lambda e: e.activation(rsr[:, 0:Cc_], rsr[:, 0:Cc_],
                                     AF.Ln),
              reads=["rowB0"], writes=["rowB0"], tag=f"l3a_msq{c}")
        fw.op("scalar",
              lambda e: e.activation(rsr[:, 0:Cc_], rsr[:, 0:Cc_],
                                     AF.Exp, scale=-0.5),
              reads=["rowB0"], writes=["rowB0"], tag=f"l3a_mr{c}")
        fw.op("tensor",
              lambda e: e.matmul(psA[:, 0:Cc_], onesrow[:, :],
                                 rsr[:, 0:Cc_], start=True, stop=True),
              reads=["rowB0", "c_onesrow"], writes=["A0"],
              tag=f"l3a_mb{c}")
        fw.op("vector",
              lambda e: e.tensor_tensor(
                  ygc[:, :].rearrange("p (k t) -> p k t", k=16),
                  ygc[:, :].rearrange("p (k t) -> p k t", k=16),
                  psA[:, 0:Cc_].unsqueeze(1).to_broadcast([128, 16, Cc_]),
                  OP.mult),
              reads=["l3a_ygc", "A0"], writes=["l3a_ygc"],
              tag=f"l3a_yn{c}")
        fw.op("vector",
              lambda e: e.tensor_tensor(
                  ygc[:, :].rearrange("p (k t) -> p k t", k=16),
                  ygc[:, :].rearrange("p (k t) -> p k t", k=16),
                  nw[:, :].unsqueeze(2).to_broadcast([128, 16, Cc_]),
                  OP.mult),
              reads=["l3a_ygc", "w_l3a_nw"], writes=["l3a_ygc"],
              tag=f"l3a_yw{c}")
        for o in range(KT):
            db = o % 2
            fw.op("gpsimd",
                  lambda e, o=o, db=db: [e.dma_start(
                      out=wov[:, db],
                      in_=wo_bfst[:, :, o * 128:(o + 1) * 128])],
                  reads=["l3a_wobf"], writes=[f"l3a_wov:{db}"], dma=True,
                  tag=f"l3a_ldwo{c}_{o}")

            def omm(e, o=o, db=db):
                last = None
                osl = slice(1024 + o * 128, 1024 + (o + 1) * 128)
                for k in range(16):
                    last = e.matmul(psB[:, osl],
                                    wov[:, db, k, :],
                                    ygc[:, k * 128:(k + 1) * 128],
                                    start=(k == 0), stop=(k == 15))
                return last
            fw.op("tensor", omm, reads=[f"l3a_wov:{db}", "l3a_ygc"],
                  writes=["B2" if o < 4 else "B3"],
                  tag=f"l3a_omm{c}_{o}")
        emit_tail(c)


def build_l3attn_block():
    p = Prog()
    p.load_consts()
    p.load_x()
    p.set_xio("x_in", "x_out")
    emit_ln(p, "ln3a")
    emit_l3attn(p)
    return p.finish()


def host_prep_l0attn(inp):
    out = {}
    Win = inp["L0_attn_W_in"]
    out["w_l0a_win"] = _f(Win.reshape(KT, 128, 4096).transpose(1, 0, 2))
    Wba = inp["L0_attn_W_ba"]
    Wba48 = np.zeros((1024, 48), np.float32)
    Wba48[:, 0:16] = Wba[:, 0:16]
    Wba48[:, 32:48] = Wba[:, 16:32]
    out["w_l0a_wba"] = _f(Wba48.reshape(KT, 128, 48).transpose(1, 0, 2))
    cw = inp["L0_attn_conv_w"][:, 0, :]
    out["w_l0a_cw"] = _f(cw.reshape(24, 128, 4).transpose(1, 0, 2))
    out["w_l0a_dtb"] = _f(inp["L0_attn_dt_bias"].reshape(16, 1))
    out["w_l0a_alog"] = _f(inp["L0_attn_A_log"].reshape(16, 1))
    out["w_l0a_nw"] = _f(inp["L0_attn_norm_w"].reshape(1, 64))
    out["w_l0a_wo"] = _f(inp["L0_attn_W_o"].reshape(KT, 128, 1024)
                         .transpose(1, 0, 2))
    sl_ = np.tril(np.ones((128, 128), np.float32), -1)
    out["c_l0a_msl"] = _f(sl_)
    out["c_l0a_msu"] = _f(sl_.T)
    out["c_l0a_mli"] = _f(np.triu(np.ones((128, 128), np.float32)))
    out["c_sel2"] = _f(np.kron(np.eye(2), np.ones((1, 64))))
    out["c_sel2T"] = _f(out["c_sel2"].T)
    return out


def emit_l0attn(p):
    fw, b = p.fw, p.b
    psA, psB = p.psA, p.psB
    xln = b["xln"]
    sbx = b["x"]
    NCH, C_ = 16, 128
    for nm, shp in (("cw", [128, 24, 4]), ("dtb", [16, 1]),
                    ("alog", [16, 1])):
        p.load(f"w_l0a_{nm}", shp, dt=F32)
    p.load("c_l0a_msl", [128, 128], dt=BF16, queue="gpsimd")
    p.load("c_l0a_msu", [128, 128], dt=BF16, queue="gpsimd")
    p.load("c_l0a_mli", [128, 128], dt=BF16, queue="gpsimd")
    p.load("c_sel2", [2, 128], dt=F32)
    p.load("c_sel2T", [128, 2], dt=BF16, queue="gpsimd")
    p.load("w_l0a_wo", [128, KT, 1024])
    win_d = p.din("w_l0a_win", [128, KT, 4096])
    wba_d = p.din("w_l0a_wba", [128, KT, 48])
    nw_d = p.din("w_l0a_nw", [1, 64])
    qkvstage = p.dscratch("l0a_qkv", [48, 64, T], BF16)
    zstage = p.dscratch("l0a_zst", [16, 64, T], BF16)
    cgstage = p.dscratch("l0a_cgs", [16, T], F32)
    ustage = p.dscratch("l0a_ust", [16, T], F32)
    rbstage = p.dscratch("l0a_rbs", [3, 16, T], BF16)

    cumg = b.S("l0a_cgr", [16, T], F32)
    urow = b.S("l0a_ur", [16, T], F32)
    betr = b.S("l0a_bet", [16, T], BF16)
    delr = b.S("l0a_del", [16, T], BF16)
    nA = b.S("l0a_nA", [16, 1], F32)
    nwbc = b.S("l0a_nwb", [128, 64], BF16)
    S0 = b.S("l0a_S0", [64, 1024], F32)
    S0b = b.S("l0a_S0b", [64, 1024], BF16)
    lnscr = b["lnscr"]
    lnF = lnscr[:, :, :].bitcast(F32).rearrange("p k t -> p (k t)")
    lnB = lnscr[:, :, :].rearrange("p k t -> p (k t)")
    accv = lnF[:, 0:T]
    zw = lnB[:, 4096:8192].rearrange("p (d t) -> p d t", d=2)
    xcw = lnB[:, 8192:12288].rearrange("p (d t) -> p d t", d=2)
    sqw = lnB[:, 12288:14336]
    fw.alias(["l0a_acc", "l0a_zw:0", "l0a_zw:1", "l0a_xcw:0", "l0a_xcw:1",
              "l0a_sqw"], [f"lnscr:{k}" for k in range(KT)])
    xF = sbx[:, :, :].rearrange("p k t -> p (k t)")
    cgBC = xF[:, 0:2048].rearrange("p (h i) -> p h i", h=16)
    uBC = xF[:, 2048:4096].rearrange("p (h i) -> p h i", h=16)
    xB = sbx[:, :, :].bitcast(BF16).rearrange("p k t -> p (k t)")
    ofs = [8192]

    def carve(n):
        r = xB[:, ofs[0]:ofs[0] + n]
        ofs[0] += n
        return r
    Gm = carve(2048)
    Mm = carve(2048)
    Pm = carve(2048)
    nKKl = carve(2048)
    nKKu = carve(2048)
    QKm = carve(2048)
    SCt = carve(2048)
    EG = carve(2048)
    Wtok = carve(1024)
    t2n = carve(1024)
    Oe = carve(1024)
    ztk = carve(1024)
    yF = carve(1024)
    sq2 = carve(1024)
    assert ofs[0] <= 32768, ofs
    xlnB = b["xln"][:, :, :].rearrange("p k t -> p (k t)")
    qc = xlnB[0:64, 0:2048].rearrange("p (h t) -> p h t", h=16)
    kc = xlnB[0:64, 2048:4096].rearrange("p (h t) -> p h t", h=16)
    vc = xlnB[0:64, 4096:6144].rearrange("p (h t) -> p h t", h=16)
    euBC = xlnB[0:64, 6144:8192].rearrange("p (h t) -> p h t", h=16)
    gaBC = xlnB[0:64, 8192:10240].rearrange("p (h t) -> p h t", h=16)
    deBC = xlnB[0:64, 10240:12288].rearrange("p (h t) -> p h t", h=16)
    Vtok2 = xlnB[:, 12288:13312]
    Ktok2 = xlnB[:, 13312:14336]
    cgT = b.S("l0a_cgT", [128, 16], F32)
    uT = b.S("l0a_uT", [128, 16], F32)
    beT = b.S("l0a_beT", [128, 16], BF16)
    deT = b.S("l0a_deT", [128, 16], BF16)
    rr = b.S("l0a_rr", [128, 16], F32)
    dl_bf = b.S("l0a_dbf", [16, C_], BF16)
    xrow = b.S("l0a_xr", [128, KT, C_], F32)
    wbuf = b.S("l0a_wb", [128, 2, KT, 128], BF16)
    ident, identb = b["c_ident"], b["c_identb"]
    msl, msu, mli = b["c_l0a_msl"], b["c_l0a_msu"], b["c_l0a_mli"]
    sel2, sel2T = b["c_sel2"], b["c_sel2T"]
    xkeys = [f"xln:{k}" for k in range(KT)]
    fw.alias(["l0a_carve"], [f"x:{k}" for k in range(KT)])

    fw.op("gpsimd",
          lambda e: [e.dma_start(out=nwbc[:],
                                 in_=nw_d.to_broadcast([128, 64]))],
          writes=["l0a_nwbc"], dma=True, tag="l0a_ldnw")

    for o in range(32):
        db = o % 2
        fw.op("gpsimd",
              lambda e, o=o, db=db: [e.dma_start(
                  out=wbuf[:, db], in_=win_d[:, :, o * 128:(o + 1) * 128])],
              writes=[f"l0a_wb:{db}"], dma=True, tag=f"l0a_ldw{o}")

        def pmm(e, db=db):
            last = None
            for k in range(KT):
                for c in range(TC):
                    sl = slice(c * 512, (c + 1) * 512)
                    last = e.matmul(psA[:, sl], wbuf[:, db, k, :],
                                    xln[:, k, sl],
                                    start=(k == 0), stop=(k == KT - 1))
            return last
        fw.op("tensor", pmm, reads=[f"l0a_wb:{db}"] + xkeys,
              writes=_akeys(), tag=f"l0a_pmm{o}")
        if o < 24:
            fw.op("scalar",
                  lambda e, db=db: e.copy(zw[:, db, :], psA[:, :]),
                  reads=_akeys(), writes=[f"l0a_zw:{db}"],
                  tag=f"l0a_qkve{o}")

            def conv(e, db=db, o=o):
                e.tensor_scalar(accv[:, :], zw[:, db, :],
                                b["w_l0a_cw"][:, o, 3:4], None, OP.mult)
                e.drain()
                for tau in range(3):
                    off = 3 - tau
                    e.scalar_tensor_tensor(
                        accv[:, off:T], zw[:, db, 0:T - off],
                        b["w_l0a_cw"][:, o, tau:tau + 1], accv[:, off:T],
                        OP.mult, OP.add)
                    e.drain()
                return e.engine_nop()
            fw.op("vector", conv, reads=[f"l0a_zw:{db}", "w_l0a_cw"],
                  writes=["l0a_acc"], tag=f"l0a_cv{o}")
            fw.op("scalar",
                  lambda e, db=db: e.activation(xcw[:, db, :], accv[:, :],
                                                AF.Silu),
                  reads=["l0a_acc"], writes=[f"l0a_xcw:{db}"],
                  tag=f"l0a_si{o}")
            if o < 16:
                fw.op("scalar",
                      lambda e, db=db: e.activation(sqw[:, :],
                                                    xcw[:, db, :],
                                                    AF.Square),
                      reads=[f"l0a_xcw:{db}"], writes=["l0a_sqw"],
                      tag=f"l0a_sq{o}")

                def ssum(e):
                    last = None
                    for cq in range(TC):
                        sl = slice(cq * 512, (cq + 1) * 512)
                        last = e.matmul(psB[0:2, sl], sel2T[:, :],
                                        sqw[:, sl], start=True, stop=True)
                    return last
                fw.op("tensor", ssum, reads=["l0a_sqw", "c_sel2T"],
                      writes=_bkeys(), tag=f"l0a_ss{o}")
                sp2 = b["rowsA"][0:2, :]
                fw.op("vector",
                      lambda e, sp2=sp2: e.tensor_scalar(
                          sp2, psB[0:2, :], 1e-6, None, OP.add),
                      reads=_bkeys(), writes=["rowA0"], tag=f"l0a_se{o}")
                fw.op("scalar",
                      lambda e, sp2=sp2: e.activation(sp2, sp2, AF.Ln),
                      reads=["rowA0"], writes=["rowA0"],
                      tag=f"l0a_sr{o}")
                fw.op("scalar",
                      lambda e, sp2=sp2: e.activation(sp2, sp2, AF.Exp,
                                                      scale=-0.5),
                      reads=["rowA0"], writes=["rowA0"],
                      tag=f"l0a_sc{o}")
                if o < 8:
                    fw.op("vector",
                          lambda e, sp2=sp2: e.tensor_scalar(
                              sp2, sp2, 0.125, None, OP.mult),
                          reads=["rowA0"], writes=["rowA0"],
                          tag=f"l0a_sh{o}")

                def nbc(e, sp2=sp2):
                    last = None
                    for cq in range(TC):
                        sl = slice(cq * 512, (cq + 1) * 512)
                        last = e.matmul(psB[:, sl], sel2[:, :],
                                        sp2[:, sl], start=True,
                                        stop=True)
                    return last
                fw.op("tensor", nbc, reads=["rowA0", "c_sel2"],
                      writes=_bkeys(), tag=f"l0a_nb{o}")
                fw.op("vector",
                      lambda e, db=db: e.tensor_tensor(xcw[:, db, :],
                                                       xcw[:, db, :],
                                                       psB[:, :], OP.mult),
                      reads=[f"l0a_xcw:{db}"] + _bkeys(),
                      writes=[f"l0a_xcw:{db}"], tag=f"l0a_nm{o}")
            fw.op("sync",
                  lambda e, o=o, db=db: [e.dma_start(
                      out=qkvstage[2 * o:2 * o + 2].rearrange(
                          "a d t -> (a d) t"),
                      in_=xcw[:, db, :])],
                  reads=[f"l0a_xcw:{db}"], writes=[f"qst:{o}"], dma=True,
                  tag=f"l0a_qst{o}")
        else:
            fw.op("scalar",
                  lambda e, db=db: e.copy(xcw[:, db, :], psA[:, :]),
                  reads=_akeys(), writes=[f"l0a_xcw:{db}"],
                  tag=f"l0a_ze{o}")
            fw.op("sync",
                  lambda e, o=o, db=db: [e.dma_start(
                      out=zstage[2 * (o - 24):2 * (o - 24) + 2].rearrange(
                          "a d t -> (a d) t"),
                      in_=xcw[:, db, :])],
                  reads=[f"l0a_xcw:{db}"], writes=[f"zst:{o - 24}"],
                  dma=True, tag=f"l0a_zst{o}")

    brow = lnF[0:16, 0:2048]
    arow = lnF[0:16, 2048:4096]
    spr = lnF[0:16, 4096:6144]
    eur = lnB[0:16, 12288:14336]
    gamr = lnB[0:16, 14336:16384]
    fw.alias(["l0a_br", "l0a_ar", "l0a_spr", "l0a_eur", "l0a_gam"],
             ["l0a_acc", "l0a_zw:0", "l0a_zw:1", "l0a_xcw:0",
              "l0a_xcw:1", "l0a_sqw"])
    wbb = b.S("l0a_wbb", [128, KT, 48], BF16)
    fw.op("gpsimd", lambda e: [e.dma_start(out=wbb[:], in_=wba_d)],
          writes=["l0a_wbb"], dma=True, tag="l0a_ldba")

    def bamm(e):
        last = None
        for k in range(KT):
            for cq in range(TC):
                sl = slice(cq * 512, (cq + 1) * 512)
                last = e.matmul(psA[0:48, sl], wbb[:, k, :], xln[:, k, sl],
                                start=(k == 0), stop=(k == KT - 1))
        return last
    fw.op("tensor", bamm, reads=["l0a_wbb"] + xkeys, writes=_akeys(),
          tag="l0a_bamm")
    fw.op("vector", lambda e: e.tensor_copy(brow[:, :], psA[0:16, :]),
          reads=_akeys(), writes=["l0a_br"], tag="l0a_bre")
    fw.op("vector", lambda e: e.tensor_copy(arow[:, :], psA[32:48, :]),
          reads=_akeys(), writes=["l0a_ar"], tag="l0a_are")
    fw.op("scalar", lambda e: e.activation(betr[:, :], brow[:, :],
                                           AF.Sigmoid),
          reads=["l0a_br"], writes=["l0a_betr"], tag="l0a_beta")

    def softplus(srcap, srckey, dstap, dstkey, tagp):
        fw.op("scalar", lambda e: e.activation(spr[:, :], srcap, AF.Abs),
              reads=[srckey], writes=["l0a_spr"], tag=tagp + "1")
        fw.op("scalar", lambda e: e.activation(spr[:, :], spr[:, :],
                                               AF.Exp, scale=-1.0),
              reads=["l0a_spr"], writes=["l0a_spr"], tag=tagp + "2")
        fw.op("scalar",
              lambda e: e.activation(spr[:, :], spr[:, :], AF.Ln,
                                     bias=b["c_onescol"][0:16, :]),
              reads=["l0a_spr", "c_onescol"], writes=["l0a_spr"],
              tag=tagp + "3")
        fw.op("vector",
              lambda e: e.tensor_scalar(dstap, srcap, 0.0, None, OP.max),
              reads=[srckey], writes=[dstkey], tag=tagp + "4")
        fw.op("vector",
              lambda e: e.tensor_tensor(dstap, dstap, spr[:, :], OP.add),
              reads=[dstkey, "l0a_spr"], writes=[dstkey], tag=tagp + "5")

    fw.op("vector",
          lambda e: e.tensor_scalar(arow[:, :], arow[:, :],
                                    b["w_l0a_dtb"][:, :], None, OP.add),
          reads=["l0a_ar", "w_l0a_dtb"], writes=["l0a_ar"], tag="l0a_adtb")
    fw.op("vector",
          lambda e: e.tensor_scalar(nA[:, :], b["w_l0a_alog"][:, :], -2.3,
                                    None, OP.max),
          reads=["w_l0a_alog"], writes=["l0a_nA"], tag="l0a_amax")
    fw.op("scalar", lambda e: e.activation(nA[:, :], nA[:, :], AF.Exp),
          reads=["l0a_nA"], writes=["l0a_nA"], tag="l0a_aexp")
    fw.op("vector",
          lambda e: e.tensor_scalar(nA[:, :], nA[:, :], -1.0, None,
                                    OP.mult),
          reads=["l0a_nA"], writes=["l0a_nA"], tag="l0a_aneg")
    softplus(arow[:, :], "l0a_ar", cumg[:, :], "l0a_cgr", "l0a_spa")
    fw.op("vector",
          lambda e: e.tensor_scalar(cumg[:, :], cumg[:, :], nA[:, :],
                                    None, OP.mult),
          reads=["l0a_cgr", "l0a_nA"], writes=["l0a_cgr"], tag="l0a_g")

    def scans(e):
        last = None
        for cq in range(NCH):
            sl = slice(cq * C_, (cq + 1) * C_)
            last = e.tensor_tensor_scan(cumg[:, sl], cumg[:, sl],
                                        cumg[:, sl], 0.0, OP.add,
                                        OP.bypass)
        return last
    fw.op("vector", scans, reads=["l0a_cgr"], writes=["l0a_cgr"],
          tag="l0a_scan")
    softplus(brow[:, :], "l0a_br", urow[:, :], "l0a_ur", "l0a_spb")
    fw.op("vector",
          lambda e: e.tensor_tensor(urow[:, :], brow[:, :], urow[:, :],
                                    OP.subtract),
          reads=["l0a_br", "l0a_ur"], writes=["l0a_ur"], tag="l0a_lnb")
    fw.op("vector",
          lambda e: e.tensor_tensor(urow[:, :], urow[:, :], cumg[:, :],
                                    OP.add),
          reads=["l0a_ur", "l0a_cgr"], writes=["l0a_ur"], tag="l0a_u")
    fw.op("scalar", lambda e: e.activation(eur[:, :], urow[:, :], AF.Exp),
          reads=["l0a_ur"], writes=["l0a_eur"], tag="l0a_eu")
    fw.op("scalar", lambda e: e.activation(gamr[:, :], cumg[:, :], AF.Exp),
          reads=["l0a_cgr"], writes=["l0a_gam"], tag="l0a_ga")

    def dels(e):
        last = None
        for cq in range(NCH):
            sl = slice(cq * C_, (cq + 1) * C_)
            last = e.activation(delr[:, sl], cumg[:, sl], AF.Exp,
                                scale=-1.0,
                                bias=cumg[:, cq * C_ + C_ - 1:cq * C_ + C_])
        return last
    fw.op("scalar", dels, reads=["l0a_cgr"], writes=["l0a_del"],
          tag="l0a_de")
    fw.op("sync", lambda e: [e.dma_start(out=cgstage, in_=cumg[:, :])],
          reads=["l0a_cgr"], writes=["cgst"], dma=True, tag="l0a_cgst")
    fw.op("sync", lambda e: [e.dma_start(out=ustage, in_=urow[:, :])],
          reads=["l0a_ur"], writes=["ust"], dma=True, tag="l0a_ust")
    fw.op("sync", lambda e: [e.dma_start(out=rbstage[0], in_=eur[:, :])],
          reads=["l0a_eur"], writes=["rbst"], dma=True, tag="l0a_rb0")
    fw.op("sync", lambda e: [e.dma_start(out=rbstage[1], in_=gamr[:, :])],
          reads=["l0a_gam"], writes=["rbst"], dma=True, tag="l0a_rb1")
    fw.op("sync", lambda e: [e.dma_start(out=rbstage[2], in_=delr[:, :])],
          reads=["l0a_del"], writes=["rbst"], dma=True, tag="l0a_rb2")
    fw.op("vector", lambda e: e.memset(S0[:, :], 0.0), writes=["l0a_S0"],
          tag="l0a_S00")
    fw.op("vector", lambda e: e.memset(S0b[:, :], 0.0), writes=["l0a_S0b"],
          tag="l0a_S0b0")

    fw.alias(["l0a_qc", "l0a_kc", "l0a_vc", "l0a_euBC", "l0a_gaBC",
              "l0a_deBC", "l0a_Vtok", "l0a_Ktok", "l0a_zcl"],
             [f"xln:{k}" for k in range(KT)])
    wo = b["w_l0a_wo"]
    xi = p.x_src_ap
    xo = p.x_dst_ap

    def hbc(stage, c, nt=C_):
        return stage[:, c * C_:c * C_ + nt].unsqueeze(0) \
            .to_broadcast([64, 16, nt])
    for c in range(NCH):
        csl = slice(c * C_, (c + 1) * C_)
        fw.op("gpsimd",
              lambda e, c=c, csl=csl: [
                  e.dma_start(out=qc, in_=qkvstage[0:16, :, csl]
                              .rearrange("h d t -> d h t")),
                  e.dma_start(out=kc, in_=qkvstage[16:32, :, csl]
                              .rearrange("h d t -> d h t")),
                  e.dma_start(out=vc, in_=qkvstage[32:48, :, csl]
                              .rearrange("h d t -> d h t"))],
              reads=[f"qst:{o}" for o in range(24)],
              writes=["l0a_qc", "l0a_kc", "l0a_vc"], dma=True, n_dma=3,
              tag=f"l0a_ldqkv{c}")
        fw.op("gpsimd",
              lambda e, c=c, csl=csl: [
                  e.dma_start(out=cgBC, in_=cgstage[:, csl].unsqueeze(0)
                              .to_broadcast([128, 16, C_])),
                  e.dma_start(out=uBC, in_=ustage[:, csl].unsqueeze(0)
                              .to_broadcast([128, 16, C_]))],
              reads=["cgst", "ust", "l0a_carve"],
              writes=["l0a_cgBC", "l0a_uBC"], dma=True, n_dma=2,
              tag=f"l0a_ldbc{c}")
        fw.op("gpsimd",
              lambda e, c=c: [
                  e.dma_start(out=euBC, in_=hbc(rbstage[0], c)),
                  e.dma_start(out=gaBC, in_=hbc(rbstage[1], c)),
                  e.dma_start(out=deBC, in_=hbc(rbstage[2], c))],
              reads=["rbst"], writes=["l0a_euBC", "l0a_gaBC", "l0a_deBC"],
              dma=True, n_dma=3, tag=f"l0a_ldrb{c}")

        fw.op("tensor",
              lambda e, csl=csl: e.transpose(psB[:, 0:16], cumg[:, csl],
                                             ident[0:16, 0:16]),
              reads=["l0a_cgr", "c_ident"], writes=["B0"],
              tag=f"l0a_cgt{c}")
        fw.op("vector", lambda e: e.tensor_copy(cgT[:, :], psB[:, 0:16]),
              reads=["B0"], writes=["l0a_cgT"], tag=f"l0a_cgtc{c}")
        fw.op("tensor",
              lambda e, csl=csl: e.transpose(psB[:, 16:32], urow[:, csl],
                                             ident[0:16, 0:16]),
              reads=["l0a_ur", "c_ident"], writes=["B0"],
              tag=f"l0a_ut{c}")
        fw.op("vector", lambda e: e.tensor_copy(uT[:, :], psB[:, 16:32]),
              reads=["B0"], writes=["l0a_uT"], tag=f"l0a_utc{c}")
        fw.op("vector",
              lambda e, csl=csl: e.tensor_copy(dl_bf[:, :], betr[:, csl]),
              reads=["l0a_betr"], writes=["l0a_dbf"], tag=f"l0a_bb{c}")
        fw.op("tensor",
              lambda e: e.transpose(psB.bitcast(BF16)[:, 64:80],
                                    dl_bf[:, :], identb[0:16, 0:16]),
              reads=["l0a_dbf", "c_identb"], writes=["B0"],
              tag=f"l0a_bt{c}")
        fw.op("vector",
              lambda e: e.tensor_copy(beT[:, :],
                                      psB.bitcast(BF16)[:, 64:80]),
              reads=["B0"], writes=["l0a_beT"], tag=f"l0a_btc{c}")
        fw.op("vector",
              lambda e, csl=csl: e.tensor_copy(dl_bf[:, :], delr[:, csl]),
              reads=["l0a_del"], writes=["l0a_dbf"], tag=f"l0a_db{c}")
        fw.op("tensor",
              lambda e: e.transpose(psB.bitcast(BF16)[:, 96:112],
                                    dl_bf[:, :], identb[0:16, 0:16]),
              reads=["l0a_dbf", "c_identb"], writes=["B0"],
              tag=f"l0a_dt{c}")
        fw.op("vector",
              lambda e: e.tensor_copy(deT[:, :],
                                      psB.bitcast(BF16)[:, 96:112]),
              reads=["B0"], writes=["l0a_deT"], tag=f"l0a_dtc{c}")
        def kkmm(e):
            last = None
            for h in range(16):
                last = e.matmul(psA[:, h * 128:(h + 1) * 128],
                                kc[:, h, :], kc[:, h, :],
                                start=True, stop=True)
            return last
        fw.op("tensor", kkmm, reads=["l0a_kc"], writes=_akeys(),
              tag=f"l0a_kk{c}")
        fw.op("scalar",
              lambda e: e.activation(nKKl[:, :], psA[:, :], AF.Identity,
                                     scale=-1.0),
              reads=_akeys(), writes=["l0a_nKKl"], tag=f"l0a_nkk{c}")

        def qkmm(e):
            last = None
            for h in range(16):
                last = e.matmul(psA[:, h * 128:(h + 1) * 128],
                                kc[:, h, :], qc[:, h, :],
                                start=True, stop=True)
            return last
        fw.op("tensor", qkmm, reads=["l0a_kc", "l0a_qc"], writes=_akeys(),
              tag=f"l0a_qk{c}")
        fw.op("vector",
              lambda e: e.tensor_tensor(
                  QKm[:, :].rearrange("p (h i) -> p h i", h=16),
                  psA[:, :].rearrange("p (h i) -> p h i", h=16),
                  mli[:, :].unsqueeze(1).to_broadcast([128, 16, 128]),
                  OP.mult),
              reads=_akeys() + ["c_l0a_mli"], writes=["l0a_QKm"],
              tag=f"l0a_qkm{c}")
        fw.op("vector",
              lambda e: e.tensor_tensor(
                  nKKu[:, :].rearrange("p (h i) -> p h i", h=16),
                  nKKl[:, :].rearrange("p (h i) -> p h i", h=16),
                  msu[:, :].unsqueeze(1).to_broadcast([128, 16, 128]),
                  OP.mult),
              reads=["l0a_nKKl", "c_l0a_msu"], writes=["l0a_nKKu"],
              tag=f"l0a_nku{c}")
        fw.op("vector",
              lambda e: e.tensor_tensor(
                  nKKl[:, :].rearrange("p (h i) -> p h i", h=16),
                  nKKl[:, :].rearrange("p (h i) -> p h i", h=16),
                  msl[:, :].unsqueeze(1).to_broadcast([128, 16, 128]),
                  OP.mult),
              reads=["l0a_nKKl", "c_l0a_msl"], writes=["l0a_nKKl"],
              tag=f"l0a_nkl{c}")
        fw.op("vector",
              lambda e: e.scalar_tensor_tensor(
                  cgBC[:, :, :], cgBC[:, :, :], -1.0,
                  uT[:, :].unsqueeze(2).to_broadcast([128, 16, C_]),
                  OP.mult, OP.add),
              reads=["l0a_cgBC", "l0a_uT"], writes=["l0a_cgBC"],
              tag=f"l0a_g0a{c}")
        fw.op("vector",
              lambda e: e.tensor_scalar(cgBC[:, :, :], cgBC[:, :, :], 0.0,
                                        None, OP.min),
              reads=["l0a_cgBC"], writes=["l0a_cgBC"], tag=f"l0a_g0b{c}")
        fw.op("scalar",
              lambda e: e.activation(EG[:, :], xF[:, 0:2048], AF.Exp),
              reads=["l0a_cgBC"], writes=["l0a_EG"], tag=f"l0a_g0e{c}")
        fw.op("vector",
              lambda e: e.tensor_tensor(Gm[:, :], EG[:, :], nKKl[:, :],
                                        OP.mult),
              reads=["l0a_EG", "l0a_nKKl"], writes=["l0a_G"],
              tag=f"l0a_g0{c}")
        fw.op("vector",
              lambda e: e.scalar_tensor_tensor(
                  uBC[:, :, :],
                  cgT[:, :].unsqueeze(2).to_broadcast([128, 16, C_]),
                  -1.0, uBC[:, :, :], OP.mult, OP.add),
              reads=["l0a_uBC", "l0a_cgT"], writes=["l0a_uBC"],
              tag=f"l0a_m0a{c}")
        fw.op("vector",
              lambda e: e.tensor_scalar(uBC[:, :, :], uBC[:, :, :], 0.0,
                                        None, OP.min),
              reads=["l0a_uBC"], writes=["l0a_uBC"], tag=f"l0a_m0b{c}")
        fw.op("scalar",
              lambda e: e.activation(EG[:, :], xF[:, 2048:4096], AF.Exp),
              reads=["l0a_uBC"], writes=["l0a_EG"], tag=f"l0a_m0e{c}")
        fw.op("vector",
              lambda e: e.tensor_tensor(Mm[:, :], EG[:, :], nKKu[:, :],
                                        OP.mult),
              reads=["l0a_EG", "l0a_nKKu"], writes=["l0a_M"],
              tag=f"l0a_m0{c}")
        fw.op("gpsimd",
              lambda e, c=c, csl=csl: [e.dma_start(
                  out=cgBC, in_=cgstage[:, csl].unsqueeze(0)
                  .to_broadcast([128, 16, C_]))],
              reads=["cgst"], writes=["l0a_cgBC"], dma=True,
              tag=f"l0a_ldbc2{c}")
        fw.op("vector",
              lambda e: e.scalar_tensor_tensor(
                  cgBC[:, :, :],
                  cgT[:, :].unsqueeze(2).to_broadcast([128, 16, C_]),
                  -1.0, cgBC[:, :, :], OP.mult, OP.add),
              reads=["l0a_cgBC", "l0a_cgT"], writes=["l0a_cgBC"],
              tag=f"l0a_sca{c}")
        fw.op("vector",
              lambda e: e.tensor_scalar(cgBC[:, :, :], cgBC[:, :, :], 0.0,
                                        None, OP.min),
              reads=["l0a_cgBC"], writes=["l0a_cgBC"], tag=f"l0a_scb{c}")
        fw.op("scalar",
              lambda e: e.activation(EG[:, :], xF[:, 0:2048], AF.Exp),
              reads=["l0a_cgBC"], writes=["l0a_EG"], tag=f"l0a_sce{c}")
        fw.op("vector",
              lambda e: e.tensor_tensor(SCt[:, :], EG[:, :], QKm[:, :],
                                        OP.mult),
              reads=["l0a_EG", "l0a_QKm"], writes=["l0a_SCt"],
              tag=f"l0a_sc{c}")
        fw.op("vector",
              lambda e: e.scalar_tensor_tensor(
                  Pm[:, :].rearrange("p (h i) -> p h i", h=16),
                  identb[:, :].unsqueeze(1).to_broadcast([128, 16, 128]),
                  1.0,
                  Mm[:, :].rearrange("p (h i) -> p h i", h=16),
                  OP.mult, OP.add),
              reads=["l0a_M", "c_identb"], writes=["l0a_P"],
              tag=f"l0a_p0{c}")
        for it in range(6):
            def sq(e):
                last = None
                for h in range(16):
                    hs = slice(h * 128, (h + 1) * 128)
                    e.matmul(psA[:, hs], Mm[:, hs], Gm[:, hs],
                             start=True, stop=True)
                    last = e.matmul(psB[:, hs], Gm[:, hs], Mm[:, hs],
                                    start=True, stop=True)
                return last
            fw.op("tensor", sq, reads=["l0a_G", "l0a_M"],
                  writes=_akeys() + _bkeys(), tag=f"l0a_sq{c}_{it}")
            fw.op("vector", lambda e: e.tensor_copy(Gm[:, :], psA[:, :]),
                  reads=_akeys(), writes=["l0a_G"], tag=f"l0a_ge{c}_{it}")
            fw.op("vector", lambda e: e.tensor_copy(Mm[:, :], psB[:, :]),
                  reads=_bkeys(), writes=["l0a_M"], tag=f"l0a_me{c}_{it}")

            def pup(e):
                last = None
                for h in range(16):
                    hs = slice(h * 128, (h + 1) * 128)
                    e.matmul(psA[:, hs], identb[:, :], Pm[:, hs],
                             start=True, stop=False)
                    last = e.matmul(psA[:, hs], Gm[:, hs], Pm[:, hs],
                                    start=False, stop=True)
                return last
            fw.op("tensor", pup, reads=["l0a_G", "l0a_P", "c_identb"],
                  writes=_akeys(), tag=f"l0a_pu{c}_{it}")
            fw.op("vector", lambda e: e.tensor_copy(Pm[:, :], psA[:, :]),
                  reads=_akeys(), writes=["l0a_P"], tag=f"l0a_pe{c}_{it}")
        def vtr(e):
            last = None
            pab = psA.bitcast(BF16)
            for h in range(16):
                last = e.transpose(pab[:, h * 64:(h + 1) * 64],
                                   vc[:, h, :], identb[0:64, 0:64])
            return last
        fw.op("tensor", vtr, reads=["l0a_vc", "c_identb"],
              writes=["A0", "A1"], tag=f"l0a_vt{c}")
        fw.op("vector",
              lambda e: e.tensor_copy(Vtok2[:, :],
                                      psA.bitcast(BF16)[:, 0:1024]),
              reads=["A0", "A1"], writes=["l0a_Vtok"], tag=f"l0a_vtc{c}")

        def ktr(e):
            last = None
            pab = psA.bitcast(BF16)
            for h in range(16):
                last = e.transpose(pab[:, 1024 + h * 64:
                                       1024 + (h + 1) * 64],
                                   kc[:, h, :], identb[0:64, 0:64])
            return last
        fw.op("tensor", ktr, reads=["l0a_kc", "c_identb"],
              writes=["A2", "A3"], tag=f"l0a_kt{c}")
        fw.op("vector",
              lambda e: e.tensor_copy(Ktok2[:, :],
                                      psA.bitcast(BF16)[:, 1024:2048]),
              reads=["A2", "A3"], writes=["l0a_Ktok"], tag=f"l0a_ktc{c}")
        fw.op("vector",
              lambda e: e.tensor_tensor(kc[:, :, :], kc[:, :, :],
                                        euBC[:, :, :], OP.mult),
              reads=["l0a_kc", "l0a_euBC"], writes=["l0a_kc"],
              tag=f"l0a_kbg{c}")
        fw.op("vector",
              lambda e: e.tensor_tensor(qc[:, :, :], qc[:, :, :],
                                        gaBC[:, :, :], OP.mult),
              reads=["l0a_qc", "l0a_gaBC"], writes=["l0a_qc"],
              tag=f"l0a_qga{c}")
        fw.op("vector",
              lambda e: e.tensor_tensor(
                  Ktok2[:, :].rearrange("p (h v) -> p h v", h=16),
                  Ktok2[:, :].rearrange("p (h v) -> p h v", h=16),
                  deT[:, :].unsqueeze(2).to_broadcast([128, 16, 64]),
                  OP.mult),
              reads=["l0a_Ktok", "l0a_deT"], writes=["l0a_Ktok"],
              tag=f"l0a_kd{c}")
        fw.op("vector",
              lambda e: e.tensor_tensor(
                  EG[:, :].rearrange("p (h i) -> p h i", h=16),
                  Pm[:, :].rearrange("p (h i) -> p h i", h=16),
                  beT[:, :].unsqueeze(2).to_broadcast([128, 16, 128]),
                  OP.mult),
              reads=["l0a_P", "l0a_beT"], writes=["l0a_EG"],
              tag=f"l0a_pb{c}")
        def a2mm(e):
            last = None
            for h in range(16):
                last = e.matmul(psB[:, h * 64:(h + 1) * 64],
                                kc[:, h, :],
                                S0b[:, h * 64:(h + 1) * 64],
                                start=True, stop=True)
            return last
        fw.op("tensor", a2mm, reads=["l0a_kc", "l0a_S0b"],
              writes=_bkeys(), tag=f"l0a_a2{c}")
        fw.op("scalar",
              lambda e: e.activation(t2n[:, :], psB[:, 0:1024],
                                     AF.Identity, scale=-1.0),
              reads=_bkeys(), writes=["l0a_t2n"], tag=f"l0a_t2n{c}")
        def wmm(e):
            last = None
            for h in range(16):
                hs = slice(h * 128, (h + 1) * 128)
                vs = slice(h * 64, (h + 1) * 64)
                e.matmul(psB[:, vs], EG[:, hs], Vtok2[:, vs],
                         start=True, stop=False)
                last = e.matmul(psB[:, vs], Pm[:, hs], t2n[:, vs],
                                start=False, stop=True)
            return last
        fw.op("tensor", wmm,
              reads=["l0a_EG", "l0a_P", "l0a_Vtok", "l0a_t2n"],
              writes=_bkeys(), tag=f"l0a_w{c}")
        fw.op("vector", lambda e: e.tensor_copy(Wtok[:, :], psB[:, 0:1024]),
              reads=_bkeys(), writes=["l0a_Wtok"], tag=f"l0a_we{c}")
        def omm(e):
            last = None
            for h in range(16):
                hs = slice(h * 128, (h + 1) * 128)
                vs = slice(h * 64, (h + 1) * 64)
                e.matmul(psA[:, vs], qc[:, h, :], S0b[:, vs],
                         start=True, stop=False)
                last = e.matmul(psA[:, vs], SCt[:, hs], Wtok[:, vs],
                                start=False, stop=True)
            return last
        fw.op("tensor", omm,
              reads=["l0a_qc", "l0a_S0b", "l0a_SCt", "l0a_Wtok"],
              writes=_akeys(), tag=f"l0a_o{c}")
        fw.op("vector", lambda e: e.tensor_copy(Oe[:, :], psA[:, 0:1024]),
              reads=_akeys(), writes=["l0a_Oe"], tag=f"l0a_oe{c}")
        def smm(e):
            last = None
            for h in range(16):
                vs = slice(h * 64, (h + 1) * 64)
                last = e.matmul(psB[0:64, vs], Ktok2[:, vs], Wtok[:, vs],
                                start=True, stop=True)
            return last
        fw.op("tensor", smm, reads=["l0a_Ktok", "l0a_Wtok"],
              writes=_bkeys(), tag=f"l0a_s{c}")
        fw.op("vector",
              lambda e: e.tensor_tensor(
                  S0[:, :].rearrange("p (h v) -> p h v", h=16),
                  S0[:, :].rearrange("p (h v) -> p h v", h=16),
                  gaBC[:, :, C_ - 1:C_].to_broadcast([64, 16, 64]),
                  OP.mult),
              reads=["l0a_S0", "l0a_gaBC"], writes=["l0a_S0"],
              tag=f"l0a_su1{c}")
        fw.op("vector",
              lambda e: e.tensor_tensor(S0[:, :], S0[:, :],
                                        psB[0:64, 0:1024], OP.add),
              reads=["l0a_S0"] + _bkeys(), writes=["l0a_S0"],
              tag=f"l0a_su2{c}")
        fw.op("vector", lambda e: e.tensor_copy(S0b[:, :], S0[:, :]),
              reads=["l0a_S0"], writes=["l0a_S0b"], tag=f"l0a_sb{c}")
        fw.op("scalar", lambda e: e.activation(sq2[:, :], Oe[:, :],
                                               AF.Square),
              reads=["l0a_Oe"], writes=["l0a_sq2"], tag=f"l0a_osq{c}")
        fw.op("vector",
              lambda e: e.tensor_reduce(
                  rr[:, :], sq2[:, :].rearrange("p (h v) -> p h v", h=16),
                  AX.X, OP.add),
              reads=["l0a_sq2"], writes=["l0a_rr"], tag=f"l0a_ors{c}")
        fw.op("vector",
              lambda e: e.tensor_scalar(rr[:, :], rr[:, :], 1.0 / 64.0,
                                        1e-6, OP.mult, OP.add),
              reads=["l0a_rr"], writes=["l0a_rr"], tag=f"l0a_orm{c}")
        fw.op("scalar", lambda e: e.activation(rr[:, :], rr[:, :],
                                               AF.Ln),
              reads=["l0a_rr"], writes=["l0a_rr"], tag=f"l0a_orq{c}")
        fw.op("scalar", lambda e: e.activation(rr[:, :], rr[:, :],
                                               AF.Exp, scale=-0.5),
              reads=["l0a_rr"], writes=["l0a_rr"], tag=f"l0a_orr{c}")
        fw.op("vector",
              lambda e: e.tensor_tensor(
                  Oe[:, :].rearrange("p (h v) -> p h v", h=16),
                  Oe[:, :].rearrange("p (h v) -> p h v", h=16),
                  rr[:, :].unsqueeze(2).to_broadcast([128, 16, 64]),
                  OP.mult),
              reads=["l0a_Oe", "l0a_rr"], writes=["l0a_Oe"],
              tag=f"l0a_on{c}")
        fw.op("vector",
              lambda e: e.tensor_tensor(
                  Oe[:, :].rearrange("p (h v) -> p h v", h=16),
                  Oe[:, :].rearrange("p (h v) -> p h v", h=16),
                  nwbc[:, :].unsqueeze(1).to_broadcast([128, 16, 64]),
                  OP.mult),
              reads=["l0a_Oe", "l0a_nwbc"], writes=["l0a_Oe"],
              tag=f"l0a_ow{c}")
        zcl = xlnB[0:64, 14336:16384].rearrange("p (h t) -> p h t", h=16)
        fw.op("gpsimd",
              lambda e, csl=csl: [e.dma_start(
                  out=zcl[:], in_=zstage[:, :, csl]
                  .rearrange("h d t -> d h t"))],
              reads=[f"zst:{o}" for o in range(8)],
              writes=["l0a_zcl"], dma=True, tag=f"l0a_ldzc{c}")

        def ztr(e):
            last = None
            pab = psA.bitcast(BF16)
            for h in range(16):
                last = e.transpose(pab[:, h * 64:(h + 1) * 64],
                                   zcl[:, h, :], identb[0:64, 0:64])
            return last
        fw.op("tensor", ztr, reads=["l0a_zcl", "c_identb"],
              writes=["A0", "A1"], tag=f"l0a_zt{c}")
        fw.op("scalar",
              lambda e: e.activation(ztk[:, :],
                                     psA.bitcast(BF16)[:, 0:1024],
                                     AF.Silu),
              reads=["A0", "A1"], writes=["l0a_ztk"], tag=f"l0a_zs{c}")
        fw.op("vector",
              lambda e: e.tensor_tensor(Oe[:, :], Oe[:, :], ztk[:, :],
                                        OP.mult),
              reads=["l0a_Oe", "l0a_ztk"], writes=["l0a_Oe"],
              tag=f"l0a_og{c}")
        def ytr(e):
            last = None
            pab = psB.bitcast(BF16)
            for j in range(8):
                last = e.transpose(pab[:, j * 128:(j + 1) * 128],
                                   Oe[:, j * 128:(j + 1) * 128],
                                   identb[:, :])
            return last
        fw.op("tensor", ytr, reads=["l0a_Oe", "c_identb"],
              writes=_bkeys(), tag=f"l0a_yt{c}")
        fw.op("vector",
              lambda e: e.tensor_copy(yF[:, :],
                                      psB.bitcast(BF16)[:, 0:1024]),
              reads=_bkeys(), writes=["l0a_yF"], tag=f"l0a_ytc{c}")
        def prmm(e):
            last = None
            for o in range(KT):
                osl = slice(o * 128, (o + 1) * 128)
                for kv in range(KT):
                    last = e.matmul(psA[:, osl],
                                    wo[:, kv, o * 128:(o + 1) * 128],
                                    yF[:, :].rearrange(
                                        "p (kk t) -> p kk t", kk=8)
                                    [:, kv, :],
                                    start=(kv == 0), stop=(kv == KT - 1))
            return last
        fw.op("tensor", prmm, reads=["w_l0a_wo", "l0a_yF"],
              writes=_akeys(), tag=f"l0a_pr{c}")
        fw.op("sync",
              lambda e, c=c: [e.dma_start(
                  out=xrow[:], in_=xi[:, :, c * C_:(c + 1) * C_])],
              writes=["l0a_xr"], dma=True, tag=f"l0a_ldx{c}")
        fw.op("vector",
              lambda e: e.tensor_tensor(
                  xrow[:].rearrange("p k t -> p (k t)"),
                  xrow[:].rearrange("p k t -> p (k t)"),
                  psA[:, 0:1024], OP.add),
              reads=["l0a_xr"] + _akeys(), writes=["l0a_xr"],
              tag=f"l0a_xa{c}")
        fw.op("sync",
              lambda e, c=c: [e.dma_start(
                  out=xo[:, :, c * C_:(c + 1) * C_], in_=xrow[:])],
              reads=["l0a_xr"], dma=True, tag=f"l0a_stx{c}")


def build_l0attn_block():
    p = Prog()
    p.load_consts()
    p.load_x()
    p.set_xio("x_in", "x_out")
    emit_ln(p, "ln0a")
    emit_l0attn(p)
    return p.finish()


def build_l1ffn_block():
    p = Prog()
    p.load_consts()
    p.load_x()
    emit_ln(p, "ln1f")
    emit_l1ffn(p)
    p.store_x()
    return p.finish()


def build_l2ffn_block():
    p = Prog()
    p.load_consts()
    p.load_x()
    emit_ln(p, "ln2f")
    emit_l2ffn(p)
    p.store_x()
    return p.finish()


def build_l1attn_block():
    p = Prog()
    p.load_consts()
    p.load_x()
    emit_ln(p, "ln1a")
    emit_l1attn(p)
    p.store_x()
    return p.finish()


def build_l0ffn_block():
    p = Prog()
    p.load_consts()
    p.load_x()
    emit_ln(p, "ln0f")
    emit_l0ffn(p)
    p.store_x()
    return p.finish()


HOST_PREP = {"l1ffn": host_prep_l1ffn, "l2ffn": host_prep_l2ffn,
             "l0ffn": host_prep_l0ffn, "l1attn": host_prep_l1attn,
             "l3ffn": host_prep_l3ffn, "l2attn": host_prep_l2attn,
             "l3attn": host_prep_l3attn, "l0attn": host_prep_l0attn}
BUILDERS = {"l1ffn": build_l1ffn_block, "l2ffn": build_l2ffn_block,
            "l0ffn": build_l0ffn_block, "l1attn": build_l1attn_block,
            "l3ffn": build_l3ffn_block, "l2attn": build_l2attn_block,
            "l3attn": build_l3attn_block, "l0attn": build_l0attn_block}


def run_block(block, inp, x_fm, sim=False):
    consts = consts_host()
    weights = HOST_PREP[block](inp)
    nc = BUILDERS[block]()
    in_maps = []
    for core in range(NCORES):
        m = dict(consts)
        m.update(weights)
        m["x_in"] = _f(x_fm[core])
        in_maps.append(m)
    if sim:
        from simrun import run_sim
        s = run_sim(nc, in_maps[0])
        return np.asarray(s.tensor("x_out"))[None]
    res = run_bass_kernel_spmd(nc, in_maps, core_ids=list(range(NCORES)))
    return np.stack([res.results[c]["x_out"] for c in range(NCORES)])


BLOCK_ORDER = ["l0attn", "l0ffn", "l1attn", "l1ffn", "l2attn", "l2ffn",
               "l3attn", "l3ffn"]


def build_mega():
    p = Prog()
    nc = p.nc
    rsem = p.ctx.enter_context(nc.semaphore("rdma_r"))
    lsem = p.ctx.enter_context(nc.semaphore("rdma_l"))
    p.load_consts()
    p.load_x()

    p.begin_block("B0_l0attn")
    p.set_xio("x_in", "xh0")
    emit_ln(p, "ln0a")
    emit_l0attn(p)
    p.end_block()
    p.reload_x_from(
        p.dscratch("xh0", [D, T]).rearrange("(k p) t -> p k t", p=128))

    p.begin_block("B1_l0ffn")
    emit_ln(p, "ln0f")
    emit_l0ffn(p)
    p.end_block()

    p.begin_block("B2_l1attn")
    emit_ln(p, "ln1a")
    emit_l1attn(p)
    p.end_block()

    p.begin_block("B3_l1ffn")
    emit_ln(p, "ln1f")
    emit_l1ffn(p)
    p.end_block()
    p.store_x_to("xh1")

    p.begin_block("B4_l2attn")
    p.set_xio("xh1", "xh2")
    p.load("w_l2a_gam", [128, KT], dt=F32)
    p.load("w_l2a_bet", [128, KT], dt=F32)
    emit_ln(p, "ln2a", gamma="w_l2a_gam", beta="w_l2a_bet")
    emit_l2attn(p)
    p.end_block()
    p.reload_x_from(
        p.dscratch("xh2", [D, T]).rearrange("(k p) t -> p k t", p=128))

    p.begin_block("B5_l2ffn")
    emit_ln(p, "ln2f")
    emit_l2ffn(p)
    p.end_block()
    p.store_x_to("xh3i")

    a2a_in = p.dscratch("a2a_in", [8, D, 256])
    a2a_out = p.dscratch("a2a_out", [8, D, 256])
    p.begin_block("B6_l3attn")
    p.set_xio("xh3i",
              a2a_in.rearrange("s (k p) c -> p k s c", p=128))
    emit_ln(p, "ln3a")
    emit_l3attn(p)
    p.end_block()
    p.fw.raw_op(
        "gpsimd",
        lambda e: e.collective_compute(
            "AllToAll", mybir.AluOpType.bypass,
            ins=[a2a_in], outs=[a2a_out],
            replica_groups=[[0, 1, 2, 3, 4, 5, 6, 7]],
        ).then_inc(rsem, 1),
        tag="a2a")
    lane_view = a2a_out.rearrange("r (k p) c -> p k r c", p=128)
    p.reload_x_from(lane_view, extra_waits=[(rsem, 1)])

    p.begin_block("B7_l3ffn")
    p.set_xio(lane_view, "x_out")
    emit_ln(p, "ln3f")
    emit_l3ffn(p)
    p.end_block()
    return p.finish()


def kernel(**inputs):
    inp = {k: np.asarray(v) for k, v in inputs.items()}
    consts = consts_host()
    weights = {}
    for blk in BLOCK_ORDER:
        weights.update(HOST_PREP[blk](inp))
    x_fm = np.ascontiguousarray(
        np.asarray(inp["L0_attn_x"], dtype=np.float32).transpose(0, 2, 1))
    nc = build_mega()
    in_maps = []
    for core in range(NCORES):
        m = dict(consts)
        m.update(weights)
        m["x_in"] = _f(x_fm[core])
        in_maps.append(m)
    res = run_bass_kernel_spmd(nc, in_maps, core_ids=list(range(NCORES)))
    x_fm = np.stack([np.asarray(res.results[c]["x_out"])
                     for c in range(NCORES)])
    out = np.empty((8, T, D), dtype=np.float32)
    for i in range(NCORES):
        y = x_fm[i].reshape(D, 8, 256)
        for b_ in range(8):
            out[b_, 256 * i:256 * (i + 1), :] = y[:, b_, :].T
    return out


# version 20 (final)
# speedup vs baseline: 1.0862x; 1.0074x over previous
import os
os.environ.setdefault("TRN_ALLOW_ICI_ALL_TO_ALL", "1")

import numpy as np
from contextlib import ExitStack
import concourse.bass as bass
import concourse.mybir as mybir
from concourse.bass_utils import run_bass_kernel_spmd

F32 = mybir.dt.float32
BF16 = mybir.dt.bfloat16
AF = mybir.ActivationFunctionType
OP = mybir.AluOpType
AX = mybir.AxisListType

NCORES = 8
D = 1024
T = 2048
KT = D // 128
TC = T // 512

ENGINES = ("tensor", "vector", "scalar", "gpsimd", "sync")
DMA_QUEUES = ("sync", "scalar", "gpsimd")
DMA_SLOTS = 8


class _Op:
    __slots__ = ("engine", "thunk", "waits", "incs", "tag", "scope")

    def __init__(self, engine, thunk, waits, incs, tag, scope=None):
        self.engine, self.thunk, self.waits, self.incs, self.tag = \
            engine, thunk, waits, incs, tag
        self.scope = scope


class FW:
    def __init__(self, nc, ctx: ExitStack):
        self.nc = nc
        self.ctx = ctx
        self.ops = []
        self.csem = {e: ctx.enter_context(nc.semaphore(f"c_{e}"))
                     for e in ENGINES}
        self.dsem = {}
        self.dcum = {}
        for q in DMA_QUEUES:
            for s in range(DMA_SLOTS):
                self.dsem[(q, s)] = ctx.enter_context(
                    nc.semaphore(f"d_{q}_{s}"))
                self.dcum[(q, s)] = 0
        self.dnext = {q: 0 for q in DMA_QUEUES}
        self.ccount = {e: 0 for e in ENGINES}
        self.last_writer = {}
        self.readers = {}
        self.waited = {e: {} for e in ENGINES}
        self.finalized = False
        self.scope = None

    def _wait_tok(self, eng, tok, waits):
        if isinstance(tok, list):
            for t in tok:
                self._wait_tok(eng, t, waits)
            return
        kind, key, cnt = tok
        if cnt <= self.waited[eng].get((kind, key), 0):
            return
        sem = self.csem[key] if kind == "c" else self.dsem[key]
        waits.append((sem, cnt))
        self.waited[eng][(kind, key)] = cnt

    def op(self, engine, thunk, reads=(), writes=(), dma=False, n_dma=1,
           tag="", extra_waits=()):
        waits = list(extra_waits)
        for k in reads:
            lw = self.last_writer.get(k)
            if lw is not None:
                self._wait_tok(engine, lw, waits)
        for k in writes:
            lw = self.last_writer.get(k)
            if lw is not None:
                self._wait_tok(engine, lw, waits)
            for tok in self.readers.get(k, {}).values():
                self._wait_tok(engine, tok, waits)
        if dma:
            assert engine in DMA_QUEUES
            incs, toks = [], []
            for _ in range(n_dma):
                s = self.dnext[engine]
                self.dnext[engine] = (s + 1) % DMA_SLOTS
                key = (engine, s)
                if self.dcum[key]:
                    self._wait_tok(engine, ("d", key, self.dcum[key]), waits)
                self.dcum[key] += 16
                incs.append((self.dsem[key], 16))
                toks.append(("d", key, self.dcum[key]))
            wtoks = toks
        else:
            self.ccount[engine] += 1
            wtoks = [("c", engine, self.ccount[engine])]
            incs = None
        for k in writes:
            self.last_writer[k] = wtoks[-1] if len(wtoks) == 1 \
                else list(wtoks)
            self.readers[k] = {}
        for k in reads:
            rd = self.readers.setdefault(k, {})
            for t in wtoks:
                rd[(t[0], t[1])] = t
        self.ops.append(_Op(engine, thunk, waits, incs, tag, self.scope))
        return wtoks[-1]

    def alias(self, newkeys, oldkeys, engine="vector"):
        self.op(engine, lambda e: e.engine_nop(),
                writes=list(newkeys) + list(oldkeys), tag="alias")

    def raw_op(self, engine, thunk, tag=""):
        self.ops.append(_Op(engine, thunk, [], "raw", tag, self.scope))

    def barrier(self):
        pre_c = dict(self.ccount)
        pre_d = dict(self.dcum)
        hub = "vector"
        waits = []
        for e2 in ENGINES:
            if e2 != hub and pre_c[e2]:
                self._wait_tok(hub, ("c", e2, pre_c[e2]), waits)
        for key, cum in pre_d.items():
            if cum:
                self._wait_tok(hub, ("d", key, cum), waits)
        self.ccount[hub] += 1
        self.ops.append(_Op(hub, lambda e: e.nop(), waits, None,
                            "bar_hub"))
        hubcnt = self.ccount[hub]
        for e2 in ENGINES:
            if e2 == hub:
                continue
            w2 = []
            self._wait_tok(e2, ("c", hub, hubcnt), w2)
            self.ccount[e2] += 1
            self.ops.append(_Op(e2, lambda e: e.nop(), w2, None,
                                "bar_spk"))
        for e2 in ENGINES:
            w = self.waited[e2]
            for e3 in ENGINES:
                w[("c", e3)] = max(w.get(("c", e3), 0), pre_c[e3])
            w[("c", hub)] = max(w.get(("c", hub), 0), hubcnt)
            for key, cum in pre_d.items():
                w[("d", key)] = max(w.get(("d", key), 0), cum)

    def sink(self, engine="sync"):
        def thunk(e):
            return e.nop()
        waits = []
        for key, cum in self.dcum.items():
            if cum:
                self._wait_tok(engine, ("d", key, cum), waits)
        self.ccount[engine] += 1
        self.ops.append(_Op(engine, thunk, waits, None, "sink"))

    def finalize(self):
        assert not self.finalized
        self.finalized = True
        per_engine = {e: [] for e in ENGINES}
        for op in self.ops:
            per_engine[op.engine].append(op)
        blk = self.ctx.enter_context(self.nc.Block())

        nc = self.nc

        def body(ops, ename):
            def run(e):
                from contextlib import nullcontext
                for op in ops:
                    sc = nc.named_scope(op.scope) if op.scope else \
                        nullcontext()
                    with sc:
                        for sem, val in op.waits:
                            e.wait_ge(sem, val)
                        if op.incs == "raw":
                            op.thunk(e)
                            continue
                        last = op.thunk(e)
                        if op.incs is not None:
                            assert isinstance(last, (list, tuple)) and \
                                len(last) == len(op.incs), \
                                f"dma op {op.tag}"
                            for ins, (sem, v) in zip(last, op.incs):
                                ins.then_inc(sem, v)
                        else:
                            assert last is not None, \
                                f"op {op.tag} returned None"
                            last.then_inc(self.csem[ename], 1)
            return run

        for ename, emeth in (("sync", blk.sync), ("tensor", blk.tensor),
                             ("vector", blk.vector), ("scalar", blk.scalar),
                             ("gpsimd", blk.gpsimd)):
            if per_engine[ename]:
                emeth(body(per_engine[ename], ename))


def _f(a):
    return np.ascontiguousarray(a, dtype=np.float32)


def lhsT_prep(w_out_in):
    din = w_out_in.shape[1]
    dout = w_out_in.shape[0]
    kt = din // 128
    return _f(w_out_in.T.reshape(kt, 128, dout).transpose(1, 0, 2))


def consts_host():
    return {
        "c_onescol": _f(np.ones((128, 1))),
        "c_onesrow": _f(np.ones((1, 128))),
        "c_onespair": _f(np.ones((64, 128))),
        "c_ident": _f(np.eye(128)),
    }


class Buf:
    def __init__(self, nc, ctx):
        self.nc, self.ctx, self.t = nc, ctx, {}
        self.scopes = []
        self.uid = 0
        self.dead = {}

    def push_scope(self):
        self.uid += 1
        self.scopes.append(([], ExitStack(), f"s{self.uid}_"))

    def pop_scope(self):
        names, stack, _ = self.scopes.pop()
        stack.close()
        for n in names:
            self.dead[n] = self.t.pop(n)

    def S(self, name, shape, dt=F32):
        if name not in self.t:
            if self.scopes:
                names, stack, pre = self.scopes[-1]
                self.t[name] = stack.enter_context(
                    self.nc.sbuf_tensor("sb_" + pre + name, shape, dt))
                names.append(name)
            else:
                self.t[name] = self.ctx.enter_context(
                    self.nc.sbuf_tensor("sb_" + name, shape, dt))
        return self.t[name]

    def P(self, name, shape, dt=F32):
        if name not in self.t:
            self.t[name] = self.ctx.enter_context(
                self.nc.psum_tensor("ps_" + name, shape, dt))
        return self.t[name]

    def has(self, name):
        return name in self.t

    def __getitem__(self, name):
        if name in self.t:
            return self.t[name]
        return self.dead[name]


class Prog:

    def __init__(self):
        self.nc = bass.Bass("TRN2", target_bir_lowering=False)
        self.ctx = ExitStack()
        self.fw = FW(self.nc, self.ctx)
        self.b = Buf(self.nc, self.ctx)
        self.dram = {}
        self.psA = self.b.P("A", [128, 2048], F32)
        self.psB = self.b.P("B", [128, 2048], F32)
        self.x_src_ap = None
        self.x_dst_ap = None

    def begin_block(self, name=None):
        self.fw.scope = name
        self.b.push_scope()

    def end_block(self):
        self.fw.barrier()
        self.b.pop_scope()
        self.fw.scope = "glue"

    def store_x_to(self, name):
        ap = self.dscratch(name, [D, T]).rearrange("(k p) t -> p k t", p=128)
        sbx = self.b["x"]
        h = KT // 2
        self.fw.op("sync",
                   lambda e: [e.dma_start(out=ap[:, 0:h], in_=sbx[:, 0:h])],
                   reads=[f"x:{k}" for k in range(h)], dma=True,
                   tag="st_" + name)
        self.fw.op("gpsimd",
                   lambda e: [e.dma_start(out=ap[:, h:KT],
                                          in_=sbx[:, h:KT])],
                   reads=[f"x:{k}" for k in range(h, KT)], dma=True,
                   tag="st2_" + name)

    def reload_x_from(self, ap, extra_waits=()):
        sbx = self.b["x"]
        h = KT // 2
        if len(ap.shape) == 4:
            def th(e, lo, hi):
                return [e.dma_start(
                    out=sbx[:, k, :].rearrange("p (r c) -> p r c",
                                               r=ap.shape[2]),
                    in_=ap[:, k]) for k in range(lo, hi)]
            self.fw.op("sync", lambda e: th(e, 0, h),
                       writes=[f"x:{k}" for k in range(h)], dma=True,
                       n_dma=h, extra_waits=list(extra_waits),
                       tag="reload_x")
            self.fw.op("gpsimd", lambda e: th(e, h, KT),
                       writes=[f"x:{k}" for k in range(h, KT)], dma=True,
                       n_dma=KT - h, extra_waits=list(extra_waits),
                       tag="reload_x2")
            return
        self.fw.op("sync",
                   lambda e: [e.dma_start(out=sbx[:, 0:h],
                                          in_=ap[:, 0:h])],
                   writes=[f"x:{k}" for k in range(h)], dma=True,
                   extra_waits=list(extra_waits), tag="reload_x")
        self.fw.op("gpsimd",
                   lambda e: [e.dma_start(out=sbx[:, h:KT],
                                          in_=ap[:, h:KT])],
                   writes=[f"x:{k}" for k in range(h, KT)], dma=True,
                   extra_waits=list(extra_waits), tag="reload_x2")

    def set_xio(self, src, dst):
        if isinstance(src, str):
            src = (self.din(src, [D, T]) if src == "x_in"
                   else self.dscratch(src, [D, T])) \
                .rearrange("(k p) t -> p k t", p=128)
        if isinstance(dst, str):
            dst = (self.dout(dst, [D, T]) if dst == "x_out"
                   else self.dscratch(dst, [D, T])) \
                .rearrange("(k p) t -> p k t", p=128)
        self.x_src_ap = src
        self.x_dst_ap = dst

    def din(self, name, shape):
        if name not in self.dram:
            self.dram[name] = self.nc.dram_tensor(
                name, list(shape), F32, kind="ExternalInput").ap()
        return self.dram[name]

    def dout(self, name, shape, dt=F32):
        if name not in self.dram:
            self.dram[name] = self.nc.dram_tensor(
                name, list(shape), dt, kind="ExternalOutput").ap()
        return self.dram[name]

    def dscratch(self, name, shape, dt=F32):
        if name not in self.dram:
            self.dram[name] = self.nc.dram_tensor(
                name, list(shape), dt, kind="Internal").ap()
        return self.dram[name]

    def load(self, name, dram_shape, sb_shape=None, dt=BF16, queue="gpsimd",
             rearr=None, sb_name=None):
        ap = self.din(name, dram_shape)
        if rearr is not None:
            ap = ap.rearrange(rearr[0], **rearr[1])
        sb_name = sb_name or name
        dst = self.b.S(sb_name, sb_shape or list(dram_shape), dt)
        q = queue if dt != F32 else "sync"
        self.fw.op(q, lambda e: [e.dma_start(out=dst[:], in_=ap)],
                   writes=[sb_name], dma=True, tag="ld_" + sb_name)
        return dst

    def load_consts(self):
        self.load("c_onescol", [128, 1], dt=F32)
        self.load("c_onesrow", [1, 128], dt=F32)
        self.load("c_onespair", [64, 128], dt=F32)
        self.load("c_ident", [128, 128], dt=F32)
        ap = self.din("c_onescol", [128, 1])
        dst = self.b.S("c_onescolb", [128, 1], BF16)
        self.fw.op("gpsimd", lambda e: [e.dma_start(out=dst[:], in_=ap)],
                   writes=["c_onescolb"], dma=True, tag="ld_onescolb")
        ap2 = self.din("c_ident", [128, 128])
        dst2 = self.b.S("c_identb", [128, 128], BF16)
        self.fw.op("gpsimd", lambda e: [e.dma_start(out=dst2[:], in_=ap2)],
                   writes=["c_identb"], dma=True, tag="ld_identb")
        ap3 = self.din("c_onesrow", [1, 128])
        dst3 = self.b.S("c_onesrowb", [1, 128], BF16)
        self.fw.op("gpsimd", lambda e: [e.dma_start(out=dst3[:], in_=ap3)],
                   writes=["c_onesrowb"], dma=True, tag="ld_onesrowb")

    def load_x(self):
        ap = self.din("x_in", [D, T]).rearrange("(k p) t -> p k t", p=128)
        sbx = self.b.S("x", [128, KT, T], F32)
        self.fw.op("sync", lambda e: [e.dma_start(out=sbx[:], in_=ap)],
                   writes=[f"x:{k}" for k in range(KT)], dma=True,
                   tag="ld_x")
        return sbx

    def store_x(self):
        ap = self.dout("x_out", [D, T]).rearrange("(k p) t -> p k t", p=128)
        sbx = self.b["x"]
        self.fw.op("sync", lambda e: [e.dma_start(out=ap, in_=sbx[:])],
                   reads=[f"x:{k}" for k in range(KT)], dma=True,
                   tag="st_x")

    def finish(self):
        self.fw.sink()
        self.fw.finalize()
        self.ctx.close()
        return self.nc


def _akeys(lo=0, hi=4):
    return [f"A{i}" for i in range(lo, hi)]


def _bkeys(lo=0, hi=4):
    return [f"B{i}" for i in range(lo, hi)]


def emit_ln(p, uid, out="xln", gamma=None, beta=None, eps=1e-5):
    fw, b = p.fw, p.b
    psA, psB = p.psA, p.psB
    sbx = b["x"]
    xln = b.S(out, [128, KT, T], BF16)
    xsq = b.S("lnscr", [128, KT, T], BF16)
    tmid = xsq
    rowsA = b.S("rowsA", [128, T], F32)
    rowsB = b.S("rowsB", [128, T], F32)
    murow = rowsA[0:1, :]
    sumrow = rowsB[0:1, :]
    varrow = rowsB[0:1, :]
    rsrow = rowsB[0:1, :]
    onescol, onescolb, onesrow = b["c_onescol"], b["c_onescolb"], \
        b["c_onesrow"]

    for k in range(KT):
        fw.op("scalar",
              lambda e, k=k: e.activation(xsq[:, k, :], sbx[:, k, :],
                                          AF.Square),
              reads=[f"x:{k}"], writes=[f"lnscr:{k}"], tag=f"{uid}sq{k}")

    def sums(e):
        last = None
        for k in range(KT):
            for c in range(TC):
                sl = slice(c * 512, (c + 1) * 512)
                e.matmul(psA[0:1, sl], onescol[:, :], sbx[:, k, sl],
                         start=(k == 0), stop=(k == KT - 1))
                last = e.matmul(psB[0:1, sl], onescolb[:, :], xsq[:, k, sl],
                                start=(k == 0), stop=(k == KT - 1))
        return last
    fw.op("tensor", sums,
          reads=[f"x:{k}" for k in range(KT)] +
                [f"lnscr:{k}" for k in range(KT)] +
                ["c_onescol", "c_onescolb"],
          writes=_akeys() + _bkeys(), tag=uid + "sums")

    fw.op("vector",
          lambda e: e.tensor_scalar(murow, psA[0:1, :],
                                    1.0 / 1024.0, None, OP.mult),
          reads=_akeys(), writes=["rowA0"], tag=uid + "mu")
    fw.op("vector",
          lambda e: e.tensor_tensor(sumrow, murow, murow, OP.mult),
          reads=["rowA0"], writes=["rowB0"], tag=uid + "mu2")
    fw.op("vector",
          lambda e: e.scalar_tensor_tensor(varrow, psB[0:1, :],
                                           1.0 / 1024.0, varrow,
                                           OP.mult, OP.subtract),
          reads=_bkeys() + ["rowB0"], writes=["rowB0"], tag=uid + "var")
    fw.op("vector",
          lambda e: e.tensor_scalar(varrow, varrow, float(eps), None,
                                    OP.add),
          reads=["rowB0"], writes=["rowB0"], tag=uid + "eps")
    fw.op("scalar",
          lambda e: e.activation(rsrow, varrow, AF.Ln),
          reads=["rowB0"], writes=["rowB0"], tag=uid + "sd")
    fw.op("scalar",
          lambda e: e.activation(rsrow, rsrow, AF.Exp, scale=-0.5),
          reads=["rowB0"], writes=["rowB0"], tag=uid + "rs")

    onespair = b["c_onespair"]

    def bcast(srcap, srckey, lhs):
        def f(e):
            last = None
            for c in range(TC):
                sl = slice(c * 512, (c + 1) * 512)
                last = e.matmul(psB[:, sl], lhs, srcap[:, sl],
                                start=True, stop=True)
            return last
        fw.op("tensor", f, reads=[srckey, "c_onespair", "c_onesrow"],
              writes=_bkeys(), tag=uid + "bc_" + srckey)

    bcast(murow, "rowA0", onesrow[:, :])
    for k in range(KT):
        fw.op("vector",
              lambda e, k=k: e.tensor_tensor(tmid[:, k, :], sbx[:, k, :],
                                             psB[:, :], OP.subtract),
              reads=[f"x:{k}"] + _bkeys(), writes=[f"lnscr:{k}"],
              tag=f"{uid}sub{k}")
    bcast(rsrow, "rowB0", onesrow[:, :])
    for k in range(KT):
        fw.op("vector",
              lambda e, k=k: e.tensor_tensor(xln[:, k, :], tmid[:, k, :],
                                             psB[:, :], OP.mult),
              reads=[f"lnscr:{k}"] + _bkeys(), writes=[f"{out}:{k}"],
              tag=f"{uid}mul{k}")
    if gamma is not None:
        g, bt = b[gamma], b[beta]
        for k in range(KT):
            fw.op("vector",
                  lambda e, k=k: e.tensor_scalar(xln[:, k, :], xln[:, k, :],
                                                 g[:, k:k + 1],
                                                 bt[:, k:k + 1],
                                                 OP.mult, OP.add),
                  reads=[f"{out}:{k}", gamma, beta],
                  writes=[f"{out}:{k}"], tag=f"{uid}aff{k}")


def host_prep_l1ffn(inp):
    out = {}
    out["w_l1f_router"] = lhsT_prep(inp["L1_ffn_router_w"])
    out["w_l1f_rbias"] = _f(inp["L1_ffn_router_bias"].reshape(32, 1))
    gw, uw, dw = (inp["L1_ffn_gate_w"], inp["L1_ffn_up_w"],
                  inp["L1_ffn_down_w"])
    sg, su, sd = (inp["L1_ffn_sgate_w"], inp["L1_ffn_sup_w"],
                  inp["L1_ffn_sdown_w"])
    gus = []
    for e in range(32):
        g = gw[e].T.reshape(KT, 128, 128)
        u = uw[e].T.reshape(KT, 128, 128)
        gus.append(np.stack([g, u], axis=0))
    for o4 in range(4):
        g = sg[o4 * 128:(o4 + 1) * 128].T.reshape(KT, 128, 128)
        u = su[o4 * 128:(o4 + 1) * 128].T.reshape(KT, 128, 128)
        gus.append(np.stack([g, u], axis=0))
    gus = np.stack(gus, axis=0)
    out["w_l1f_gu"] = _f(gus.transpose(0, 3, 1, 2, 4))
    downs = [dw[e].T for e in range(32)]
    sdT = sd.T
    for o4 in range(4):
        downs.append(sdT[o4 * 128:(o4 + 1) * 128])
    big = np.stack(downs, axis=0)
    out["w_l1f_d"] = _f(big.reshape(36, 128, KT, 128)
                        .transpose(2, 1, 0, 3))
    return out


def emit_l1ffn(p):
    fw, b = p.fw, p.b
    psA, psB = p.psA, p.psB
    xln = b["xln"]
    xkeys = [f"xln:{k}" for k in range(KT)]
    p.load("w_l1f_router", [128, KT, 32], dt=F32)
    p.load("w_l1f_rbias", [32, 1], dt=F32)
    gu_dram = p.din("w_l1f_gu", [36, 128, 2, KT, 128])
    d_dram = p.din("w_l1f_d", [KT, 128, 36, 128])
    hwstage = p.dscratch("l1f_hwstage", [36, 128, T], BF16)
    cwstage = p.dscratch("l1f_cwstage", [32, T], BF16)
    wrt, rbias = b["w_l1f_router"], b["w_l1f_rbias"]
    ident = b["c_ident"]
    sbx = b["x"]
    xfkeys = [f"x:{k}" for k in range(KT)]
    murow = b["rowsA"][0:1, :]

    nrw = b.S("l1f_nrw", [1, 32], F32)

    def rsum(e):
        last = None
        for k in range(KT):
            last = e.matmul(psB[0:1, 0:32], b["c_onescol"][:, :],
                            wrt[:, k, :], start=(k == 0),
                            stop=(k == KT - 1))
        return last
    fw.op("tensor", rsum, reads=["w_l1f_router", "c_onescol"],
          writes=["B0"], tag="l1f_rws")
    fw.op("vector",
          lambda e: e.tensor_scalar(nrw[:, :], psB[0:1, 0:32], -1.0, None,
                                    OP.mult),
          reads=["B0"], writes=["l1f_nrw"], tag="l1f_nrw")
    onespair = b["c_onespair"]

    _rowsB = b["rowsB"]
    _onesrow = b["c_onesrow"]

    def rebc(e):
        last = None
        for c in range(TC):
            sl = slice(c * 512, (c + 1) * 512)
            last = e.matmul(psB[:, sl], _onesrow[:, :],
                            _rowsB[0:1, sl], start=True, stop=True)
        return last
    fw.op("tensor", rebc, reads=["rowB0", "c_onesrow"], writes=_bkeys(),
          tag="l1f_rebc")

    def rmm(e):
        last = None
        for k in range(KT):
            for c in range(TC):
                sl = slice(c * 512, (c + 1) * 512)
                e.matmul(psA[0:32, sl], wrt[:, k, :], sbx[:, k, sl],
                         start=(k == 0), stop=False)
        for c in range(TC):
            sl = slice(c * 512, (c + 1) * 512)
            last = e.matmul(psA[0:32, sl], nrw[:, :], murow[:, sl],
                            start=False, stop=True)
        return last
    fw.op("tensor", rmm,
          reads=["w_l1f_router", "l1f_nrw", "rowA0"] + xfkeys,
          writes=_akeys(), tag="l1f_rmm")
    lgT = b.S("l1f_lgT", [32, T], F32)
    fw.op("scalar", lambda e: e.copy(lgT[:, :], psA[0:32, :]),
          reads=_akeys(), writes=["l1f_lgT"], tag="l1f_lg0")
    fw.op("vector",
          lambda e: e.tensor_tensor(lgT[:, :], lgT[:, :], psB[0:32, :],
                                    OP.mult),
          reads=["l1f_lgT"] + _bkeys(), writes=["l1f_lgT"], tag="l1f_lgm")
    fw.op("vector",
          lambda e: e.tensor_scalar(lgT[:, :], lgT[:, :], rbias[:, :],
                                    None, OP.add),
          reads=["l1f_lgT", "w_l1f_rbias"], writes=["l1f_lgT"],
          tag="l1f_lgb")

    lg = b.S("l1f_lg", [128, 16, 32], F32)
    for half in range(2):
        def trs(e, half=half):
            last = None
            for c in range(8):
                cc = half * 8 + c
                last = e.transpose(psB[:, c * 32:(c + 1) * 32],
                                   lgT[:, cc * 128:(cc + 1) * 128],
                                   ident[0:32, 0:32])
            return last
        fw.op("tensor", trs, reads=["l1f_lgT", "c_ident"], writes=["B0"],
              tag=f"l1f_tr{half}")
        fw.op("vector",
              lambda e, half=half: e.tensor_copy(
                  lg[:, half * 8:(half + 1) * 8, :],
                  psB[:, 0:256].rearrange("p (c e) -> p c e", c=8)),
              reads=["B0"], writes=["l1f_lg"], tag=f"l1f_lgcp{half}")

    ex = b.S("l1f_ex", [128, 16, 32], F32)
    sm = b.S("l1f_sm", [128, 16], F32)
    t8 = b.S("l1f_t8", [128, 16, 8], F32)
    cw = b.S("l1f_cw", [128, 16, 32], F32)
    t4s = b.S("l1f_t4s", [128, 16], F32)
    msk = lg
    fw.op("scalar", lambda e: e.activation(ex[:, :, :], lg[:, :, :], AF.Exp),
          reads=["l1f_lg"], writes=["l1f_ex"], tag="l1f_exp")
    fw.op("vector", lambda e: e.tensor_reduce(sm[:, :], ex[:, :, :], AX.X,
                                              OP.add),
          reads=["l1f_ex"], writes=["l1f_sm"], tag="l1f_sum")
    fw.op("vector", lambda e: e.reciprocal(sm[:, :], sm[:, :]),
          reads=["l1f_sm"], writes=["l1f_sm"], tag="l1f_rc")
    fw.op("vector",
          lambda e: e.tensor_tensor(ex[:, :, :], ex[:, :, :],
                                    sm[:, :].unsqueeze(2)
                                    .to_broadcast([128, 16, 32]), OP.mult),
          reads=["l1f_ex", "l1f_sm"], writes=["l1f_ex"], tag="l1f_sc")

    def max8s(e):
        last = None
        for c in range(16):
            last = e.max(out=t8[:, c, :], in_=ex[:, c, :])
            e.drain()
        return last
    fw.op("vector", max8s, reads=["l1f_ex"], writes=["l1f_t8"],
          tag="l1f_max8")
    fw.op("vector",
          lambda e: e.tensor_reduce(t4s[:, :], t8[:, :, 0:4], AX.X, OP.add),
          reads=["l1f_t8"], writes=["l1f_t4s"], tag="l1f_t4s")
    fw.op("vector",
          lambda e: e.tensor_scalar(t4s[:, :], t4s[:, :], 1e-12, None,
                                    OP.max),
          reads=["l1f_t4s"], writes=["l1f_t4s"], tag="l1f_clip")
    fw.op("vector", lambda e: e.reciprocal(t4s[:, :], t4s[:, :]),
          reads=["l1f_t4s"], writes=["l1f_t4s"], tag="l1f_rc2")
    fw.op("vector",
          lambda e: e.tensor_tensor(msk[:, :, :], ex[:, :, :],
                                    t8[:, :, 3:4]
                                    .to_broadcast([128, 16, 32]), OP.is_ge),
          reads=["l1f_ex", "l1f_t8"], writes=["l1f_lg"], tag="l1f_lg")
    fw.op("vector",
          lambda e: e.tensor_tensor(cw[:, :, :], ex[:, :, :], msk[:, :, :],
                                    OP.mult),
          reads=["l1f_ex", "l1f_lg"], writes=["l1f_cw"], tag="l1f_cw1")
    fw.op("vector",
          lambda e: e.tensor_tensor(cw[:, :, :], cw[:, :, :],
                                    t4s[:, :].unsqueeze(2)
                                    .to_broadcast([128, 16, 32]), OP.mult),
          reads=["l1f_cw", "l1f_t4s"], writes=["l1f_cw"], tag="l1f_cw2")

    def trc(e):
        last = None
        for c in range(16):
            last = e.transpose(psA[0:32, c * 128:(c + 1) * 128],
                               cw[:, c, :], ident[:, :])
        return last
    fw.op("tensor", trc, reads=["l1f_cw", "c_ident"], writes=_akeys(),
          tag="l1f_trc")
    cwT = lgT
    fw.op("vector", lambda e: e.tensor_copy(cwT[0:32, :], psA[0:32, :]),
          reads=_akeys(), writes=["l1f_lgT"], tag="l1f_cwTcp")
    fw.op("gpsimd", lambda e: [e.dma_start(out=cwstage, in_=cwT[0:32, :])],
          reads=["l1f_lgT"], writes=["cwstage"], dma=True,
          tag="l1f_cwst")

    emit_moe_passes(p, "l1f", 36, 32,
                    lambda e_i: e_i if e_i < 32 else None, gu_dram, d_dram,
                    cwstage, hwstage)


def emit_moe_passes(p, pre, n_virt, n_route, cw_row, gu_dram, d_dram,
                    cwstage, hwstage, tcw=512, pool_elems=23040,
                    wd_in_xln=False):
    fw, b = p.fw, p.b
    psA, psB = p.psA, p.psB
    xln = b["xln"]
    sbx = b["x"]
    xkeys = [f"xln:{k}" for k in range(KT)]
    pool = b.S(pre + "pool", [128, pool_elems], BF16)
    prestaged = (pre + "_dbf") in p.dram
    d_bf = p.dscratch(pre + "_dbf", [KT, 128, n_virt, 128], BF16)
    if not prestaged:
        fw.op("gpsimd",
              lambda e: [e.dma_start(out=d_bf, in_=d_dram)],
              writes=[pre + "_dbf"], dma=True, tag=pre + "_stgd")
    wgu = pool[:, 0:4096].rearrange("p (d g k h) -> p d g k h",
                                    d=2, g=2, k=KT, h=128)
    gt = pool[:, 4096:6144]
    hwe = pool[:, 6144:10240].rearrange("p (d t) -> p d t", d=2)
    cwbe = pool[:, 10240:14336].rearrange("p (d t) -> p d t", d=2)
    for e_i in range(n_virt):
        db = e_i % 2
        crow = cw_row(e_i)
        if crow is not None:
            fw.op("gpsimd",
                  lambda e, crow=crow: [e.dma_start(
                      out=cwbe[:, 0, :],
                      in_=cwstage[crow:crow + 1, :]
                      .to_broadcast([128, T]))],
                  reads=["cwstage"], writes=[pre + "_cwbe:0"], dma=True,
                  tag=f"{pre}_ldcw{e_i}")
        fw.op("gpsimd",
              lambda e, e_i=e_i, db=db: [e.dma_start(
                  out=wgu[:, db, :, :, :], in_=gu_dram[e_i])],
              writes=[f"{pre}_wgu:{db}"], dma=True, tag=f"{pre}_ldgu{e_i}")

        def gmm(e, db=db):
            last = None
            for k in range(KT):
                for c in range(TC):
                    sl = slice(c * 512, (c + 1) * 512)
                    last = e.matmul(psA[:, sl], wgu[:, db, 0, k, :],
                                    xln[:, k, sl],
                                    start=(k == 0), stop=(k == KT - 1))
            return last
        fw.op("tensor", gmm, reads=[f"{pre}_wgu:{db}"] + xkeys,
              writes=_akeys(), tag=f"{pre}_g{e_i}")

        def umm(e, db=db):
            last = None
            for k in range(KT):
                for c in range(TC):
                    sl = slice(c * 512, (c + 1) * 512)
                    last = e.matmul(psB[:, sl], wgu[:, db, 1, k, :],
                                    xln[:, k, sl],
                                    start=(k == 0), stop=(k == KT - 1))
            return last
        fw.op("tensor", umm, reads=[f"{pre}_wgu:{db}"] + xkeys,
              writes=_bkeys(), tag=f"{pre}_u{e_i}")
        fw.op("scalar",
              lambda e: e.activation(gt, psA[:, :], AF.Silu),
              reads=_akeys(), writes=[pre + "_gts"], tag=f"{pre}_si{e_i}")
        fw.op("vector",
              lambda e: e.tensor_tensor(hwe[:, 0, :], psB[:, :],
                                        gt, OP.mult),
              reads=_bkeys() + [pre + "_gts"], writes=[pre + "_hwe:0"],
              tag=f"{pre}_hw{e_i}")
        if crow is not None:
            fw.op("vector",
                  lambda e: e.tensor_tensor(hwe[:, 0, :], hwe[:, 0, :],
                                            cwbe[:, 0, :], OP.mult),
                  reads=[pre + "_hwe:0", pre + "_cwbe:0"],
                  writes=[pre + "_hwe:0"], tag=f"{pre}_hwc{e_i}")
        fw.op("sync",
              lambda e, e_i=e_i: [e.dma_start(
                  out=hwstage[e_i], in_=hwe[:, 0, :])],
              reads=[pre + "_hwe:0"], writes=[f"hws:{e_i}"], dma=True,
              tag=f"{pre}_sthw{e_i}")

    if wd_in_xln:
        n1 = min(n_virt, pool_elems // tcw)
        n2 = n_virt - n1
        assert n2 * tcw <= KT * T, "xln carve overflow"
        assert n_virt * 128 <= KT * T, "lnscr carve overflow"
        hwbuf = pool[:, 0:n1 * tcw].rearrange("p (e t) -> p e t", e=n1)
        hwbuf2 = b["xln"][:, :, :].rearrange("p k t -> p (k t)") \
            [:, 0:n2 * tcw].rearrange("p (e t) -> p e t", e=n2)
        fw.alias([pre + "_hwb"],
                 [f"{pre}_wgu:0", f"{pre}_wgu:1", pre + "_gts",
                  pre + "_hwe:0", pre + "_cwbe:0"])
        fw.alias([pre + "_hwb2"], [f"xln:{k}" for k in range(KT)])
        fw.alias([pre + "_wd"], [f"lnscr:{k}" for k in range(KT)])
        wd = b["lnscr"][:, :, :].rearrange("p k t -> p (k t)") \
            [:, 0:n_virt * 128].rearrange("p (e h) -> p e h", e=n_virt)
        wd2 = None
    else:
        hwbuf = pool[:, 0:n_virt * tcw].rearrange("p (e t) -> p e t",
                                                  e=n_virt)
        hwbuf2 = None
        n1 = n_virt
        fw.alias([pre + "_hwb"],
                 [f"{pre}_wgu:0", f"{pre}_wgu:1", pre + "_gts",
                  pre + "_hwe:0", pre + "_cwbe:0"])
        assert 2 * n_virt * 128 <= KT * T, "lnscr wd overflow"
        fw.alias([pre + "_wd", pre + "_wd2"],
                 [f"lnscr:{k}" for k in range(KT)])
        lnf = b["lnscr"][:, :, :].rearrange("p k t -> p (k t)")
        wd = lnf[:, 0:n_virt * 128].rearrange("p (e h) -> p e h",
                                              e=n_virt)
        wd2 = lnf[:, n_virt * 128:2 * n_virt * 128] \
            .rearrange("p (e h) -> p e h", e=n_virt)
        assert n_virt * tcw <= pool_elems, "moe pool overflow"
    ntc = T // tcw
    for c in range(ntc):
        sl = slice(c * tcw, (c + 1) * tcw)
        if wd_in_xln:
            fw.op("gpsimd",
                  lambda e, c=c: [e.dma_start(
                      out=hwbuf[:],
                      in_=hwstage[0:n1, :, c * tcw:(c + 1) * tcw]
                      .rearrange("e p t -> p e t"))],
                  reads=[f"hws:{i}" for i in range(n1)],
                  writes=[pre + "_hwb"], dma=True, tag=f"{pre}_ldhw{c}")
            fw.op("sync",
                  lambda e, c=c: [e.dma_start(
                      out=hwbuf2[:],
                      in_=hwstage[n1:n_virt, :, c * tcw:(c + 1) * tcw]
                      .rearrange("e p t -> p e t"))],
                  reads=[f"hws:{i}" for i in range(n1, n_virt)],
                  writes=[pre + "_hwb2"], dma=True,
                  tag=f"{pre}_ldhw2_{c}")
        else:
            fw.op("gpsimd",
                  lambda e, c=c: [e.dma_start(
                      out=hwbuf[:],
                      in_=hwstage[:, :, c * tcw:(c + 1) * tcw]
                      .rearrange("e p t -> p e t"))],
                  reads=[f"hws:{i}" for i in range(n_virt)],
                  writes=[pre + "_hwb"], dma=True, tag=f"{pre}_ldhw{c}")
        for o in range(KT):
            db = o % 2
            wdo = wd if (wd2 is None or o % 2 == 0) else wd2
            wdk = pre + ("_wd" if (wd2 is None or o % 2 == 0) else "_wd2")
            fw.op("gpsimd",
                  lambda e, o=o, wdo=wdo: [e.dma_start(out=wdo,
                                                       in_=d_bf[o])],
                  reads=[pre + "_dbf"], writes=[wdk], dma=True,
                  tag=f"{pre}_ldd{c}_{o}")
            po = psB[:, db * 512:db * 512 + tcw]
            ko = f"B{db}"

            def dmm(e, po=po, wdo=wdo):
                last = None
                for e_i in range(n_virt):
                    if wd_in_xln and e_i >= n1:
                        hslc = hwbuf2[:, e_i - n1, :]
                    else:
                        hslc = hwbuf[:, e_i, :]
                    last = e.matmul(po, wdo[:, e_i, :], hslc,
                                    start=(e_i == 0),
                                    stop=(e_i == n_virt - 1))
                return last
            rd = [wdk, pre + "_hwb"]
            if wd_in_xln:
                rd.append(pre + "_hwb2")
            fw.op("tensor", dmm, reads=rd, writes=[ko],
                  tag=f"{pre}_d{c}_{o}")
            fw.op("vector",
                  lambda e, o=o, po=po, sl=sl: e.tensor_tensor(
                      sbx[:, o, sl], sbx[:, o, sl], po, OP.add),
                  reads=[ko, f"x:{o}"], writes=[f"x:{o}"],
                  tag=f"{pre}_add{c}_{o}")


def host_prep_l2ffn(inp):
    out = {}
    out["w_l2f_gate"] = lhsT_prep(inp["L2_ffn_w_gate"])
    w1, w3, w2 = inp["L2_ffn_w1_e"], inp["L2_ffn_w3_e"], inp["L2_ffn_w2_e"]
    s1, s3, s2 = inp["L2_ffn_w1_s"], inp["L2_ffn_w3_s"], inp["L2_ffn_w2_s"]
    gus, downs = [], []
    for ex in range(16):
        for j in range(4):
            g = w1[ex][j * 128:(j + 1) * 128].T.reshape(KT, 128, 128)
            u = w3[ex][j * 128:(j + 1) * 128].T.reshape(KT, 128, 128)
            gus.append(np.stack([g, u], axis=0))
            downs.append(w2[ex].T[j * 128:(j + 1) * 128])
    for j in range(8):
        g = s1[j * 128:(j + 1) * 128].T.reshape(KT, 128, 128)
        u = s3[j * 128:(j + 1) * 128].T.reshape(KT, 128, 128)
        gus.append(np.stack([g, u], axis=0))
        downs.append(s2.T[j * 128:(j + 1) * 128])
    gus = np.stack(gus, axis=0)
    out["w_l2f_gu"] = _f(gus.transpose(0, 3, 1, 2, 4))
    big = np.stack(downs, axis=0)
    out["w_l2f_d"] = _f(big.reshape(72, 128, KT, 128)
                        .transpose(2, 1, 0, 3))
    return out


def emit_l2ffn(p):
    fw, b = p.fw, p.b
    psA, psB = p.psA, p.psB
    sbx = b["x"]
    xfkeys = [f"x:{k}" for k in range(KT)]
    p.load("w_l2f_gate", [128, KT, 16], dt=F32)
    gu_dram = p.din("w_l2f_gu", [72, 128, 2, KT, 128])
    d_dram = p.din("w_l2f_d", [KT, 128, 72, 128])
    hwstage = p.dscratch("l2f_hwstage", [72, 128, T], BF16)
    cwstage = p.dscratch("l2f_cwstage", [16, T], BF16)
    wg = b["w_l2f_gate"]
    ident = b["c_ident"]
    murow = b["rowsA"][0:1, :]
    rsrow = b["rowsB"][0:1, :]

    ngw = b.S("l2f_ngw", [1, 16], F32)

    def gsum(e):
        last = None
        for k in range(KT):
            last = e.matmul(psB[0:1, 0:16], b["c_onescol"][:, :],
                            wg[:, k, :], start=(k == 0),
                            stop=(k == KT - 1))
        return last
    fw.op("tensor", gsum, reads=["w_l2f_gate", "c_onescol"],
          writes=["B0"], tag="l2f_gws")
    fw.op("vector",
          lambda e: e.tensor_scalar(ngw[:, :], psB[0:1, 0:16], -1.0, None,
                                    OP.mult),
          reads=["B0"], writes=["l2f_ngw"], tag="l2f_ngw")

    def rebc(e):
        last = None
        for c in range(TC):
            sl = slice(c * 512, (c + 1) * 512)
            last = e.matmul(psB[:, sl], b["c_onesrow"][:, :],
                            rsrow[:, sl], start=True, stop=True)
        return last
    fw.op("tensor", rebc, reads=["rowB0", "c_onesrow"], writes=_bkeys(),
          tag="l2f_rebc")

    def rmm(e):
        last = None
        for k in range(KT):
            for c in range(TC):
                sl = slice(c * 512, (c + 1) * 512)
                e.matmul(psA[0:16, sl], wg[:, k, :], sbx[:, k, sl],
                         start=(k == 0), stop=False)
        for c in range(TC):
            sl = slice(c * 512, (c + 1) * 512)
            last = e.matmul(psA[0:16, sl], ngw[:, :], murow[:, sl],
                            start=False, stop=True)
        return last
    fw.op("tensor", rmm,
          reads=["w_l2f_gate", "l2f_ngw", "rowA0"] + xfkeys,
          writes=_akeys(), tag="l2f_rmm")
    lgT = b.S("l2f_lgT", [16, T], F32)
    fw.op("scalar", lambda e: e.copy(lgT[:, :], psA[0:16, :]),
          reads=_akeys(), writes=["l2f_lgT"], tag="l2f_lg0")
    fw.op("vector",
          lambda e: e.tensor_tensor(lgT[:, :], lgT[:, :], psB[0:16, :],
                                    OP.mult),
          reads=["l2f_lgT"] + _bkeys(), writes=["l2f_lgT"], tag="l2f_lgm")

    lg = b.S("l2f_lg", [128, 16, 16], F32)
    for half in range(2):
        def trs(e, half=half):
            last = None
            for c in range(8):
                cc = half * 8 + c
                last = e.transpose(psB[:, c * 16:(c + 1) * 16],
                                   lgT[:, cc * 128:(cc + 1) * 128],
                                   ident[0:16, 0:16])
            return last
        fw.op("tensor", trs, reads=["l2f_lgT", "c_ident"], writes=["B0"],
              tag=f"l2f_tr{half}")
        fw.op("vector",
              lambda e, half=half: e.tensor_copy(
                  lg[:, half * 8:(half + 1) * 8, :],
                  psB[:, 0:128].rearrange("p (c e) -> p c e", c=8)),
              reads=["B0"], writes=["l2f_lg"], tag=f"l2f_lgcp{half}")
    s = b.S("l2f_s", [128, 16, 16], F32)
    fw.op("scalar", lambda e: e.activation(s[:, :, :], lg[:, :, :],
                                           AF.Sigmoid),
          reads=["l2f_lg"], writes=["l2f_s"], tag="l2f_sig")

    sv = s[:, :, :].rearrange("p c (g j) -> p c g j", g=4)
    m1 = b.S("l2f_m1", [128, 16, 4], F32)
    mk1 = b.S("l2f_mk1", [128, 16, 4, 4], F32)
    gs = b.S("l2f_gs", [128, 16, 4], F32)
    gm = b.S("l2f_gm", [128, 16], F32)
    am = b.S("l2f_am", [128, 16, 4], F32)
    smk = b.S("l2f_smk", [128, 16, 16], F32)
    t8 = b.S("l2f_t8", [128, 16, 8], F32)
    cw = b.S("l2f_cw", [128, 16, 16], F32)
    den = b.S("l2f_den", [128, 16], F32)
    fw.op("vector", lambda e: e.tensor_reduce(m1[:, :, :], sv, AX.X, OP.max),
          reads=["l2f_s"], writes=["l2f_m1"], tag="l2f_m1")
    fw.op("vector",
          lambda e: e.tensor_tensor(mk1[:, :, :, :], sv,
                                    m1[:, :, :].unsqueeze(3)
                                    .to_broadcast([128, 16, 4, 4]),
                                    OP.is_ge),
          reads=["l2f_s", "l2f_m1"], writes=["l2f_mk1"], tag="l2f_mk1")
    fw.op("vector",
          lambda e: e.scalar_tensor_tensor(mk1[:, :, :, :], mk1[:, :, :, :],
                                           -1e9, sv, OP.mult, OP.add),
          reads=["l2f_mk1", "l2f_s"], writes=["l2f_mk1"], tag="l2f_sv2")
    fw.op("vector",
          lambda e: e.tensor_reduce(gs[:, :, :], mk1[:, :, :, :], AX.X,
                                    OP.max),
          reads=["l2f_mk1"], writes=["l2f_gs"], tag="l2f_m2")
    fw.op("vector",
          lambda e: e.tensor_tensor(gs[:, :, :], gs[:, :, :], m1[:, :, :],
                                    OP.add),
          reads=["l2f_gs", "l2f_m1"], writes=["l2f_gs"], tag="l2f_gs")
    fw.op("vector", lambda e: e.tensor_reduce(gm[:, :], gs[:, :, :], AX.X,
                                              OP.max),
          reads=["l2f_gs"], writes=["l2f_gm"], tag="l2f_gm1")
    fw.op("vector",
          lambda e: e.tensor_tensor(am[:, :, :], gs[:, :, :],
                                    gm[:, :].unsqueeze(2)
                                    .to_broadcast([128, 16, 4]), OP.is_ge),
          reads=["l2f_gs", "l2f_gm"], writes=["l2f_am"], tag="l2f_amk")
    fw.op("vector",
          lambda e: e.scalar_tensor_tensor(am[:, :, :], am[:, :, :], -1e9,
                                           gs[:, :, :], OP.mult, OP.add),
          reads=["l2f_am", "l2f_gs"], writes=["l2f_am"], tag="l2f_gs2")
    fw.op("vector", lambda e: e.tensor_reduce(gm[:, :], am[:, :, :], AX.X,
                                              OP.max),
          reads=["l2f_am"], writes=["l2f_gm"], tag="l2f_gm2")
    fw.op("vector",
          lambda e: e.tensor_tensor(am[:, :, :], gs[:, :, :],
                                    gm[:, :].unsqueeze(2)
                                    .to_broadcast([128, 16, 4]), OP.is_ge),
          reads=["l2f_gs", "l2f_gm"], writes=["l2f_am"], tag="l2f_am")
    fw.op("vector",
          lambda e: e.tensor_tensor(
              smk[:, :, :].rearrange("p c (g j) -> p c g j", g=4), sv,
              am[:, :, :].unsqueeze(3).to_broadcast([128, 16, 4, 4]),
              OP.mult),
          reads=["l2f_am", "l2f_s"], writes=["l2f_smk"], tag="l2f_smk")
    mbig = b.S("l2f_mbig", [128, 16, 4], F32)
    fw.op("vector",
          lambda e: e.tensor_scalar(mbig[:, :, :], am[:, :, :], 1e9, -1e9,
                                    OP.mult, OP.add),
          reads=["l2f_am"], writes=["l2f_mbig"], tag="l2f_mbig")
    fw.op("vector",
          lambda e: e.tensor_tensor(
              smk[:, :, :].rearrange("p c (g j) -> p c g j", g=4),
              smk[:, :, :].rearrange("p c (g j) -> p c g j", g=4),
              mbig[:, :, :].unsqueeze(3).to_broadcast([128, 16, 4, 4]),
              OP.add),
          reads=["l2f_smk", "l2f_mbig"], writes=["l2f_smk"],
          tag="l2f_smk2")

    def max8s(e):
        last = None
        for c in range(16):
            last = e.max(out=t8[:, c, :], in_=smk[:, c, :])
            e.drain()
        return last
    fw.op("vector", max8s, reads=["l2f_smk"], writes=["l2f_t8"],
          tag="l2f_max8")
    fw.op("vector",
          lambda e: e.tensor_tensor(cw[:, :, :], smk[:, :, :],
                                    t8[:, :, 3:4]
                                    .to_broadcast([128, 16, 16]), OP.is_ge),
          reads=["l2f_smk", "l2f_t8"], writes=["l2f_cw"], tag="l2f_mk4")
    fw.op("vector",
          lambda e: e.tensor_tensor(cw[:, :, :], cw[:, :, :], s[:, :, :],
                                    OP.mult),
          reads=["l2f_cw", "l2f_s"], writes=["l2f_cw"], tag="l2f_w")
    fw.op("vector",
          lambda e: e.tensor_reduce(den[:, :], cw[:, :, :], AX.X, OP.add),
          reads=["l2f_cw"], writes=["l2f_den"], tag="l2f_den")
    fw.op("vector",
          lambda e: e.tensor_scalar(den[:, :], den[:, :], 1e-6, None,
                                    OP.add),
          reads=["l2f_den"], writes=["l2f_den"], tag="l2f_den2")
    fw.op("vector", lambda e: e.reciprocal(den[:, :], den[:, :]),
          reads=["l2f_den"], writes=["l2f_den"], tag="l2f_rc")
    fw.op("vector",
          lambda e: e.tensor_tensor(cw[:, :, :], cw[:, :, :],
                                    den[:, :].unsqueeze(2)
                                    .to_broadcast([128, 16, 16]), OP.mult),
          reads=["l2f_cw", "l2f_den"], writes=["l2f_cw"], tag="l2f_cwn")

    def trc(e):
        last = None
        for c in range(16):
            last = e.transpose(psA[0:16, c * 128:(c + 1) * 128],
                               cw[:, c, :], ident[:, :])
        return last
    fw.op("tensor", trc, reads=["l2f_cw", "c_ident"], writes=_akeys(),
          tag="l2f_trc")
    cwT = lgT
    fw.op("vector", lambda e: e.tensor_copy(cwT[0:16, :], psA[0:16, :]),
          reads=_akeys(), writes=["l2f_lgT"], tag="l2f_cwTcp")
    fw.op("gpsimd", lambda e: [e.dma_start(out=cwstage, in_=cwT[0:16, :])],
          reads=["l2f_lgT"], writes=["cwstage"], dma=True, tag="l2f_cwst")

    emit_moe_passes(p, "l2f", 72, 64,
                    lambda e_i: e_i // 4 if e_i < 64 else None,
                    gu_dram, d_dram, cwstage, hwstage, tcw=512,
                    pool_elems=20480, wd_in_xln=True)


def host_prep_l0ffn(inp):
    out = {}
    Wg = inp["L0_ffn_Wg"]
    out["w_l0f_gate"] = _f(Wg.reshape(KT, 128, 8).transpose(1, 0, 2))
    A = inp["L0_ffn_A"]
    Astack = np.concatenate([A[e] for e in range(8)], axis=1)
    out["w_l0f_A"] = _f(Astack.reshape(KT, 128, 128).transpose(1, 0, 2))
    B = inp["L0_ffn_B"]
    out["w_l0f_B"] = _f(B.reshape(128, 1024))
    return out


def emit_l0ffn(p):
    fw, b = p.fw, p.b
    psA, psB = p.psA, p.psB
    sbx = b["x"]
    xfkeys = [f"x:{k}" for k in range(KT)]
    xkeys = [f"xln:{k}" for k in range(KT)]
    xln = b["xln"]
    p.load("w_l0f_gate", [128, KT, 8], dt=F32)
    p.load("w_l0f_A", [128, KT, 128])
    p.load("w_l0f_B", [128, 1024])
    cwstage = p.dscratch("l0f_cwstage", [8, T], BF16)
    wg, wa, wb = b["w_l0f_gate"], b["w_l0f_A"], b["w_l0f_B"]
    ident = b["c_ident"]
    murow = b["rowsA"][0:1, :]
    rsrow = b["rowsB"][0:1, :]

    ngw = b.S("l0f_ngw", [1, 8], F32)

    def gsum(e):
        last = None
        for k in range(KT):
            last = e.matmul(psB[0:1, 0:8], b["c_onescol"][:, :],
                            wg[:, k, :], start=(k == 0),
                            stop=(k == KT - 1))
        return last
    fw.op("tensor", gsum, reads=["w_l0f_gate", "c_onescol"],
          writes=["B0"], tag="l0f_gws")
    fw.op("vector",
          lambda e: e.tensor_scalar(ngw[:, :], psB[0:1, 0:8], -1.0, None,
                                    OP.mult),
          reads=["B0"], writes=["l0f_ngw"], tag="l0f_ngw")

    def rebc(e):
        last = None
        for c in range(TC):
            sl = slice(c * 512, (c + 1) * 512)
            last = e.matmul(psB[:, sl], b["c_onesrow"][:, :],
                            rsrow[:, sl], start=True, stop=True)
        return last
    fw.op("tensor", rebc, reads=["rowB0", "c_onesrow"], writes=_bkeys(),
          tag="l0f_rebc")

    def rmm(e):
        last = None
        for k in range(KT):
            for c in range(TC):
                sl = slice(c * 512, (c + 1) * 512)
                e.matmul(psA[0:8, sl], wg[:, k, :], sbx[:, k, sl],
                         start=(k == 0), stop=False)
        for c in range(TC):
            sl = slice(c * 512, (c + 1) * 512)
            last = e.matmul(psA[0:8, sl], ngw[:, :], murow[:, sl],
                            start=False, stop=True)
        return last
    fw.op("tensor", rmm,
          reads=["w_l0f_gate", "l0f_ngw", "rowA0"] + xfkeys,
          writes=_akeys(), tag="l0f_rmm")
    lgT = b.S("l0f_lgT", [8, T], F32)
    fw.op("scalar", lambda e: e.copy(lgT[:, :], psA[0:8, :]),
          reads=_akeys(), writes=["l0f_lgT"], tag="l0f_lg0")
    fw.op("vector",
          lambda e: e.tensor_tensor(lgT[:, :], lgT[:, :], psB[0:8, :],
                                    OP.mult),
          reads=["l0f_lgT"] + _bkeys(), writes=["l0f_lgT"], tag="l0f_lgm")

    lg = b.S("l0f_lg", [128, 16, 8], F32)
    for half in range(2):
        def trs(e, half=half):
            last = None
            for c in range(8):
                cc = half * 8 + c
                last = e.transpose(psB[:, c * 8:(c + 1) * 8],
                                   lgT[:, cc * 128:(cc + 1) * 128],
                                   ident[0:8, 0:8])
            return last
        fw.op("tensor", trs, reads=["l0f_lgT", "c_ident"], writes=["B0"],
              tag=f"l0f_tr{half}")
        fw.op("vector",
              lambda e, half=half: e.tensor_copy(
                  lg[:, half * 8:(half + 1) * 8, :],
                  psB[:, 0:64].rearrange("p (c e) -> p c e", c=8)),
              reads=["B0"], writes=["l0f_lg"], tag=f"l0f_lgcp{half}")
    m1 = b.S("l0f_m1", [128, 16], F32)
    m2 = b.S("l0f_m2", [128, 16], F32)
    mk1 = b.S("l0f_mk1", [128, 16, 8], F32)
    mk2 = b.S("l0f_mk2", [128, 16, 8], F32)
    dm = b.S("l0f_dm", [128, 16], F32)
    s1 = b.S("l0f_s1", [128, 16], F32)
    cw = b.S("l0f_cw", [128, 16, 8], F32)
    fw.op("vector", lambda e: e.tensor_reduce(m1[:, :], lg[:, :, :], AX.X,
                                              OP.max),
          reads=["l0f_lg"], writes=["l0f_m1"], tag="l0f_m1")
    fw.op("vector",
          lambda e: e.tensor_tensor(mk1[:, :, :], lg[:, :, :],
                                    m1[:, :].unsqueeze(2)
                                    .to_broadcast([128, 16, 8]), OP.is_ge),
          reads=["l0f_lg", "l0f_m1"], writes=["l0f_mk1"], tag="l0f_mk1")
    fw.op("vector",
          lambda e: e.scalar_tensor_tensor(mk2[:, :, :], mk1[:, :, :],
                                           -1e9, lg[:, :, :], OP.mult,
                                           OP.add),
          reads=["l0f_mk1", "l0f_lg"], writes=["l0f_mk2"], tag="l0f_lg2")
    fw.op("vector", lambda e: e.tensor_reduce(m2[:, :], mk2[:, :, :], AX.X,
                                              OP.max),
          reads=["l0f_mk2"], writes=["l0f_m2"], tag="l0f_m2")
    fw.op("vector",
          lambda e: e.tensor_tensor(mk2[:, :, :], mk2[:, :, :],
                                    m2[:, :].unsqueeze(2)
                                    .to_broadcast([128, 16, 8]), OP.is_ge),
          reads=["l0f_mk2", "l0f_m2"], writes=["l0f_mk2"], tag="l0f_msk2")
    fw.op("vector", lambda e: e.tensor_tensor(dm[:, :], m2[:, :], m1[:, :],
                                              OP.subtract),
          reads=["l0f_m2", "l0f_m1"], writes=["l0f_dm"], tag="l0f_dm")
    fw.op("scalar", lambda e: e.activation(dm[:, :], dm[:, :], AF.Exp),
          reads=["l0f_dm"], writes=["l0f_dm"], tag="l0f_dexp")
    fw.op("vector",
          lambda e: e.tensor_scalar(s1[:, :], dm[:, :], 1.0, None, OP.add),
          reads=["l0f_dm"], writes=["l0f_s1"], tag="l0f_s1a")
    fw.op("vector", lambda e: e.reciprocal(s1[:, :], s1[:, :]),
          reads=["l0f_s1"], writes=["l0f_s1"], tag="l0f_s1r")
    fw.op("vector", lambda e: e.tensor_tensor(dm[:, :], dm[:, :], s1[:, :],
                                              OP.mult),
          reads=["l0f_dm", "l0f_s1"], writes=["l0f_dm"], tag="l0f_w2")
    fw.op("vector",
          lambda e: e.tensor_tensor(cw[:, :, :], mk1[:, :, :],
                                    s1[:, :].unsqueeze(2)
                                    .to_broadcast([128, 16, 8]), OP.mult),
          reads=["l0f_mk1", "l0f_s1"], writes=["l0f_cw"], tag="l0f_cw1")
    fw.op("vector",
          lambda e: e.scalar_tensor_tensor(mk2[:, :, :], mk2[:, :, :], 1.0,
                                           dm[:, :].unsqueeze(2)
                                           .to_broadcast([128, 16, 8]),
                                           OP.mult, OP.mult),
          reads=["l0f_mk2", "l0f_dm"], writes=["l0f_mk2"], tag="l0f_cw2")
    fw.op("vector",
          lambda e: e.tensor_tensor(cw[:, :, :], cw[:, :, :],
                                    mk2[:, :, :], OP.add),
          reads=["l0f_cw", "l0f_mk2"], writes=["l0f_cw"], tag="l0f_cw3")

    def trc(e):
        last = None
        for c in range(16):
            last = e.transpose(psA[0:8, c * 128:(c + 1) * 128],
                               cw[:, c, :], ident[:, :])
        return last
    fw.op("tensor", trc, reads=["l0f_cw", "c_ident"], writes=_akeys(),
          tag="l0f_trc")
    cwT = lgT
    fw.op("vector", lambda e: e.tensor_copy(cwT[0:8, :], psA[0:8, :]),
          reads=_akeys(), writes=["l0f_lgT"], tag="l0f_cwTcp")
    fw.op("gpsimd", lambda e: [e.dma_start(out=cwstage, in_=cwT[0:8, :])],
          reads=["l0f_lgT"], writes=["cwstage"], dma=True, tag="l0f_cwst")
    cwbc = b.S("l0f_cwbc", [128, T], BF16)
    fw.op("gpsimd",
          lambda e: [e.dma_start(
              out=cwbc[:],
              in_=cwstage.unsqueeze(1).to_broadcast([8, 16, T]))],
          reads=["cwstage"], writes=["l0f_cwbc"], dma=True, tag="l0f_bc")

    def hmm(e):
        last = None
        for k in range(KT):
            for c in range(TC):
                sl = slice(c * 512, (c + 1) * 512)
                last = e.matmul(psA[:, sl], wa[:, k, :], xln[:, k, sl],
                                start=(k == 0), stop=(k == KT - 1))
        return last
    fw.op("tensor", hmm, reads=["w_l0f_A"] + xkeys, writes=_akeys(),
          tag="l0f_hmm")
    h8 = b.S("l0f_h8", [128, T], BF16)
    fw.op("vector",
          lambda e: e.tensor_tensor(h8[:, :], psA[:, :], cwbc[:, :],
                                    OP.mult),
          reads=_akeys() + ["l0f_cwbc"], writes=["l0f_h8"], tag="l0f_hw")
    for o in range(KT):
        db = o % 2
        po = psB[:, db * 512:(db + 1) * 512]
        ko = f"B{db}"
        for c in range(TC):
            sl = slice(c * 512, (c + 1) * 512)
            fw.op("tensor",
                  lambda e, o=o, po=po, sl=sl: e.matmul(
                      po, wb[:, o * 128:(o + 1) * 128], h8[:, sl],
                      start=True, stop=True),
                  reads=["w_l0f_B", "l0f_h8"], writes=[ko],
                  tag=f"l0f_d{o}_{c}")
            fw.op("vector",
                  lambda e, o=o, po=po, sl=sl: e.tensor_tensor(
                      sbx[:, o, sl], sbx[:, o, sl], po, OP.add),
                  reads=[ko, f"x:{o}"], writes=[f"x:{o}"],
                  tag=f"l0f_add{o}_{c}")


def host_prep_l1attn(inp):
    out = {}
    for i in (0, 1):
        out[f"w_l1a_wz{i}"] = lhsT_prep(inp[f"L1_attn_Wz{i}"])
        out[f"w_l1a_wh{i}"] = lhsT_prep(inp[f"L1_attn_Wh{i}"])
        bz = inp[f"L1_attn_bz{i}"]
        bh = inp[f"L1_attn_bh{i}"]
        out[f"w_l1a_bz{i}"] = _f(bz.reshape(KT, 128).T)
        out[f"w_l1a_nbz{i}"] = _f(-bz.reshape(KT, 128).T)
        out[f"w_l1a_bh{i}"] = _f(bh.reshape(KT, 128).T)
    return out


def emit_mingru(p, uid, srcget, srckeys, wzd, whd, bz, bzk, nbz, nbzk,
                bh, bhk, hout, houtkey):
    fw, b = p.fw, p.b
    psA, psB = p.psA, p.psB
    wz = b.S(uid + "wz", [128, 2, KT, 128], BF16)
    wh = b.S(uid + "wh", [128, 2, KT, 128], BF16)
    cbuf = b.S("gru_c", [128, T], F32)
    ubuf = b.S("gru_u", [128, T], F32)
    thb = b.S("gru_th", [128, T], F32)
    zbuf = b.S("gru_z", [128, T], BF16)
    sbuf_ = b.S("gru_s", [128, T], BF16)
    mbuf = b.S("gru_m", [128, T], mybir.dt.uint8)
    for o in range(KT):
        db = o % 2
        fw.op("gpsimd",
              lambda e, o=o, db=db: [
                  e.dma_start(out=wz[:, db],
                              in_=wzd[:, :, o * 128:(o + 1) * 128]),
                  e.dma_start(out=wh[:, db],
                              in_=whd[:, :, o * 128:(o + 1) * 128])],
              writes=[f"{uid}wz:{db}", f"{uid}wh:{db}"], dma=True, n_dma=2,
              tag=f"{uid}ldw{o}")

        def zmm(e, db=db):
            last = None
            for k in range(KT):
                for c in range(TC):
                    sl = slice(c * 512, (c + 1) * 512)
                    last = e.matmul(psA[:, sl], wz[:, db, k, :], srcget(k, sl),
                                    start=(k == 0), stop=(k == KT - 1))
            return last
        fw.op("tensor", zmm, reads=[f"{uid}wz:{db}"] + srckeys,
              writes=_akeys(), tag=f"{uid}zmm{o}")

        def hmm(e, db=db):
            last = None
            for k in range(KT):
                for c in range(TC):
                    sl = slice(c * 512, (c + 1) * 512)
                    last = e.matmul(psB[:, sl], wh[:, db, k, :], srcget(k, sl),
                                    start=(k == 0), stop=(k == KT - 1))
            return last
        fw.op("tensor", hmm, reads=[f"{uid}wh:{db}"] + srckeys,
              writes=_bkeys(), tag=f"{uid}hmm{o}")

        fw.op("scalar",
              lambda e, o=o: e.activation(cbuf[:, :], psA[:, :], AF.Sigmoid,
                                          bias=nbz[:, o:o + 1], scale=-1.0),
              reads=_akeys() + [nbzk], writes=["gru_c"],
              tag=f"{uid}c{o}")
        fw.op("scalar",
              lambda e, o=o: e.activation(zbuf[:, :], psA[:, :], AF.Sigmoid,
                                          bias=bz[:, o:o + 1]),
              reads=_akeys() + [bzk], writes=["gru_z"],
              tag=f"{uid}z{o}")
        fw.op("scalar",
              lambda e, o=o: e.activation(sbuf_[:, :], psB[:, :], AF.Sigmoid,
                                          bias=bh[:, o:o + 1]),
              reads=_bkeys() + [bhk], writes=["gru_s"],
              tag=f"{uid}s{o}")
        fw.op("scalar",
              lambda e, o=o: e.activation(thb[:, :], psB[:, :], AF.Identity,
                                          bias=bh[:, o:o + 1]),
              reads=_bkeys() + [bhk], writes=["gru_th"],
              tag=f"{uid}th{o}")
        fw.op("vector",
              lambda e: e.tensor_scalar(mbuf[:, :], thb[:, :], 0.0, None,
                                        OP.is_ge),
              reads=["gru_th"], writes=["gru_m"], tag=f"{uid}m{o}")
        fw.op("vector",
              lambda e: e.tensor_scalar(thb[:, :], thb[:, :], 0.0, 0.5,
                                        OP.max, OP.add),
              reads=["gru_th"], writes=["gru_th"], tag=f"{uid}p{o}")
        fw.op("vector",
              lambda e: e.select(ubuf[:, :], mbuf[:, :], thb[:, :],
                                 sbuf_[:, :], add_drain=True),
              reads=["gru_m", "gru_th", "gru_s"], writes=["gru_u"],
              tag=f"{uid}sel{o}")
        fw.op("vector",
              lambda e: e.tensor_tensor(ubuf[:, :], ubuf[:, :], zbuf[:, :],
                                        OP.mult),
              reads=["gru_u", "gru_z"], writes=["gru_u"], tag=f"{uid}u{o}")
        fw.op("vector",
              lambda e, o=o: e.tensor_tensor_scan(hout[:, o, :], cbuf[:, :],
                                                  ubuf[:, :], 0.5,
                                                  OP.mult, OP.add),
              reads=["gru_c", "gru_u"], writes=[f"{houtkey}:{o}"],
              tag=f"{uid}scan{o}")


def emit_l1attn(p):
    fw, b = p.fw, p.b
    for i in (0, 1):
        for nm in ("bz", "nbz", "bh"):
            p.load(f"w_l1a_{nm}{i}", [128, KT], dt=F32)
    wz0 = p.din("w_l1a_wz0", [128, KT, 1024])
    wh0 = p.din("w_l1a_wh0", [128, KT, 1024])
    wz1 = p.din("w_l1a_wz1", [128, KT, 1024])
    wh1 = p.din("w_l1a_wh1", [128, KT, 1024])
    xln = b["xln"]
    h1 = b["lnscr"]
    fw.alias(["l1a_h1:" + str(k) for k in range(KT)],
             ["lnscr:" + str(k) for k in range(KT)])
    emit_mingru(p, "g0", lambda k, sl: xln[:, k, sl],
                [f"xln:{k}" for k in range(KT)], wz0, wh0,
                b["w_l1a_bz0"], "w_l1a_bz0", b["w_l1a_nbz0"], "w_l1a_nbz0",
                b["w_l1a_bh0"], "w_l1a_bh0", h1, "l1a_h1")
    fw.alias(["l1a_h2:" + str(k) for k in range(KT)],
             ["xln:" + str(k) for k in range(KT)])
    emit_mingru(p, "g1", lambda k, sl: h1[:, k, sl],
                [f"l1a_h1:{k}" for k in range(KT)], wz1, wh1,
                b["w_l1a_bz1"], "w_l1a_bz1", b["w_l1a_nbz1"], "w_l1a_nbz1",
                b["w_l1a_bh1"], "w_l1a_bh1", xln, "l1a_h2")
    sbx = b["x"]
    for o in range(KT):
        fw.op("vector",
              lambda e, o=o: e.tensor_tensor(sbx[:, o, :], sbx[:, o, :],
                                             xln[:, o, :], OP.add),
              reads=[f"l1a_h2:{o}", f"x:{o}"], writes=[f"x:{o}"],
              tag=f"l1a_add{o}")


def host_prep_l3ffn(inp):
    out = {}
    for nm in ("wf", "uf", "wi", "ui", "wc", "uc", "wo", "uo"):
        out[f"w_l3f_{nm}"] = lhsT_prep(inp[f"L3_ffn_{nm}"])
    for nm in ("bf", "bi", "bc", "bo"):
        out[f"w_l3f_{nm}"] = _f(inp[f"L3_ffn_{nm}"].reshape(KT, 128).T)
    c0 = inp["L3_ffn_in_c"].reshape(KT, 128).T
    out["w_l3f_c0"] = _f(np.repeat(c0[:, :, None], 256, axis=2))
    return out


def emit_l3ffn(p):
    fw, b = p.fw, p.b
    psA, psB = p.psA, p.psB
    xln = b["xln"]
    sbx = b["x"]
    lnscr = b["lnscr"]
    for nm in ("bf", "bi", "bc", "bo"):
        p.load(f"w_l3f_{nm}", [128, KT], dt=F32)
    cst = p.load("w_l3f_c0", [128, KT, 256], dt=F32, sb_name="l3f_c")
    xv = sbx[:, :, :].bitcast(BF16)
    sv = lnscr[:, :, :].rearrange("p k t -> p (k t)")
    wviews = {
        "wf": xv[:, :, 0:1024].rearrange("p k o -> p k o"),
        "uf": xv[:, :, 1024:2048],
        "wi": xv[:, :, 2048:3072],
        "ui": xv[:, :, 3072:4096],
        "wc": sv[:, 0:8192].rearrange("p (k o) -> p k o", k=KT),
        "uc": sv[:, 8192:16384].rearrange("p (k o) -> p k o", k=KT),
    }
    xkeys = [f"x:{k}" for k in range(KT)]
    lnkeys = [f"lnscr:{k}" for k in range(KT)]
    fw.alias(["l3f_w"], xkeys + lnkeys)
    for nm, view in wviews.items():
        ap = p.din(f"w_l3f_{nm}", [128, KT, 1024])
        fw.op("gpsimd", lambda e, v=view, ap=ap: [e.dma_start(out=v, in_=ap)],
              reads=["l3f_w"], writes=[f"l3f_{nm}"], dma=True,
              tag=f"l3f_ld{nm}")
    cb = b.S("l3f_cb", [128, KT, 256], BF16)
    fb = b.S("l3f_fb", [128, KT, 256], BF16)
    ib = b.S("l3f_ib", [128, KT, 256], BF16)
    gb = b.S("l3f_gb", [128, KT, 256], BF16)
    wkeys = lambda *nms: [f"l3f_{n}" for n in nms]
    gates = (("f", "wf", "uf", "bf", AF.Sigmoid, fb),
             ("i", "wi", "ui", "bi", AF.Sigmoid, ib),
             ("g", "wc", "uc", "bc", AF.Tanh, gb))
    for step in range(8):
        tsl = slice(step * 256, (step + 1) * 256)
        fw.op("vector",
              lambda e: e.tensor_copy(cb[:, :, :], cst[:, :, :]),
              reads=["l3f_c"], writes=["l3f_cbf"], tag=f"l3f_cb{step}")
        for gnm, wn, un, bn, af, dst in gates:
            wv, uv = wviews[wn], wviews[un]
            bcol = b[f"w_l3f_{bn}"]

            def gmm(e, wv=wv, uv=uv, tsl=tsl):
                last = None
                for o in range(KT):
                    osl = slice(o * 256, (o + 1) * 256)
                    for k in range(KT):
                        e.matmul(psA[:, osl],
                                 wv[:, k, o * 128:(o + 1) * 128],
                                 xln[:, k, tsl], start=(k == 0), stop=False)
                    for k in range(KT):
                        last = e.matmul(psA[:, osl],
                                        uv[:, k, o * 128:(o + 1) * 128],
                                        cb[:, k, :], start=False,
                                        stop=(k == KT - 1))
                return last
            fw.op("tensor", gmm,
                  reads=wkeys(wn, un) + ["l3f_cbf"] +
                        [f"xln:{k}" for k in range(KT)],
                  writes=_akeys(), tag=f"l3f_{gnm}mm{step}")

            def act(e, af=af, bcol=bcol, dst=dst):
                last = None
                for o in range(KT):
                    osl = slice(o * 256, (o + 1) * 256)
                    last = e.activation(dst[:, o, :], psA[:, osl], af,
                                        bias=bcol[:, o:o + 1])
                return last
            fw.op("scalar", act, reads=_akeys() + [f"w_l3f_{bn}"],
                  writes=[f"l3f_{gnm}b"], tag=f"l3f_{gnm}a{step}")
        fw.op("vector",
              lambda e: e.tensor_tensor(cst[:, :, :], cst[:, :, :],
                                        fb[:, :, :], OP.mult),
              reads=["l3f_c", "l3f_fb"], writes=["l3f_c"],
              tag=f"l3f_cf{step}")
        fw.op("vector",
              lambda e: e.tensor_tensor(ib[:, :, :], ib[:, :, :],
                                        gb[:, :, :], OP.mult),
              reads=["l3f_ib", "l3f_gb"], writes=["l3f_ib"],
              tag=f"l3f_ig{step}")
        fw.op("vector",
              lambda e: e.tensor_tensor(cst[:, :, :], cst[:, :, :],
                                        ib[:, :, :], OP.add),
              reads=["l3f_c", "l3f_ib"], writes=["l3f_c"],
              tag=f"l3f_cu{step}")
    fw.op("vector", lambda e: e.tensor_copy(cb[:, :, :], cst[:, :, :]),
          reads=["l3f_c"], writes=["l3f_cbf"], tag="l3f_cbF")
    wo_d = p.din("w_l3f_wo", [128, KT, 1024])
    uo_d = p.din("w_l3f_uo", [128, KT, 1024])
    wob = b.S("l3f_wo", [128, 2, KT, 128], BF16)
    uob = b.S("l3f_uo", [128, 2, KT, 128], BF16)
    up = b.S("l3f_up", [128, KT, 256], F32)
    hb = b.S("l3f_hb", [128, T], BF16)
    xi = p.x_src_ap
    xo = p.x_dst_ap
    bo = b["w_l3f_bo"]
    for o in range(KT):
        db = o % 2
        fw.op("gpsimd",
              lambda e, o=o, db=db: [
                  e.dma_start(out=wob[:, db],
                              in_=wo_d[:, :, o * 128:(o + 1) * 128]),
                  e.dma_start(out=uob[:, db],
                              in_=uo_d[:, :, o * 128:(o + 1) * 128])],
              writes=[f"l3f_wo:{db}", f"l3f_uo:{db}"], dma=True, n_dma=2,
              tag=f"l3f_ldo{o}")

        def umm(e, db=db):
            last = None
            for k in range(KT):
                last = e.matmul(psB[:, 0:256], uob[:, db, k, :],
                                cb[:, k, :], start=(k == 0),
                                stop=(k == KT - 1))
            return last
        fw.op("tensor", umm, reads=[f"l3f_uo:{db}", "l3f_cbf"],
              writes=["B0"], tag=f"l3f_umm{o}")
        fw.op("vector",
              lambda e, o=o: e.tensor_copy(up[:, o, :], psB[:, 0:256]),
              reads=["B0"], writes=["l3f_up"], tag=f"l3f_upc{o}")

        def omm(e, db=db):
            last = None
            for k in range(KT):
                for c in range(TC):
                    sl = slice(c * 512, (c + 1) * 512)
                    last = e.matmul(psA[:, sl], wob[:, db, k, :],
                                    xln[:, k, sl], start=(k == 0),
                                    stop=(k == KT - 1))
            return last
        fw.op("tensor", omm,
              reads=[f"l3f_wo:{db}"] + [f"xln:{k}" for k in range(KT)],
              writes=_akeys(), tag=f"l3f_omm{o}")
        fw.op("vector",
              lambda e, o=o: e.tensor_tensor(
                  hb[:, :].rearrange("p (s t) -> p s t", s=8),
                  psA[:, :].rearrange("p (s t) -> p s t", s=8),
                  up[:, o, :].unsqueeze(1).to_broadcast([128, 8, 256]),
                  OP.add),
              reads=_akeys() + ["l3f_up"], writes=["l3f_hb"],
              tag=f"l3f_hs{o}")
        fw.op("scalar",
              lambda e, o=o: e.activation(hb[:, :], hb[:, :], AF.Tanh,
                                          bias=bo[:, o:o + 1]),
              reads=["l3f_hb", "w_l3f_bo"], writes=["l3f_hb"],
              tag=f"l3f_ht{o}")
        xrow = b.S("l3f_xr", [128, 1, T], F32)
        if len(xi.shape) == 4:
            fw.op("sync",
                  lambda e, o=o: [e.dma_start(
                      out=xrow[:, 0, :].rearrange("p (r c) -> p r c", r=8),
                      in_=xi[:, o])],
                  writes=["l3f_xr"], dma=True, tag=f"l3f_ldx{o}")
        else:
            fw.op("sync",
                  lambda e, o=o: [e.dma_start(out=xrow[:, 0, :],
                                              in_=xi[:, o, :])],
                  writes=["l3f_xr"], dma=True, tag=f"l3f_ldx{o}")
        fw.op("vector",
              lambda e: e.tensor_tensor(xrow[:, 0, :], xrow[:, 0, :],
                                        hb[:, :], OP.add),
              reads=["l3f_xr", "l3f_hb"], writes=["l3f_xr"],
              tag=f"l3f_add{o}")
        fw.op("sync",
              lambda e, o=o: [e.dma_start(out=xo[:, o, :],
                                          in_=xrow[:, 0, :])],
              reads=["l3f_xr"], dma=True, tag=f"l3f_stx{o}")


def build_l2attn_block():
    p = Prog()
    p.load_consts()
    p.load_x()
    p.set_xio("x_in", "x_out")
    p.load("w_l2a_gam", [128, KT], dt=F32)
    p.load("w_l2a_bet", [128, KT], dt=F32)
    emit_ln(p, "ln2a", gamma="w_l2a_gam", beta="w_l2a_bet")
    emit_l2attn(p)
    return p.finish()


def build_l3ffn_block():
    p = Prog()
    p.load_consts()
    p.load_x()
    p.set_xio("x_in", "x_out")
    emit_ln(p, "ln3f")
    emit_l3ffn(p)
    return p.finish()


def host_prep_l2attn(inp):
    import math
    out = {}
    Wq, Wk, Wv = inp["L2_attn_Wq"], inp["L2_attn_Wk"], inp["L2_attn_Wv"]
    qkv = np.stack([lhsT_prep(Wq), lhsT_prep(Wk), lhsT_prep(Wv)],
                   axis=0)
    qkv = qkv.reshape(3, 128, KT, 16, 64)
    out["w_l2a_qkv"] = _f(qkv.transpose(3, 1, 0, 2, 4))
    Wpos = inp["L2_attn_Wpos"]
    pe_blocks = []
    for h in range(16):
        blk = Wpos[h * 64:(h + 1) * 64, :]
        pe_blocks.append(np.concatenate([blk[:, 0::2], blk[:, 1::2]],
                                        axis=1))
    out["w_l2a_pe"] = _f(np.stack(pe_blocks, axis=0))
    div = np.exp(np.arange(0, D, 2, dtype=np.float64) *
                 (-math.log(10000.0) / D))
    u = np.arange(T, dtype=np.float64)
    ang = u[None, :] * div[:, None]
    out["c_l2a_sin"] = _f(np.sin(ang).reshape(4, 128, T).transpose(1, 0, 2))
    out["c_l2a_cos"] = _f(np.cos(ang).reshape(4, 128, T).transpose(1, 0, 2))
    bq, bk, bv = inp["L2_attn_bq"], inp["L2_attn_bk"], inp["L2_attn_bv"]
    ub, vb = inp["L2_attn_u_bias"], inp["L2_attn_v_bias"]
    out["w_l2a_bqu"] = _f((bq.reshape(16, 64) + ub).T)
    out["w_l2a_bqv"] = _f((bq.reshape(16, 64) + vb).T)
    out["w_l2a_bk"] = _f(bk.reshape(16, 64).T)
    out["w_l2a_bv"] = _f(bv.reshape(16, 64).T)
    out["w_l2a_wout"] = lhsT_prep(inp["L2_attn_Wout"])
    out["w_l2a_bout"] = _f(inp["L2_attn_bout"].reshape(KT, 128).T)
    out["w_l2a_gam"] = _f(inp["L2_attn_ln_gamma"].reshape(KT, 128).T)
    out["w_l2a_bet"] = _f(inp["L2_attn_ln_beta"].reshape(KT, 128).T)
    out["c_negten"] = _f(np.full((128, 1), -10.0))
    return out


def emit_l2attn(p):
    fw, b = p.fw, p.b
    psA, psB = p.psA, p.psB
    xln = b["xln"]
    sbx = b["x"]
    lnscr = b["lnscr"]
    for nm in ("bqu", "bqv", "bk", "bv"):
        p.load(f"w_l2a_{nm}", [64, 16], dt=F32)
    p.load("w_l2a_bout", [128, KT], dt=F32)
    p.load("c_negten", [128, 1], dt=F32)
    qkv_d = p.din("w_l2a_qkv", [16, 128, 3, KT, 64])
    pe_d = p.din("w_l2a_pe", [16, 64, 1024])
    sin_d = p.din("c_l2a_sin", [128, 4, T])
    cos_d = p.din("c_l2a_cos", [128, 4, T])
    wout_d = p.din("w_l2a_wout", [128, KT, 1024])
    ctxstage = p.dscratch("l2a_ctx", [16, 64, T], BF16)
    onesrow = b["c_onesrow"]
    negten = b["c_negten"]

    lv = lnscr[:, :, :].rearrange("p k t -> p (k t)")
    SST = lv[:, 0:8192].rearrange("p (m t) -> p m t", m=4)
    CCT = lv[:, 8192:16384].rearrange("p (m t) -> p m t", m=4)
    fw.alias(["l2a_SS", "l2a_CC"], [f"lnscr:{k}" for k in range(KT)])
    fw.op("gpsimd", lambda e: [e.dma_start(out=SST, in_=sin_d)],
          reads=["l2a_SS"], writes=["l2a_SS"], dma=True, tag="l2a_ldS")
    fw.op("gpsimd", lambda e: [e.dma_start(out=CCT, in_=cos_d)],
          reads=["l2a_CC"], writes=["l2a_CC"], dma=True, tag="l2a_ldC")

    xv = sbx[:, :, :].bitcast(BF16).rearrange("p k t -> p (k t)")
    Ubuf = xv[:, 0:8192].rearrange("p (m t) -> p m t", m=4)
    Vbuf = xv[:, 8192:16384].rearrange("p (m t) -> p m t", m=4)
    gbuf = xv[:, 16384:18432]
    quB = xv[0:64, 18432:20480]
    qvB = xv[0:64, 20480:22528]
    kkB = xv[0:64, 22528:24576]
    vvB = xv[0:64, 24576:26624]
    ctxn = xv[0:64, 26624:28672]
    Vtok = xv[:, 28672:29824].rearrange("p (c d) -> p c d", c=16)
    at2 = xv[:, 29824:30848].rearrange("p (d t) -> p d t", d=2)
    tsc = b["rowsA"].ap().bitcast(BF16)[:, 0:2048]
    fw.alias(["l2a_carve"], [f"x:{k}" for k in range(KT)])
    fw.alias(["l2a_ts"], ["rowA0"])
    fw.op("vector", lambda e: e.memset(Vtok[:, :, 64:65], 1.0),
          reads=["l2a_carve"], writes=["l2a_vt1"], tag="l2a_vt1")

    wqkv = b.S("l2a_wqkv", [128, 2, 3, KT, 64], BF16)
    wpe = b.S("l2a_wpe", [64, 2, 1024], BF16)
    rsr = b["rowsB"][0:1, :]

    for h in range(16):
        db = h % 2
        fw.op("gpsimd",
              lambda e, h=h, db=db: [
                  e.dma_start(out=wqkv[:, db], in_=qkv_d[h]),
                  e.dma_start(out=wpe[:, db], in_=pe_d[h])],
              writes=[f"l2a_wqkv:{db}", f"l2a_wpe:{db}"], dma=True, n_dma=2,
              tag=f"l2a_ldw{h}")

        def qkvmm(e, which, db):
            last = None
            for k in range(KT):
                for c in range(TC):
                    sl = slice(c * 512, (c + 1) * 512)
                    last = e.matmul(psA[0:64, sl], wqkv[:, db, which, k, :],
                                    xln[:, k, sl],
                                    start=(k == 0), stop=(k == KT - 1))
            return last
        xkeys = [f"xln:{k}" for k in range(KT)]
        fw.op("tensor", lambda e, db=db: qkvmm(e, 0, db),
              reads=[f"l2a_wqkv:{db}"] + xkeys, writes=_akeys(),
              tag=f"l2a_qmm{h}")
        fw.op("scalar",
              lambda e, h=h: e.activation(quB, psA[0:64, :], AF.Identity,
                                          bias=b["w_l2a_bqu"][:, h:h + 1]),
              reads=_akeys() + ["w_l2a_bqu", "l2a_carve"], writes=["l2a_qu"],
              tag=f"l2a_qu{h}")
        fw.op("scalar",
              lambda e, h=h: e.activation(qvB, psA[0:64, :], AF.Identity,
                                          bias=b["w_l2a_bqv"][:, h:h + 1]),
              reads=_akeys() + ["w_l2a_bqv", "l2a_carve"], writes=["l2a_qv"],
              tag=f"l2a_qv{h}")
        fw.op("tensor", lambda e, db=db: qkvmm(e, 1, db),
              reads=[f"l2a_wqkv:{db}", "l2a_qu", "l2a_qv"] + xkeys,
              writes=_akeys(), tag=f"l2a_kmm{h}")
        fw.op("scalar",
              lambda e, h=h: e.activation(kkB, psA[0:64, :], AF.Identity,
                                          bias=b["w_l2a_bk"][:, h:h + 1]),
              reads=_akeys() + ["w_l2a_bk", "l2a_carve"], writes=["l2a_kk"],
              tag=f"l2a_kk{h}")
        fw.op("tensor", lambda e, db=db: qkvmm(e, 2, db),
              reads=[f"l2a_wqkv:{db}", "l2a_kk"] + xkeys,
              writes=_akeys(), tag=f"l2a_vmm{h}")
        fw.op("scalar",
              lambda e, h=h: e.activation(vvB, psA[0:64, :], AF.Identity,
                                          bias=b["w_l2a_bv"][:, h:h + 1]),
              reads=_akeys() + ["w_l2a_bv", "l2a_carve"], writes=["l2a_vv"],
              tag=f"l2a_vv{h}")
        pvb = psB.bitcast(BF16)

        def vtr(e):
            last = None
            for c in range(16):
                last = e.transpose(pvb[:, c * 64:(c + 1) * 64],
                                   vvB[:, c * 128:(c + 1) * 128],
                                   b["c_identb"][0:64, 0:64])
            return last
        fw.op("tensor", vtr, reads=["l2a_vv", "c_identb"], writes=["B0"],
              tag=f"l2a_vtr{h}")
        fw.op("vector",
              lambda e: e.tensor_copy(Vtok[:, :, 0:64],
                                      pvb[:, 0:1024]
                                      .rearrange("p (c d) -> p c d", c=16)),
              reads=["B0", "l2a_vt1"], writes=["l2a_vtok"],
              tag=f"l2a_vtc{h}")

        for m in range(4):
            def gemm(e, m=m, db=db, which=0):
                last = None
                for c in range(TC):
                    sl = slice(c * 512, (c + 1) * 512)
                    last = e.matmul(psA[:, sl],
                                    wpe[:, db, m * 128 + which * 512:
                                        m * 128 + which * 512 + 128],
                                    qvB[:, sl], start=True, stop=True)
                return last
            fw.op("tensor", lambda e, m=m, db=db: gemm(e, m, db, 0),
                  reads=[f"l2a_wpe:{db}", "l2a_qv"], writes=_akeys(),
                  tag=f"l2a_ge{h}_{m}")
            fw.op("vector",
                  lambda e: e.tensor_copy(gbuf, psA[:, :]),
                  reads=_akeys(), writes=["l2a_gb"], tag=f"l2a_gec{h}_{m}")
            fw.op("tensor", lambda e, m=m, db=db: gemm(e, m, db, 1),
                  reads=[f"l2a_wpe:{db}", "l2a_qv"], writes=_akeys(),
                  tag=f"l2a_go{h}_{m}")
            fw.op("vector",
                  lambda e, m=m: e.tensor_tensor(Ubuf[:, m, :], psA[:, :],
                                                 SST[:, m, :], OP.mult),
                  reads=_akeys() + ["l2a_SS"], writes=["l2a_U"],
                  tag=f"l2a_u1{h}_{m}")
            fw.op("vector",
                  lambda e, m=m: e.tensor_tensor(tsc[:, 0:T], gbuf,
                                                 CCT[:, m, :], OP.mult),
                  reads=["l2a_gb", "l2a_CC"], writes=["l2a_ts"],
                  tag=f"l2a_u2{h}_{m}")
            fw.op("vector",
                  lambda e, m=m: e.tensor_tensor(Ubuf[:, m, :],
                                                 Ubuf[:, m, :],
                                                 tsc[:, 0:T], OP.subtract),
                  reads=["l2a_U", "l2a_ts"], writes=["l2a_U"],
                  tag=f"l2a_u3{h}_{m}")
            fw.op("vector",
                  lambda e, m=m: e.tensor_tensor(Vbuf[:, m, :], psA[:, :],
                                                 CCT[:, m, :], OP.mult),
                  reads=_akeys() + ["l2a_CC"], writes=["l2a_V"],
                  tag=f"l2a_v1{h}_{m}")
            fw.op("vector",
                  lambda e, m=m: e.tensor_tensor(tsc[:, 0:T], gbuf,
                                                 SST[:, m, :], OP.mult),
                  reads=["l2a_gb", "l2a_SS"], writes=["l2a_ts"],
                  tag=f"l2a_v2{h}_{m}")
            fw.op("vector",
                  lambda e, m=m: e.tensor_tensor(Vbuf[:, m, :],
                                                 Vbuf[:, m, :],
                                                 tsc[:, 0:T], OP.add),
                  reads=["l2a_V", "l2a_ts"], writes=["l2a_V"],
                  tag=f"l2a_v3{h}_{m}")

        for tg in range(TC):
            tsl = slice(tg * 512, (tg + 1) * 512)
            avsl = slice(512, 1024)
            smsl = slice(1536, 2048)
            for jt in range(16):
                ssl = slice((jt % 2) * 512, (jt % 2) * 512 + 512)
                kjt = f"A{jt % 2}"

                def smm(e, jt=jt, ssl=ssl, tsl=tsl):
                    e.matmul(psA[:, ssl], kkB[:, jt * 128:(jt + 1) * 128],
                             quB[:, tsl], start=True, stop=False)
                    for m in range(4):
                        e.matmul(psA[:, ssl],
                                 SST[:, m, jt * 128:(jt + 1) * 128],
                                 Ubuf[:, m, tsl], start=False, stop=False)
                    for m in range(4):
                        last = e.matmul(psA[:, ssl],
                                        CCT[:, m, jt * 128:(jt + 1) * 128],
                                        Vbuf[:, m, tsl], start=False,
                                        stop=(m == 3))
                    return last
                fw.op("tensor", smm,
                      reads=["l2a_kk", "l2a_qu", "l2a_U", "l2a_V",
                             "l2a_SS", "l2a_CC"],
                      writes=[kjt], tag=f"l2a_s{h}_{tg}_{jt}")
                fw.op("scalar",
                      lambda e, jt=jt, ssl=ssl: e.activation(
                          at2[:, jt % 2, :], psA[:, ssl], AF.Exp,
                          bias=negten[:, :], scale=0.125),
                      reads=[kjt, "c_negten"], writes=[f"l2a_at:{jt % 2}"],
                      tag=f"l2a_e{h}_{tg}_{jt}")

                def avmm(e, jt=jt, avsl=avsl, smsl=smsl):
                    e.matmul(psB[0:65, avsl], Vtok[:, jt, 0:65],
                             at2[:, jt % 2, :], start=(jt == 0),
                             stop=(jt == 15))
                    return e.matmul(psB[0:1, smsl],
                                    b["c_onescolb"][:, :],
                                    at2[:, jt % 2, :], start=(jt == 0),
                                    stop=(jt == 15))
                fw.op("tensor", avmm,
                      reads=[f"l2a_at:{jt % 2}", "l2a_vtok", "c_onescolb"],
                      writes=["B1", "B3"], tag=f"l2a_av{h}_{tg}_{jt}")
            fw.op("vector",
                  lambda e: e.reciprocal(rsr[:, 0:512], psB[0:1, smsl]),
                  reads=["B3"], writes=["rowB0"], tag=f"l2a_rc{h}_{tg}")

            def rbc(e):
                return e.matmul(psA[0:64, 0:512], onesrow[0:1, 0:64],
                                rsr[:, 0:512], start=True, stop=True)
            fw.op("tensor", rbc, reads=["rowB0", "c_onesrow"],
                  writes=["A0"], tag=f"l2a_rb{h}_{tg}")
            fw.op("scalar",
                  lambda e: e.copy(tsc[0:64, 0:512], psB[0:64, avsl]),
                  reads=["B1"], writes=["l2a_ts"], tag=f"l2a_cc{h}_{tg}")
            fw.op("vector",
                  lambda e, tsl=tsl: e.tensor_tensor(
                      ctxn[:, tsl], tsc[0:64, 0:512], psA[0:64, 0:512],
                      OP.mult),
                  reads=["l2a_ts", "A0"], writes=["l2a_ctxn"],
                  tag=f"l2a_cn{h}_{tg}")
        fw.op("sync",
              lambda e, h=h: [e.dma_start(out=ctxstage[h], in_=ctxn)],
              reads=["l2a_ctxn"], writes=[f"l2a_cst:{h}"], dma=True,
              tag=f"l2a_stc{h}")

    ctxall = xln
    fw.alias(["l2a_call"], [f"xln:{k}" for k in range(KT)] +
             [f"l2a_cst:{h}" for h in range(16)])
    fw.op("gpsimd",
          lambda e: [e.dma_start(
              out=ctxall[:, :, :],
              in_=ctxstage.rearrange("(a c) d t -> (c d) a t", a=KT))],
          reads=["l2a_call"], writes=["l2a_call"], dma=True,
          tag="l2a_ldctx")
    wo = b.S("l2a_wo", [128, 2, KT, 128], BF16)
    xrow = b.S("l2a_xr", [128, 1, T], F32)
    hbuf = b.S("l2a_hbuf", [128, T], BF16)
    xi = p.x_src_ap
    xo = p.x_dst_ap
    bout = b["w_l2a_bout"]
    for o in range(KT):
        db = o % 2
        fw.op("gpsimd",
              lambda e, o=o, db=db: [e.dma_start(
                  out=wo[:, db], in_=wout_d[:, :, o * 128:(o + 1) * 128])],
              writes=[f"l2a_wo:{db}"], dma=True, tag=f"l2a_ldwo{o}")

        def omm(e, db=db):
            last = None
            for k in range(KT):
                for c in range(TC):
                    sl = slice(c * 512, (c + 1) * 512)
                    last = e.matmul(psA[:, sl], wo[:, db, k, :],
                                    ctxall[:, k, sl], start=(k == 0),
                                    stop=(k == KT - 1))
            return last
        fw.op("tensor", omm, reads=[f"l2a_wo:{db}", "l2a_call"],
              writes=_akeys(), tag=f"l2a_omm{o}")
        fw.op("sync",
              lambda e, o=o: [e.dma_start(out=xrow[:, 0, :],
                                          in_=xi[:, o, :])],
              writes=["l2a_xr"], dma=True, tag=f"l2a_ldx{o}")
        fw.op("scalar",
              lambda e, o=o: e.activation(hbuf[:, :], psA[:, :],
                                          AF.Identity,
                                          bias=bout[:, o:o + 1]),
              reads=_akeys() + ["w_l2a_bout"], writes=["l2a_hb"],
              tag=f"l2a_hb{o}")
        fw.op("vector",
              lambda e: e.tensor_tensor(xrow[:, 0, :], xrow[:, 0, :],
                                        hbuf[:, :], OP.add),
              reads=["l2a_xr", "l2a_hb"], writes=["l2a_xr"],
              tag=f"l2a_add{o}")
        fw.op("sync",
              lambda e, o=o: [e.dma_start(out=xo[:, o, :],
                                          in_=xrow[:, 0, :])],
              reads=["l2a_xr"], dma=True, tag=f"l2a_stx{o}")


def host_prep_l3attn(inp):
    out = {}
    Win = inp["L3_attn_W_in"]
    out["w_l3a_win"] = lhsT_prep(Win)
    cw = inp["L3_attn_conv_w"][:, 0, :]
    out["w_l3a_cwx"] = _f(cw[0:2048].reshape(16, 128, 4).transpose(1, 0, 2))
    out["w_l3a_cwB"] = _f(cw[2048:2064].reshape(16, 4))
    out["w_l3a_cwC"] = _f(cw[2064:2080].reshape(16, 4))
    cb = inp["L3_attn_conv_b"]
    out["w_l3a_cbx"] = _f(cb[0:2048].reshape(16, 128).T)
    out["w_l3a_cbB"] = _f(cb[2048:2064].reshape(16, 1))
    out["w_l3a_cbC"] = _f(cb[2064:2080].reshape(16, 1))
    out["w_l3a_dtb"] = _f(inp["L3_attn_dt_bias"].reshape(64, 1))
    out["w_l3a_alog"] = _f(inp["L3_attn_A_log"].reshape(64, 1))
    Dch = np.repeat(inp["L3_attn_D"], 32)
    out["w_l3a_Dch"] = _f(Dch.reshape(16, 128).T)
    nw = np.asarray(inp["L3_attn_norm_w"])
    out["w_l3a_nw"] = _f(nw.reshape(16, 128).T)
    out["w_l3a_wout"] = lhsT_prep(inp["L3_attn_W_out"])
    out["c_l3a_mask"] = _f(np.triu(np.ones((128, 128), np.float32)))
    return out


def emit_l3attn(p):
    fw, b = p.fw, p.b
    psA, psB = p.psA, p.psB
    xln = b["xln"]
    sbx = b["x"]
    NCH, Cc_ = 16, 128
    for nm, shp in (("cwx", [128, 16, 4]), ("cbx", [128, 16]),
                    ("cwB", [16, 4]), ("cwC", [16, 4]),
                    ("cbB", [16, 1]), ("cbC", [16, 1]),
                    ("dtb", [64, 1]), ("alog", [64, 1]),
                    ("Dch", [128, 16]), ("nw", [128, 16])):
        p.load(f"w_l3a_{nm}", shp, dt=F32)
    p.load("c_l3a_mask", [128, 128], dt=F32)
    win_d = p.din("w_l3a_win", [128, KT, 4192])
    wout_d = p.din("w_l3a_wout", [128, 16, 1024])
    zstage = p.dscratch("l3a_z", [16, 128, T], BF16)
    xcstage = p.dscratch("l3a_xcs", [16, 128, T], BF16)
    acstage = p.dscratch("l3a_acs", [64, T], F32)

    xrawstage = p.dscratch("l3a_xrs", [16, 128, T], BF16)
    Bf = b.S("l3a_B", [16, T], BF16)
    Cf = b.S("l3a_C", [16, T], BF16)
    dt_ = b.S("l3a_dt", [64, T], F32)
    Acum = b.S("l3a_Acum", [64, T], F32)
    AcT = b.S("l3a_AcT", [128, NCH, 64], F32)
    xlnB = b["xln"][:, :, :].rearrange("p k t -> p (k t)")
    xlnF = b["xln"][:, :, :].bitcast(F32).rearrange("p k t -> p (k t)")
    prevT = xlnF[0:16, 4608:6656]
    prevTb = xlnB[0:16, 7168:9216]
    nA = b.S("l3a_nA", [64, 1], F32)
    lnscr = b["lnscr"]
    lnF = lnscr[:, :, :].bitcast(F32).rearrange("p k t -> p (k t)")
    lnB = lnscr[:, :, :].rearrange("p k t -> p (k t)")
    accv = lnF[:, 0:T]
    fw.alias(["l3a_acc", "l3a_zw:0", "l3a_zw:1", "l3a_xcw:0",
              "l3a_xcw:1", "l3a_ygc", "l3a_sqc"],
             [f"lnscr:{k}" for k in range(KT)])
    AcumBC = sbx[:, 4:8, :].rearrange("p k t -> p (k t)").rearrange(
        "p (h i) -> p h i", h=64)
    xvb = sbx[:, 0:4, :].bitcast(BF16).rearrange("p k t -> p (k t)")
    Mbuf = xvb[:, 0:8192].rearrange("p (h i) -> p h i", h=64)
    CcE = xvb[0:16, 8192:16384].rearrange("p (h i) -> p h i", h=64)
    woutv = sbx[:, 4:8, :].bitcast(BF16).rearrange("p k t -> p (k t)") \
        [:, 0:16384].rearrange("p (k o) -> p k o", k=16)
    xcc = xlnB[:, 0:2048].rearrange("p (k t) -> p k t", k=16)
    xcT = xlnB[:, 2048:4096]
    zc = xlnB[:, 4096:6144]
    ygc = lnB[:, 12288:14336]
    sqc = lnB[:, 14336:16384]
    dtT = b.S("l3a_dtT", [128, 64], BF16)
    Btok = b.S("l3a_Bt", [128, 16], BF16)
    dterm = b.S("l3a_dl", [64, Cc_], F32)
    dlT = b.S("l3a_dlT", [128, 64], BF16)
    Bd = xlnB[:, 6144:7168]
    gcb = xlnB[0:16, 13312:15360]
    CBm = b.S("l3a_CBm", [128, Cc_], BF16)
    dtbfc = b.S("l3a_dtbf", [64, Cc_], BF16)
    dlbf = b.S("l3a_dlbf", [64, Cc_], BF16)
    Bbfc = b.S("l3a_Bbf", [16, Cc_], BF16)
    xrow = b.S("l3a_xr", [128, KT, Cc_], F32)
    ident, identb = b["c_ident"], b["c_identb"]
    xkeys = [f"xln:{k}" for k in range(KT)]

    wbuf = b.S("l3a_wb", [128, 2, KT, 128], BF16)
    zw = lnB[:, 4096:8192].rearrange("p (d t) -> p d t", d=2)
    ogroups = [(o, o * 128, 128) for o in range(32)] + \
              [(32, 4128, 64), (33, 4096, 16), (34, 4112, 16)]
    for oi, (o, col0, ncol) in enumerate(ogroups):
        db = oi % 2
        fw.op("gpsimd",
              lambda e, col0=col0, ncol=ncol, db=db: [e.dma_start(
                  out=wbuf[:, db, :, 0:ncol],
                  in_=win_d[:, :, col0:col0 + ncol])],
              writes=[f"l3a_wb:{db}"], dma=True, tag=f"l3a_ldw{oi}")

        def pmm(e, db=db, ncol=ncol):
            last = None
            for k in range(KT):
                for c in range(TC):
                    sl = slice(c * 512, (c + 1) * 512)
                    last = e.matmul(psA[0:ncol, sl],
                                    wbuf[:, db, k, 0:ncol], xln[:, k, sl],
                                    start=(k == 0), stop=(k == KT - 1))
            return last
        fw.op("tensor", pmm, reads=[f"l3a_wb:{db}"] + xkeys,
              writes=_akeys(), tag=f"l3a_pmm{oi}")
        if o < 16:
            fw.op("scalar",
                  lambda e, db=db: e.copy(zw[:, db, :], psA[:, :]),
                  reads=_akeys(), writes=[f"l3a_zw:{db}"],
                  tag=f"l3a_zc{oi}")
            fw.op("sync",
                  lambda e, o=o, db=db: [e.dma_start(out=zstage[o],
                                                     in_=zw[:, db, :])],
                  reads=[f"l3a_zw:{db}"], writes=[f"zst:{o}"], dma=True,
                  tag=f"l3a_zst{oi}")
        elif o < 32:
            fw.op("scalar",
                  lambda e, db=db: e.copy(zw[:, db, :], psA[:, :]),
                  reads=_akeys(), writes=[f"l3a_zw:{db}"],
                  tag=f"l3a_xrw{oi}")
            fw.op("sync",
                  lambda e, o=o, db=db: [e.dma_start(
                      out=xrawstage[o - 16], in_=zw[:, db, :])],
                  reads=[f"l3a_zw:{db}"], writes=[f"xrst:{o - 16}"],
                  dma=True, tag=f"l3a_xrst{oi}")
        elif o == 32:
            dtb = b["w_l3a_dtb"]
            fw.op("scalar",
                  lambda e: e.activation(dt_[:, :], psA[0:64, :],
                                         AF.Identity, bias=dtb[:, :]),
                  reads=_akeys() + ["w_l3a_dtb"], writes=["l3a_dt"],
                  tag="l3a_dtr")
            fw.op("scalar",
                  lambda e: e.activation(accv[0:64, :], dt_[:, :], AF.Abs),
                  reads=["l3a_dt"], writes=["l3a_acc"], tag="l3a_sp1")
            fw.op("scalar",
                  lambda e: e.activation(accv[0:64, :], accv[0:64, :],
                                         AF.Exp, scale=-1.0),
                  reads=["l3a_acc"], writes=["l3a_acc"], tag="l3a_sp2")
            fw.op("scalar",
                  lambda e: e.activation(accv[0:64, :], accv[0:64, :],
                                         AF.Ln,
                                         bias=b["c_onescol"][0:64, :]),
                  reads=["l3a_acc", "c_onescol"], writes=["l3a_acc"],
                  tag="l3a_sp3")
            fw.op("vector",
                  lambda e: e.tensor_scalar(dt_[:, :], dt_[:, :], 0.0,
                                            None, OP.max),
                  reads=["l3a_dt"], writes=["l3a_dt"], tag="l3a_sp4")
            fw.op("vector",
                  lambda e: e.tensor_tensor(dt_[:, :], dt_[:, :],
                                            accv[0:64, :], OP.add),
                  reads=["l3a_dt", "l3a_acc"], writes=["l3a_dt"],
                  tag="l3a_sp5")
        else:
            dst = Bf if o == 33 else Cf
            key = "l3a_B" if o == 33 else "l3a_C"
            fw.op("scalar",
                  lambda e, dst=dst: e.copy(dst[:, :], psA[0:16, :]),
                  reads=_akeys(), writes=[key], tag=f"l3a_bc{oi}")

    fw.alias(["l3a_xcc", "l3a_xcT", "l3a_zc", "l3a_Bd", "l3a_pv",
              "l3a_pvb"], [f"xln:{k}" for k in range(KT)])

    fw.op("scalar", lambda e: e.activation(nA[:, :], b["w_l3a_alog"][:, :],
                                           AF.Exp),
          reads=["w_l3a_alog"], writes=["l3a_nA"], tag="l3a_eA")
    fw.op("vector", lambda e: e.tensor_scalar(nA[:, :], nA[:, :], -1.0,
                                              None, OP.mult),
          reads=["l3a_nA"], writes=["l3a_nA"], tag="l3a_negA")
    fw.op("vector",
          lambda e: e.tensor_scalar(accv[0:64, :], dt_[:, :], nA[:, :],
                                    None, OP.mult),
          reads=["l3a_dt", "l3a_nA"], writes=["l3a_acc"], tag="l3a_dtA")

    def scans(e):
        last = None
        for c in range(NCH):
            sl = slice(c * Cc_, (c + 1) * Cc_)
            last = e.tensor_tensor_scan(Acum[:, sl], accv[0:64, sl],
                                        accv[0:64, sl], 0.0, OP.add,
                                        OP.bypass)
        return last
    fw.op("vector", scans, reads=["l3a_acc"], writes=["l3a_Acum"],
          tag="l3a_scan")
    fw.op("sync", lambda e: [e.dma_start(out=acstage, in_=Acum[:, :])],
          reads=["l3a_Acum"], writes=["acst"], dma=True, tag="l3a_acst")
    for c in range(NCH):
        fw.op("tensor",
              lambda e, c=c: e.transpose(psB[0:128, 0:64],
                                         Acum[:, c * Cc_:(c + 1) * Cc_],
                                         ident[0:64, 0:64]),
              reads=["l3a_Acum", "c_ident"], writes=["B0"],
              tag=f"l3a_actr{c}")
        fw.op("vector",
              lambda e, c=c: e.tensor_copy(AcT[:, c, :], psB[:, 0:64]),
              reads=["B0"], writes=["l3a_AcT"], tag=f"l3a_actc{c}")

    cwx, cbx = b["w_l3a_cwx"], b["w_l3a_cbx"]
    xcw = lnB[:, 8192:12288].rearrange("p (d t) -> p d t", d=2)
    xrb = zw
    for k in range(16):
        db = k % 2
        fw.op("gpsimd",
              lambda e, k=k, db=db: [e.dma_start(out=xrb[:, db, :],
                                                 in_=xrawstage[k])],
              reads=[f"xrst:{k}"], writes=[f"l3a_zw:{db}"], dma=True,
              tag=f"l3a_ldxr{k}")

        def conv(e, k=k, db=db):
            e.tensor_scalar(accv[:, :], xrb[:, db, :], cwx[:, k, 3:4],
                            None, OP.mult)
            e.drain()
            for tau in range(3):
                off = 3 - tau
                e.scalar_tensor_tensor(
                    accv[:, off:T], xrb[:, db, 0:T - off],
                    cwx[:, k, tau:tau + 1], accv[:, off:T],
                    OP.mult, OP.add)
                e.drain()
            return e.engine_nop()
        fw.op("vector", conv,
              reads=[f"l3a_zw:{db}", "w_l3a_cwx"], writes=["l3a_acc"],
              tag=f"l3a_cv{k}")
        fw.op("scalar",
              lambda e, k=k, db=db: e.activation(xcw[:, db, :], accv[:, :],
                                                 AF.Silu,
                                                 bias=cbx[:, k:k + 1]),
              reads=["l3a_acc", "w_l3a_cbx"], writes=[f"l3a_xcw:{db}"],
              tag=f"l3a_cs{k}")
        fw.op("sync",
              lambda e, k=k, db=db: [e.dma_start(out=xcstage[k],
                                                 in_=xcw[:, db, :])],
              reads=[f"l3a_xcw:{db}"], writes=[f"xcst:{k}"], dma=True,
              tag=f"l3a_xst{k}")
    for which, buf, key in ((0, Bf, "l3a_B"), (1, Cf, "l3a_C")):
        cwW = b["w_l3a_cwB" if which == 0 else "w_l3a_cwC"]
        cbW = b["w_l3a_cbB" if which == 0 else "w_l3a_cbC"]
        cwk = "w_l3a_cwB" if which == 0 else "w_l3a_cwC"
        cbk = "w_l3a_cbB" if which == 0 else "w_l3a_cbC"

        def convbc(e, buf=buf, cwW=cwW):
            e.tensor_scalar(accv[0:16, :], buf[:, :], cwW[:, 3:4],
                            None, OP.mult)
            e.drain()
            for tau in range(3):
                off = 3 - tau
                e.scalar_tensor_tensor(
                    accv[0:16, off:T], buf[:, 0:T - off],
                    cwW[:, tau:tau + 1], accv[0:16, off:T],
                    OP.mult, OP.add)
                e.drain()
            return e.engine_nop()
        fw.op("vector", convbc, reads=[key, cwk],
              writes=["l3a_acc"], tag=f"l3a_cvbc{which}")
        fw.op("scalar",
              lambda e, buf=buf, cbW=cbW: e.activation(
                  buf[:, :], accv[0:16, :], AF.Silu, bias=cbW[:, :]),
              reads=["l3a_acc", cbk], writes=[key],
              tag=f"l3a_csbc{which}")

    fw.op("vector", lambda e: e.memset(prevT[:, :], 0.0),
          writes=["l3a_pv"], tag="l3a_pv0")

    maskc = b["c_l3a_mask"]
    Dch, nw = b["w_l3a_Dch"], b["w_l3a_nw"]
    onesrow = b["c_onesrow"]
    onescolb = b["c_onescolb"]
    rsr = b["rowsB"][0:1, :]
    xi = p.x_src_ap
    xo = p.x_dst_ap
    wov = b.S("l3a_wov", [128, 2, 16, 128], BF16)
    prestaged = "l3a_wobf" in p.dram
    wo_bfst = p.dscratch("l3a_wobf", [128, 16, 1024], BF16)
    if not prestaged:
        fw.op("gpsimd",
              lambda e: [e.dma_start(out=wo_bfst, in_=wout_d)],
              writes=["l3a_wobf"], dma=True, tag="l3a_stgwo")

    def emit_tail(c):
        fw.op("sync",
              lambda e, c=c: [e.dma_start(
                  out=xrow[:],
                  in_=xi[:, :, c * Cc_:(c + 1) * Cc_])],
              writes=["l3a_xr"], dma=True, tag=f"l3a_ldx{c}")
        fw.op("vector",
              lambda e: e.tensor_tensor(
                  xrow[:].rearrange("p k t -> p (k t)"),
                  xrow[:].rearrange("p k t -> p (k t)"),
                  psB[:, 1024:2048], OP.add),
              reads=["l3a_xr", "B2", "B3"], writes=["l3a_xr"],
              tag=f"l3a_xa{c}")
        if len(xo.shape) == 4:
            fw.op("sync",
                  lambda e, c=c: [e.dma_start(
                      out=xo[:, :, c // 2,
                             (c % 2) * Cc_:(c % 2) * Cc_ + Cc_],
                      in_=xrow[:])],
                  reads=["l3a_xr"], dma=True, tag=f"l3a_stx{c}")
        else:
            fw.op("sync",
                  lambda e, c=c: [e.dma_start(
                      out=xo[:, :, c * Cc_:(c + 1) * Cc_], in_=xrow[:])],
                  reads=["l3a_xr"], dma=True, tag=f"l3a_stx{c}")

    def emit_ldab(c):
        if c == 0:
            fw.alias(["l3a_abc", "l3a_abc2"],
                     [f"x:{k}" for k in range(KT)])
        fw.op("sync",
              lambda e, c=c: [e.dma_start(
                  out=AcumBC[0:64],
                  in_=acstage[:, c * Cc_:(c + 1) * Cc_].unsqueeze(0)
                  .to_broadcast([64, 64, Cc_]))],
              reads=["acst"], writes=["l3a_abc"], dma=True,
              tag=f"l3a_ldab{c}")
        fw.op("gpsimd",
              lambda e, c=c: [e.dma_start(
                  out=AcumBC[64:128],
                  in_=acstage[:, c * Cc_:(c + 1) * Cc_].unsqueeze(0)
                  .to_broadcast([64, 64, Cc_]))],
              reads=["acst"], writes=["l3a_abc2"], dma=True,
              tag=f"l3a_ldab2_{c}")

    emit_ldab(0)
    for c in range(NCH):
        csl = slice(c * Cc_, (c + 1) * Cc_)
        fw.op("gpsimd",
              lambda e, c=c: [e.dma_start(
                  out=xcc[:],
                  in_=xcstage[:, :, c * Cc_:(c + 1) * Cc_]
                  .rearrange("k p t -> p k t"))],
              reads=[f"xcst:{k}" for k in range(16)], writes=["l3a_xcc"],
              dma=True, tag=f"l3a_ldxc{c}")
        fw.op("gpsimd",
              lambda e, c=c: [e.dma_start(
                  out=zc[:].rearrange("p (k t) -> p k t", k=16),
                  in_=zstage[:, :, c * Cc_:(c + 1) * Cc_]
                  .rearrange("k p t -> p k t"))],
              reads=[f"zst:{k}" for k in range(16)], writes=["l3a_zc"],
              dma=True, tag=f"l3a_ldz{c}")
        fw.op("scalar",
              lambda e: e.activation(
                  gcb[0:16, 0:64],
                  AcumBC[0:16, :, Cc_ - 1], AF.Exp),
              reads=["l3a_abc"], writes=["l3a_gcb"], tag=f"l3a_gcb{c}")
        fw.op("scalar",
              lambda e: e.activation(CcE[:, :, :], AcumBC[0:16, :, :],
                                     AF.Exp),
              reads=["l3a_abc"], writes=["l3a_CcE"], tag=f"l3a_cce1{c}")
        fw.op("vector",
              lambda e, csl=csl: e.tensor_tensor(
                  CcE[:, :, :], CcE[:, :, :],
                  Cf[:, csl].unsqueeze(1).to_broadcast([16, 64, Cc_]),
                  OP.mult),
              reads=["l3a_CcE", "l3a_C"], writes=["l3a_CcE"],
              tag=f"l3a_cce2{c}")
        fw.op("tensor",
              lambda e, csl=csl: e.matmul(psB[:, 0:Cc_], Bf[:, csl],
                                          Cf[:, csl], start=True,
                                          stop=True),
              reads=["l3a_B", "l3a_C"], writes=["B0"], tag=f"l3a_cb{c}")
        fw.op("vector",
              lambda e: e.tensor_tensor(CBm[:, :], maskc[:, :],
                                        psB[:, 0:Cc_], OP.mult),
              reads=["c_l3a_mask", "B0"], writes=["l3a_CBm"],
              tag=f"l3a_cbm{c}")
        fw.op("vector",
              lambda e, c=c: e.scalar_tensor_tensor(
                  AcumBC[:, :, :],
                  AcT[:, c, :].unsqueeze(2).to_broadcast([128, 64, Cc_]),
                  -1.0, AcumBC[:, :, :], OP.mult, OP.add),
              reads=["l3a_AcT", "l3a_abc", "l3a_abc2"],
              writes=["l3a_abc", "l3a_abc2"],
              tag=f"l3a_seg{c}")
        fw.op("vector",
              lambda e: e.tensor_scalar(AcumBC[:, :, :], AcumBC[:, :, :],
                                        0.0, None, OP.min),
              reads=["l3a_abc", "l3a_abc2"],
              writes=["l3a_abc", "l3a_abc2"], tag=f"l3a_cl{c}")
        fw.op("scalar",
              lambda e: e.activation(Mbuf[:, :, :], AcumBC[:, :, :],
                                     AF.Exp),
              reads=["l3a_abc", "l3a_abc2"], writes=["l3a_M"],
              tag=f"l3a_me{c}")
        if c + 1 < NCH:
            emit_ldab(c + 1)
        fw.op("vector",
              lambda e: e.tensor_tensor(
                  Mbuf[:, :, :], Mbuf[:, :, :],
                  CBm[:, :].unsqueeze(1).to_broadcast([128, 64, Cc_]),
                  OP.mult),
              reads=["l3a_M", "l3a_CBm"], writes=["l3a_M"],
              tag=f"l3a_mc{c}")
        def xtr2(e):
            last = None
            pab = psA.bitcast(BF16)
            for k in range(16):
                last = e.transpose(pab[:, k * 128:(k + 1) * 128],
                                   xcc[:, k, :], identb[:, :])
            return last
        fw.op("tensor", xtr2, reads=["l3a_xcc", "c_identb"],
              writes=["A0", "A1"], tag=f"l3a_xtr{c}")
        fw.op("vector",
              lambda e: e.tensor_copy(xcT[:, :],
                                      psA.bitcast(BF16)[:, 0:2048]),
              reads=["A0", "A1"], writes=["l3a_xcT"], tag=f"l3a_xtc{c}")
        fw.op("vector",
              lambda e, csl=csl: e.tensor_copy(dtbfc[:, :], dt_[:, csl]),
              reads=["l3a_dt"], writes=["l3a_dtbf"], tag=f"l3a_dtb{c}")
        fw.op("scalar",
              lambda e, c=c, csl=csl: e.activation(
                  dterm[:, :], Acum[:, csl], AF.Exp, scale=-1.0,
                  bias=Acum[:, c * Cc_ + Cc_ - 1:c * Cc_ + Cc_]),
              reads=["l3a_Acum"], writes=["l3a_dl"], tag=f"l3a_dle{c}")
        fw.op("vector", lambda e: e.tensor_copy(dlbf[:, :], dterm[:, :]),
              reads=["l3a_dl"], writes=["l3a_dlbf"], tag=f"l3a_dlb{c}")
        fw.op("vector",
              lambda e, csl=csl: e.tensor_copy(Bbfc[:, :], Bf[:, csl]),
              reads=["l3a_B"], writes=["l3a_Bbf"], tag=f"l3a_bbf{c}")
        pbb = psB.bitcast(BF16)

        def smalltr(e):
            e.transpose(pbb[:, 1024:1088], dtbfc[:, :], identb[0:64, 0:64])
            e.transpose(pbb[:, 1088:1152], dlbf[:, :], identb[0:64, 0:64])
            return e.transpose(pbb[:, 1152:1168], Bbfc[:, :],
                               identb[0:16, 0:16])
        fw.op("tensor", smalltr,
              reads=["l3a_dtbf", "l3a_dlbf", "l3a_Bbf", "c_identb"],
              writes=["B1"], tag=f"l3a_str{c}")
        fw.op("vector",
              lambda e: e.tensor_copy(dtT[:, :], pbb[:, 1024:1088]),
              reads=["B1"], writes=["l3a_dtT"], tag=f"l3a_dttc{c}")
        fw.op("vector",
              lambda e: e.tensor_copy(dlT[:, :], pbb[:, 1088:1152]),
              reads=["B1"], writes=["l3a_dlT"], tag=f"l3a_dltc{c}")
        fw.op("vector",
              lambda e: e.tensor_copy(Btok[:, :], pbb[:, 1152:1168]),
              reads=["B1"], writes=["l3a_Btok"], tag=f"l3a_btc{c}")
        fw.op("vector",
              lambda e: e.tensor_tensor(
                  xcT[:, :].rearrange("p (h q) -> p h q", h=64),
                  xcT[:, :].rearrange("p (h q) -> p h q", h=64),
                  dtT[:, :].unsqueeze(2).to_broadcast([128, 64, 32]),
                  OP.mult),
              reads=["l3a_xcT", "l3a_dtT"], writes=["l3a_xcT"],
              tag=f"l3a_xdt{c}")
        fw.op("vector",
              lambda e: e.tensor_tensor(
                  Bd[:, :].rearrange("p (h n) -> p h n", h=64),
                  Btok[:, :].unsqueeze(1).to_broadcast([128, 64, 16]),
                  dlT[:, :].unsqueeze(2).to_broadcast([128, 64, 16]),
                  OP.mult),
              reads=["l3a_Btok", "l3a_dlT"], writes=["l3a_Bd"],
              tag=f"l3a_bd{c}")
        fw.op("vector", lambda e: e.tensor_copy(prevTb[:, :], prevT[:, :]),
              reads=["l3a_pv"], writes=["l3a_pvb"], tag=f"l3a_pvb{c}")
        def ymm(e):
            last = None
            for g in range(16):
                gs = slice(g * 128, (g + 1) * 128)
                for j in range(4):
                    h = g * 4 + j
                    tp = (0, 96) if j == 3 else None
                    e.matmul(psA[32 * j:32 * j + 32, gs],
                             xcT[:, h * 32:(h + 1) * 32],
                             Mbuf[:, h, :], start=True, stop=False,
                             tile_position=tp)
                    last = e.matmul(psA[32 * j:32 * j + 32, gs],
                                    prevTb[:, h * 32:(h + 1) * 32],
                                    CcE[:, h, :], start=False, stop=True,
                                    tile_position=tp)
            return last
        fw.op("tensor", ymm,
              reads=["l3a_xcT", "l3a_M", "l3a_pvb", "l3a_CcE"],
              writes=_akeys(), tag=f"l3a_ymm{c}")
        def stmm(e):
            last = None
            for h in range(64):
                last = e.matmul(psB[0:16, h * 32:(h + 1) * 32],
                                Bd[:, h * 16:(h + 1) * 16],
                                xcT[:, h * 32:(h + 1) * 32],
                                start=True, stop=True)
            return last
        fw.op("tensor", stmm, reads=["l3a_Bd", "l3a_xcT"],
              writes=_bkeys(), tag=f"l3a_st{c}")
        fw.op("scalar", lambda e: e.activation(zc[:, :], zc[:, :], AF.Silu),
              reads=["l3a_zc"], writes=["l3a_zc"], tag=f"l3a_zs{c}")
        fw.op("vector",
              lambda e: e.tensor_tensor(
                  ygc[:, :].rearrange("p (k t) -> p k t", k=16),
                  xcc[:, :, :],
                  Dch[:, :].unsqueeze(2).to_broadcast([128, 16, Cc_]),
                  OP.mult),
              reads=["l3a_xcc", "w_l3a_Dch"], writes=["l3a_ygc"],
              tag=f"l3a_yd{c}")
        fw.op("vector",
              lambda e: e.tensor_tensor(ygc[:, :], ygc[:, :], psA[:, :],
                                        OP.add),
              reads=["l3a_ygc"] + _akeys(), writes=["l3a_ygc"],
              tag=f"l3a_ya{c}")
        fw.op("vector",
              lambda e: e.tensor_tensor(ygc[:, :], ygc[:, :], zc[:, :],
                                        OP.mult),
              reads=["l3a_ygc", "l3a_zc"], writes=["l3a_ygc"],
              tag=f"l3a_yz{c}")
        fw.op("vector",
              lambda e: e.tensor_tensor(
                  prevT[:, :].rearrange("p (h q) -> p h q", h=64),
                  prevT[:, :].rearrange("p (h q) -> p h q", h=64),
                  gcb[0:16, 0:64].unsqueeze(2).to_broadcast([16, 64, 32]),
                  OP.mult),
              reads=["l3a_pv", "l3a_gcb"], writes=["l3a_pv"],
              tag=f"l3a_pu1{c}")
        fw.op("vector",
              lambda e: e.tensor_tensor(prevT[:, :], prevT[:, :],
                                        psB[0:16, :], OP.add),
              reads=["l3a_pv"] + _bkeys(), writes=["l3a_pv"],
              tag=f"l3a_pu2{c}")
        fw.op("scalar", lambda e: e.activation(sqc[:, :], ygc[:, :],
                                               AF.Square),
              reads=["l3a_ygc"], writes=["l3a_sqc"], tag=f"l3a_sq{c}")

        def ssum(e):
            last = None
            for k in range(16):
                last = e.matmul(psB[0:1, 0:Cc_], onescolb[:, :],
                                sqc[:, k * 128:(k + 1) * 128],
                                start=(k == 0), stop=(k == 15))
            return last
        fw.op("tensor", ssum, reads=["l3a_sqc", "c_onescolb"],
              writes=["B0"], tag=f"l3a_ss{c}")
        fw.op("vector",
              lambda e: e.tensor_scalar(rsr[:, 0:Cc_], psB[0:1, 0:Cc_],
                                        1.0 / 2048.0, 1e-5, OP.mult,
                                        OP.add),
              reads=["B0"], writes=["rowB0"], tag=f"l3a_ms{c}")
        fw.op("scalar",
              lambda e: e.activation(rsr[:, 0:Cc_], rsr[:, 0:Cc_],
                                     AF.Ln),
              reads=["rowB0"], writes=["rowB0"], tag=f"l3a_msq{c}")
        fw.op("scalar",
              lambda e: e.activation(rsr[:, 0:Cc_], rsr[:, 0:Cc_],
                                     AF.Exp, scale=-0.5),
              reads=["rowB0"], writes=["rowB0"], tag=f"l3a_mr{c}")
        fw.op("tensor",
              lambda e: e.matmul(psA[:, 0:Cc_], onesrow[:, :],
                                 rsr[:, 0:Cc_], start=True, stop=True),
              reads=["rowB0", "c_onesrow"], writes=["A0"],
              tag=f"l3a_mb{c}")
        fw.op("vector",
              lambda e: e.tensor_tensor(
                  ygc[:, :].rearrange("p (k t) -> p k t", k=16),
                  ygc[:, :].rearrange("p (k t) -> p k t", k=16),
                  psA[:, 0:Cc_].unsqueeze(1).to_broadcast([128, 16, Cc_]),
                  OP.mult),
              reads=["l3a_ygc", "A0"], writes=["l3a_ygc"],
              tag=f"l3a_yn{c}")
        fw.op("vector",
              lambda e: e.tensor_tensor(
                  ygc[:, :].rearrange("p (k t) -> p k t", k=16),
                  ygc[:, :].rearrange("p (k t) -> p k t", k=16),
                  nw[:, :].unsqueeze(2).to_broadcast([128, 16, Cc_]),
                  OP.mult),
              reads=["l3a_ygc", "w_l3a_nw"], writes=["l3a_ygc"],
              tag=f"l3a_yw{c}")
        for o in range(KT):
            db = o % 2
            fw.op("gpsimd",
                  lambda e, o=o, db=db: [e.dma_start(
                      out=wov[:, db],
                      in_=wo_bfst[:, :, o * 128:(o + 1) * 128])],
                  reads=["l3a_wobf"], writes=[f"l3a_wov:{db}"], dma=True,
                  tag=f"l3a_ldwo{c}_{o}")

            def omm(e, o=o, db=db):
                last = None
                osl = slice(1024 + o * 128, 1024 + (o + 1) * 128)
                for k in range(16):
                    last = e.matmul(psB[:, osl],
                                    wov[:, db, k, :],
                                    ygc[:, k * 128:(k + 1) * 128],
                                    start=(k == 0), stop=(k == 15))
                return last
            fw.op("tensor", omm, reads=[f"l3a_wov:{db}", "l3a_ygc"],
                  writes=["B2" if o < 4 else "B3"],
                  tag=f"l3a_omm{c}_{o}")
        emit_tail(c)


def build_l3attn_block():
    p = Prog()
    p.load_consts()
    p.load_x()
    p.set_xio("x_in", "x_out")
    emit_ln(p, "ln3a")
    emit_l3attn(p)
    return p.finish()


def host_prep_l0attn(inp):
    out = {}
    Win = inp["L0_attn_W_in"]
    out["w_l0a_win"] = _f(Win.reshape(KT, 128, 4096).transpose(1, 0, 2))
    Wba = inp["L0_attn_W_ba"]
    Wba48 = np.zeros((1024, 48), np.float32)
    Wba48[:, 0:16] = Wba[:, 0:16]
    Wba48[:, 32:48] = Wba[:, 16:32]
    out["w_l0a_wba"] = _f(Wba48.reshape(KT, 128, 48).transpose(1, 0, 2))
    cw = inp["L0_attn_conv_w"][:, 0, :]
    out["w_l0a_cw"] = _f(cw.reshape(24, 128, 4).transpose(1, 0, 2))
    out["w_l0a_dtb"] = _f(inp["L0_attn_dt_bias"].reshape(16, 1))
    out["w_l0a_alog"] = _f(inp["L0_attn_A_log"].reshape(16, 1))
    out["w_l0a_nw"] = _f(inp["L0_attn_norm_w"].reshape(1, 64))
    out["w_l0a_wo"] = _f(inp["L0_attn_W_o"].reshape(KT, 128, 1024)
                         .transpose(1, 0, 2))
    sl_ = np.tril(np.ones((128, 128), np.float32), -1)
    out["c_l0a_msl"] = _f(sl_)
    out["c_l0a_msu"] = _f(sl_.T)
    out["c_l0a_mli"] = _f(np.triu(np.ones((128, 128), np.float32)))
    out["c_sel2"] = _f(np.kron(np.eye(2), np.ones((1, 64))))
    out["c_sel2T"] = _f(out["c_sel2"].T)
    return out


def emit_l0attn(p):
    fw, b = p.fw, p.b
    psA, psB = p.psA, p.psB
    xln = b["xln"]
    sbx = b["x"]
    NCH, C_ = 16, 128
    for nm, shp in (("cw", [128, 24, 4]), ("dtb", [16, 1]),
                    ("alog", [16, 1])):
        p.load(f"w_l0a_{nm}", shp, dt=F32)
    p.load("c_l0a_msl", [128, 128], dt=BF16, queue="gpsimd")
    p.load("c_l0a_msu", [128, 128], dt=BF16, queue="gpsimd")
    p.load("c_l0a_mli", [128, 128], dt=BF16, queue="gpsimd")
    p.load("c_sel2", [2, 128], dt=F32)
    p.load("c_sel2T", [128, 2], dt=BF16, queue="gpsimd")
    p.load("w_l0a_wo", [128, KT, 1024])
    win_d = p.din("w_l0a_win", [128, KT, 4096])
    wba_d = p.din("w_l0a_wba", [128, KT, 48])
    nw_d = p.din("w_l0a_nw", [1, 64])
    qkvstage = p.dscratch("l0a_qkv", [48, 64, T], BF16)
    zstage = p.dscratch("l0a_zst", [16, 64, T], BF16)
    cgstage = p.dscratch("l0a_cgs", [16, T], F32)
    ustage = p.dscratch("l0a_ust", [16, T], F32)
    rbstage = p.dscratch("l0a_rbs", [3, 16, T], BF16)

    cumg = b.S("l0a_cgr", [16, T], F32)
    urow = b.S("l0a_ur", [16, T], F32)
    betr = b.S("l0a_bet", [16, T], BF16)
    delr = b.S("l0a_del", [16, T], BF16)
    nA = b.S("l0a_nA", [16, 1], F32)
    nwbc = b.S("l0a_nwb", [128, 64], BF16)
    S0 = b.S("l0a_S0", [64, 1024], F32)
    S0b = b.S("l0a_S0b", [64, 1024], BF16)
    lnscr = b["lnscr"]
    lnF = lnscr[:, :, :].bitcast(F32).rearrange("p k t -> p (k t)")
    lnB = lnscr[:, :, :].rearrange("p k t -> p (k t)")
    accv = lnF[:, 0:T]
    zw = lnB[:, 4096:8192].rearrange("p (d t) -> p d t", d=2)
    xcw = lnB[:, 8192:12288].rearrange("p (d t) -> p d t", d=2)
    sqw = lnB[:, 12288:14336]
    fw.alias(["l0a_acc", "l0a_zw:0", "l0a_zw:1", "l0a_xcw:0", "l0a_xcw:1",
              "l0a_sqw"], [f"lnscr:{k}" for k in range(KT)])
    xF = sbx[:, :, :].rearrange("p k t -> p (k t)")
    cgBC = xF[:, 0:2048].rearrange("p (h i) -> p h i", h=16)
    uBC = xF[:, 2048:4096].rearrange("p (h i) -> p h i", h=16)
    xB = sbx[:, :, :].bitcast(BF16).rearrange("p k t -> p (k t)")
    ofs = [8192]

    def carve(n):
        r = xB[:, ofs[0]:ofs[0] + n]
        ofs[0] += n
        return r
    Gm = carve(2048)
    Mm = carve(2048)
    Pm = carve(2048)
    nKKl = carve(2048)
    nKKu = carve(2048)
    QKm = carve(2048)
    SCt = carve(2048)
    EG = carve(2048)
    Wtok = carve(1024)
    t2n = carve(1024)
    Oe = carve(1024)
    ztk = carve(1024)
    yF = carve(1024)
    sq2 = carve(1024)
    assert ofs[0] <= 32768, ofs
    xlnB = b["xln"][:, :, :].rearrange("p k t -> p (k t)")
    qc = xlnB[0:64, 0:2048].rearrange("p (h t) -> p h t", h=16)
    kc = xlnB[0:64, 2048:4096].rearrange("p (h t) -> p h t", h=16)
    vc = xlnB[0:64, 4096:6144].rearrange("p (h t) -> p h t", h=16)
    euBC = xlnB[0:64, 6144:8192].rearrange("p (h t) -> p h t", h=16)
    gaBC = xlnB[0:64, 8192:10240].rearrange("p (h t) -> p h t", h=16)
    deBC = xlnB[0:64, 10240:12288].rearrange("p (h t) -> p h t", h=16)
    Vtok2 = xlnB[:, 12288:13312]
    Ktok2 = xlnB[:, 13312:14336]
    cgT = b.S("l0a_cgT", [128, 16], F32)
    uT = b.S("l0a_uT", [128, 16], F32)
    beT = b.S("l0a_beT", [128, 16], BF16)
    deT = b.S("l0a_deT", [128, 16], BF16)
    rr = b.S("l0a_rr", [128, 16], F32)
    dl_bf = b.S("l0a_dbf", [16, C_], BF16)
    xrow = b.S("l0a_xr", [128, KT, C_], F32)
    wbuf = b.S("l0a_wb", [128, 2, KT, 128], BF16)
    ident, identb = b["c_ident"], b["c_identb"]
    msl, msu, mli = b["c_l0a_msl"], b["c_l0a_msu"], b["c_l0a_mli"]
    sel2, sel2T = b["c_sel2"], b["c_sel2T"]
    xkeys = [f"xln:{k}" for k in range(KT)]
    fw.alias(["l0a_carve"], [f"x:{k}" for k in range(KT)])

    fw.op("gpsimd",
          lambda e: [e.dma_start(out=nwbc[:],
                                 in_=nw_d.to_broadcast([128, 64]))],
          writes=["l0a_nwbc"], dma=True, tag="l0a_ldnw")

    for o in range(32):
        db = o % 2
        fw.op("gpsimd",
              lambda e, o=o, db=db: [e.dma_start(
                  out=wbuf[:, db], in_=win_d[:, :, o * 128:(o + 1) * 128])],
              writes=[f"l0a_wb:{db}"], dma=True, tag=f"l0a_ldw{o}")

        def pmm(e, db=db):
            last = None
            for k in range(KT):
                for c in range(TC):
                    sl = slice(c * 512, (c + 1) * 512)
                    last = e.matmul(psA[:, sl], wbuf[:, db, k, :],
                                    xln[:, k, sl],
                                    start=(k == 0), stop=(k == KT - 1))
            return last
        fw.op("tensor", pmm, reads=[f"l0a_wb:{db}"] + xkeys,
              writes=_akeys(), tag=f"l0a_pmm{o}")
        if o < 24:
            fw.op("scalar",
                  lambda e, db=db: e.copy(zw[:, db, :], psA[:, :]),
                  reads=_akeys(), writes=[f"l0a_zw:{db}"],
                  tag=f"l0a_qkve{o}")

            def conv(e, db=db, o=o):
                e.tensor_scalar(accv[:, :], zw[:, db, :],
                                b["w_l0a_cw"][:, o, 3:4], None, OP.mult)
                e.drain()
                for tau in range(3):
                    off = 3 - tau
                    e.scalar_tensor_tensor(
                        accv[:, off:T], zw[:, db, 0:T - off],
                        b["w_l0a_cw"][:, o, tau:tau + 1], accv[:, off:T],
                        OP.mult, OP.add)
                    e.drain()
                return e.engine_nop()
            fw.op("vector", conv, reads=[f"l0a_zw:{db}", "w_l0a_cw"],
                  writes=["l0a_acc"], tag=f"l0a_cv{o}")
            fw.op("scalar",
                  lambda e, db=db: e.activation(xcw[:, db, :], accv[:, :],
                                                AF.Silu),
                  reads=["l0a_acc"], writes=[f"l0a_xcw:{db}"],
                  tag=f"l0a_si{o}")
            if o < 16:
                fw.op("scalar",
                      lambda e, db=db: e.activation(sqw[:, :],
                                                    xcw[:, db, :],
                                                    AF.Square),
                      reads=[f"l0a_xcw:{db}"], writes=["l0a_sqw"],
                      tag=f"l0a_sq{o}")

                def ssum(e):
                    last = None
                    for cq in range(TC):
                        sl = slice(cq * 512, (cq + 1) * 512)
                        last = e.matmul(psB[0:2, sl], sel2T[:, :],
                                        sqw[:, sl], start=True, stop=True)
                    return last
                fw.op("tensor", ssum, reads=["l0a_sqw", "c_sel2T"],
                      writes=_bkeys(), tag=f"l0a_ss{o}")
                sp2 = b["rowsA"][0:2, :]
                fw.op("vector",
                      lambda e, sp2=sp2: e.tensor_scalar(
                          sp2, psB[0:2, :], 1e-6, None, OP.add),
                      reads=_bkeys(), writes=["rowA0"], tag=f"l0a_se{o}")
                fw.op("scalar",
                      lambda e, sp2=sp2: e.activation(sp2, sp2, AF.Ln),
                      reads=["rowA0"], writes=["rowA0"],
                      tag=f"l0a_sr{o}")
                fw.op("scalar",
                      lambda e, sp2=sp2: e.activation(sp2, sp2, AF.Exp,
                                                      scale=-0.5),
                      reads=["rowA0"], writes=["rowA0"],
                      tag=f"l0a_sc{o}")
                if o < 8:
                    fw.op("vector",
                          lambda e, sp2=sp2: e.tensor_scalar(
                              sp2, sp2, 0.125, None, OP.mult),
                          reads=["rowA0"], writes=["rowA0"],
                          tag=f"l0a_sh{o}")

                def nbc(e, sp2=sp2):
                    last = None
                    for cq in range(TC):
                        sl = slice(cq * 512, (cq + 1) * 512)
                        last = e.matmul(psB[:, sl], sel2[:, :],
                                        sp2[:, sl], start=True,
                                        stop=True)
                    return last
                fw.op("tensor", nbc, reads=["rowA0", "c_sel2"],
                      writes=_bkeys(), tag=f"l0a_nb{o}")
                fw.op("vector",
                      lambda e, db=db: e.tensor_tensor(xcw[:, db, :],
                                                       xcw[:, db, :],
                                                       psB[:, :], OP.mult),
                      reads=[f"l0a_xcw:{db}"] + _bkeys(),
                      writes=[f"l0a_xcw:{db}"], tag=f"l0a_nm{o}")
            fw.op("sync",
                  lambda e, o=o, db=db: [e.dma_start(
                      out=qkvstage[2 * o:2 * o + 2].rearrange(
                          "a d t -> (a d) t"),
                      in_=xcw[:, db, :])],
                  reads=[f"l0a_xcw:{db}"], writes=[f"qst:{o}"], dma=True,
                  tag=f"l0a_qst{o}")
        else:
            fw.op("scalar",
                  lambda e, db=db: e.copy(xcw[:, db, :], psA[:, :]),
                  reads=_akeys(), writes=[f"l0a_xcw:{db}"],
                  tag=f"l0a_ze{o}")
            fw.op("sync",
                  lambda e, o=o, db=db: [e.dma_start(
                      out=zstage[2 * (o - 24):2 * (o - 24) + 2].rearrange(
                          "a d t -> (a d) t"),
                      in_=xcw[:, db, :])],
                  reads=[f"l0a_xcw:{db}"], writes=[f"zst:{o - 24}"],
                  dma=True, tag=f"l0a_zst{o}")

    brow = lnF[0:16, 0:2048]
    arow = lnF[0:16, 2048:4096]
    spr = lnF[0:16, 4096:6144]
    eur = lnB[0:16, 12288:14336]
    gamr = lnB[0:16, 14336:16384]
    fw.alias(["l0a_br", "l0a_ar", "l0a_spr", "l0a_eur", "l0a_gam"],
             ["l0a_acc", "l0a_zw:0", "l0a_zw:1", "l0a_xcw:0",
              "l0a_xcw:1", "l0a_sqw"])
    wbb = b.S("l0a_wbb", [128, KT, 48], BF16)
    fw.op("gpsimd", lambda e: [e.dma_start(out=wbb[:], in_=wba_d)],
          writes=["l0a_wbb"], dma=True, tag="l0a_ldba")

    def bamm(e):
        last = None
        for k in range(KT):
            for cq in range(TC):
                sl = slice(cq * 512, (cq + 1) * 512)
                last = e.matmul(psA[0:48, sl], wbb[:, k, :], xln[:, k, sl],
                                start=(k == 0), stop=(k == KT - 1))
        return last
    fw.op("tensor", bamm, reads=["l0a_wbb"] + xkeys, writes=_akeys(),
          tag="l0a_bamm")
    fw.op("vector", lambda e: e.tensor_copy(brow[:, :], psA[0:16, :]),
          reads=_akeys(), writes=["l0a_br"], tag="l0a_bre")
    fw.op("vector", lambda e: e.tensor_copy(arow[:, :], psA[32:48, :]),
          reads=_akeys(), writes=["l0a_ar"], tag="l0a_are")
    fw.op("scalar", lambda e: e.activation(betr[:, :], brow[:, :],
                                           AF.Sigmoid),
          reads=["l0a_br"], writes=["l0a_betr"], tag="l0a_beta")

    def softplus(srcap, srckey, dstap, dstkey, tagp):
        fw.op("scalar", lambda e: e.activation(spr[:, :], srcap, AF.Abs),
              reads=[srckey], writes=["l0a_spr"], tag=tagp + "1")
        fw.op("scalar", lambda e: e.activation(spr[:, :], spr[:, :],
                                               AF.Exp, scale=-1.0),
              reads=["l0a_spr"], writes=["l0a_spr"], tag=tagp + "2")
        fw.op("scalar",
              lambda e: e.activation(spr[:, :], spr[:, :], AF.Ln,
                                     bias=b["c_onescol"][0:16, :]),
              reads=["l0a_spr", "c_onescol"], writes=["l0a_spr"],
              tag=tagp + "3")
        fw.op("vector",
              lambda e: e.tensor_scalar(dstap, srcap, 0.0, None, OP.max),
              reads=[srckey], writes=[dstkey], tag=tagp + "4")
        fw.op("vector",
              lambda e: e.tensor_tensor(dstap, dstap, spr[:, :], OP.add),
              reads=[dstkey, "l0a_spr"], writes=[dstkey], tag=tagp + "5")

    fw.op("vector",
          lambda e: e.tensor_scalar(arow[:, :], arow[:, :],
                                    b["w_l0a_dtb"][:, :], None, OP.add),
          reads=["l0a_ar", "w_l0a_dtb"], writes=["l0a_ar"], tag="l0a_adtb")
    fw.op("vector",
          lambda e: e.tensor_scalar(nA[:, :], b["w_l0a_alog"][:, :], -2.3,
                                    None, OP.max),
          reads=["w_l0a_alog"], writes=["l0a_nA"], tag="l0a_amax")
    fw.op("scalar", lambda e: e.activation(nA[:, :], nA[:, :], AF.Exp),
          reads=["l0a_nA"], writes=["l0a_nA"], tag="l0a_aexp")
    fw.op("vector",
          lambda e: e.tensor_scalar(nA[:, :], nA[:, :], -1.0, None,
                                    OP.mult),
          reads=["l0a_nA"], writes=["l0a_nA"], tag="l0a_aneg")
    softplus(arow[:, :], "l0a_ar", cumg[:, :], "l0a_cgr", "l0a_spa")
    fw.op("vector",
          lambda e: e.tensor_scalar(cumg[:, :], cumg[:, :], nA[:, :],
                                    None, OP.mult),
          reads=["l0a_cgr", "l0a_nA"], writes=["l0a_cgr"], tag="l0a_g")

    def scans(e):
        last = None
        for cq in range(NCH):
            sl = slice(cq * C_, (cq + 1) * C_)
            last = e.tensor_tensor_scan(cumg[:, sl], cumg[:, sl],
                                        cumg[:, sl], 0.0, OP.add,
                                        OP.bypass)
        return last
    fw.op("vector", scans, reads=["l0a_cgr"], writes=["l0a_cgr"],
          tag="l0a_scan")
    softplus(brow[:, :], "l0a_br", urow[:, :], "l0a_ur", "l0a_spb")
    fw.op("vector",
          lambda e: e.tensor_tensor(urow[:, :], brow[:, :], urow[:, :],
                                    OP.subtract),
          reads=["l0a_br", "l0a_ur"], writes=["l0a_ur"], tag="l0a_lnb")
    fw.op("vector",
          lambda e: e.tensor_tensor(urow[:, :], urow[:, :], cumg[:, :],
                                    OP.add),
          reads=["l0a_ur", "l0a_cgr"], writes=["l0a_ur"], tag="l0a_u")
    fw.op("scalar", lambda e: e.activation(eur[:, :], urow[:, :], AF.Exp),
          reads=["l0a_ur"], writes=["l0a_eur"], tag="l0a_eu")
    fw.op("scalar", lambda e: e.activation(gamr[:, :], cumg[:, :], AF.Exp),
          reads=["l0a_cgr"], writes=["l0a_gam"], tag="l0a_ga")

    def dels(e):
        last = None
        for cq in range(NCH):
            sl = slice(cq * C_, (cq + 1) * C_)
            last = e.activation(delr[:, sl], cumg[:, sl], AF.Exp,
                                scale=-1.0,
                                bias=cumg[:, cq * C_ + C_ - 1:cq * C_ + C_])
        return last
    fw.op("scalar", dels, reads=["l0a_cgr"], writes=["l0a_del"],
          tag="l0a_de")
    fw.op("sync", lambda e: [e.dma_start(out=cgstage, in_=cumg[:, :])],
          reads=["l0a_cgr"], writes=["cgst"], dma=True, tag="l0a_cgst")
    fw.op("sync", lambda e: [e.dma_start(out=ustage, in_=urow[:, :])],
          reads=["l0a_ur"], writes=["ust"], dma=True, tag="l0a_ust")
    fw.op("sync", lambda e: [e.dma_start(out=rbstage[0], in_=eur[:, :])],
          reads=["l0a_eur"], writes=["rbst"], dma=True, tag="l0a_rb0")
    fw.op("sync", lambda e: [e.dma_start(out=rbstage[1], in_=gamr[:, :])],
          reads=["l0a_gam"], writes=["rbst"], dma=True, tag="l0a_rb1")
    fw.op("sync", lambda e: [e.dma_start(out=rbstage[2], in_=delr[:, :])],
          reads=["l0a_del"], writes=["rbst"], dma=True, tag="l0a_rb2")
    fw.op("vector", lambda e: e.memset(S0[:, :], 0.0), writes=["l0a_S0"],
          tag="l0a_S00")
    fw.op("vector", lambda e: e.memset(S0b[:, :], 0.0), writes=["l0a_S0b"],
          tag="l0a_S0b0")

    fw.alias(["l0a_qc", "l0a_kc", "l0a_vc", "l0a_euBC", "l0a_gaBC",
              "l0a_deBC", "l0a_Vtok", "l0a_Ktok", "l0a_zcl"],
             [f"xln:{k}" for k in range(KT)])
    wo = b["w_l0a_wo"]
    xi = p.x_src_ap
    xo = p.x_dst_ap

    def hbc(stage, c, nt=C_):
        return stage[:, c * C_:c * C_ + nt].unsqueeze(0) \
            .to_broadcast([64, 16, nt])
    for c in range(NCH):
        csl = slice(c * C_, (c + 1) * C_)
        fw.op("gpsimd",
              lambda e, c=c, csl=csl: [
                  e.dma_start(out=qc, in_=qkvstage[0:16, :, csl]
                              .rearrange("h d t -> d h t")),
                  e.dma_start(out=kc, in_=qkvstage[16:32, :, csl]
                              .rearrange("h d t -> d h t")),
                  e.dma_start(out=vc, in_=qkvstage[32:48, :, csl]
                              .rearrange("h d t -> d h t"))],
              reads=[f"qst:{o}" for o in range(24)],
              writes=["l0a_qc", "l0a_kc", "l0a_vc"], dma=True, n_dma=3,
              tag=f"l0a_ldqkv{c}")
        fw.op("gpsimd",
              lambda e, c=c, csl=csl: [
                  e.dma_start(out=cgBC, in_=cgstage[:, csl].unsqueeze(0)
                              .to_broadcast([128, 16, C_])),
                  e.dma_start(out=uBC, in_=ustage[:, csl].unsqueeze(0)
                              .to_broadcast([128, 16, C_]))],
              reads=["cgst", "ust", "l0a_carve"],
              writes=["l0a_cgBC", "l0a_uBC"], dma=True, n_dma=2,
              tag=f"l0a_ldbc{c}")
        fw.op("gpsimd",
              lambda e, c=c: [
                  e.dma_start(out=euBC, in_=hbc(rbstage[0], c)),
                  e.dma_start(out=gaBC, in_=hbc(rbstage[1], c)),
                  e.dma_start(out=deBC, in_=hbc(rbstage[2], c))],
              reads=["rbst"], writes=["l0a_euBC", "l0a_gaBC", "l0a_deBC"],
              dma=True, n_dma=3, tag=f"l0a_ldrb{c}")

        fw.op("tensor",
              lambda e, csl=csl: e.transpose(psB[:, 0:16], cumg[:, csl],
                                             ident[0:16, 0:16]),
              reads=["l0a_cgr", "c_ident"], writes=["B0"],
              tag=f"l0a_cgt{c}")
        fw.op("vector", lambda e: e.tensor_copy(cgT[:, :], psB[:, 0:16]),
              reads=["B0"], writes=["l0a_cgT"], tag=f"l0a_cgtc{c}")
        fw.op("tensor",
              lambda e, csl=csl: e.transpose(psB[:, 16:32], urow[:, csl],
                                             ident[0:16, 0:16]),
              reads=["l0a_ur", "c_ident"], writes=["B0"],
              tag=f"l0a_ut{c}")
        fw.op("vector", lambda e: e.tensor_copy(uT[:, :], psB[:, 16:32]),
              reads=["B0"], writes=["l0a_uT"], tag=f"l0a_utc{c}")
        fw.op("vector",
              lambda e, csl=csl: e.tensor_copy(dl_bf[:, :], betr[:, csl]),
              reads=["l0a_betr"], writes=["l0a_dbf"], tag=f"l0a_bb{c}")
        fw.op("tensor",
              lambda e: e.transpose(psB.bitcast(BF16)[:, 64:80],
                                    dl_bf[:, :], identb[0:16, 0:16]),
              reads=["l0a_dbf", "c_identb"], writes=["B0"],
              tag=f"l0a_bt{c}")
        fw.op("vector",
              lambda e: e.tensor_copy(beT[:, :],
                                      psB.bitcast(BF16)[:, 64:80]),
              reads=["B0"], writes=["l0a_beT"], tag=f"l0a_btc{c}")
        fw.op("vector",
              lambda e, csl=csl: e.tensor_copy(dl_bf[:, :], delr[:, csl]),
              reads=["l0a_del"], writes=["l0a_dbf"], tag=f"l0a_db{c}")
        fw.op("tensor",
              lambda e: e.transpose(psB.bitcast(BF16)[:, 96:112],
                                    dl_bf[:, :], identb[0:16, 0:16]),
              reads=["l0a_dbf", "c_identb"], writes=["B0"],
              tag=f"l0a_dt{c}")
        fw.op("vector",
              lambda e: e.tensor_copy(deT[:, :],
                                      psB.bitcast(BF16)[:, 96:112]),
              reads=["B0"], writes=["l0a_deT"], tag=f"l0a_dtc{c}")
        def kkmm(e):
            last = None
            for h in range(16):
                last = e.matmul(psA[:, h * 128:(h + 1) * 128],
                                kc[:, h, :], kc[:, h, :],
                                start=True, stop=True)
            return last
        fw.op("tensor", kkmm, reads=["l0a_kc"], writes=_akeys(),
              tag=f"l0a_kk{c}")
        fw.op("scalar",
              lambda e: e.activation(nKKl[:, :], psA[:, :], AF.Identity,
                                     scale=-1.0),
              reads=_akeys(), writes=["l0a_nKKl"], tag=f"l0a_nkk{c}")

        def qkmm(e):
            last = None
            for h in range(16):
                last = e.matmul(psA[:, h * 128:(h + 1) * 128],
                                kc[:, h, :], qc[:, h, :],
                                start=True, stop=True)
            return last
        fw.op("tensor", qkmm, reads=["l0a_kc", "l0a_qc"], writes=_akeys(),
              tag=f"l0a_qk{c}")
        fw.op("vector",
              lambda e: e.tensor_tensor(
                  QKm[:, :].rearrange("p (h i) -> p h i", h=16),
                  psA[:, :].rearrange("p (h i) -> p h i", h=16),
                  mli[:, :].unsqueeze(1).to_broadcast([128, 16, 128]),
                  OP.mult),
              reads=_akeys() + ["c_l0a_mli"], writes=["l0a_QKm"],
              tag=f"l0a_qkm{c}")
        fw.op("vector",
              lambda e: e.tensor_tensor(
                  nKKu[:, :].rearrange("p (h i) -> p h i", h=16),
                  nKKl[:, :].rearrange("p (h i) -> p h i", h=16),
                  msu[:, :].unsqueeze(1).to_broadcast([128, 16, 128]),
                  OP.mult),
              reads=["l0a_nKKl", "c_l0a_msu"], writes=["l0a_nKKu"],
              tag=f"l0a_nku{c}")
        fw.op("vector",
              lambda e: e.tensor_tensor(
                  nKKl[:, :].rearrange("p (h i) -> p h i", h=16),
                  nKKl[:, :].rearrange("p (h i) -> p h i", h=16),
                  msl[:, :].unsqueeze(1).to_broadcast([128, 16, 128]),
                  OP.mult),
              reads=["l0a_nKKl", "c_l0a_msl"], writes=["l0a_nKKl"],
              tag=f"l0a_nkl{c}")
        fw.op("vector",
              lambda e: e.scalar_tensor_tensor(
                  cgBC[:, :, :], cgBC[:, :, :], -1.0,
                  uT[:, :].unsqueeze(2).to_broadcast([128, 16, C_]),
                  OP.mult, OP.add),
              reads=["l0a_cgBC", "l0a_uT"], writes=["l0a_cgBC"],
              tag=f"l0a_g0a{c}")
        fw.op("vector",
              lambda e: e.tensor_scalar(cgBC[:, :, :], cgBC[:, :, :], 0.0,
                                        None, OP.min),
              reads=["l0a_cgBC"], writes=["l0a_cgBC"], tag=f"l0a_g0b{c}")
        fw.op("scalar",
              lambda e: e.activation(EG[:, :], xF[:, 0:2048], AF.Exp),
              reads=["l0a_cgBC"], writes=["l0a_EG"], tag=f"l0a_g0e{c}")
        fw.op("vector",
              lambda e: e.tensor_tensor(Gm[:, :], EG[:, :], nKKl[:, :],
                                        OP.mult),
              reads=["l0a_EG", "l0a_nKKl"], writes=["l0a_G"],
              tag=f"l0a_g0{c}")
        fw.op("vector",
              lambda e: e.scalar_tensor_tensor(
                  uBC[:, :, :],
                  cgT[:, :].unsqueeze(2).to_broadcast([128, 16, C_]),
                  -1.0, uBC[:, :, :], OP.mult, OP.add),
              reads=["l0a_uBC", "l0a_cgT"], writes=["l0a_uBC"],
              tag=f"l0a_m0a{c}")
        fw.op("vector",
              lambda e: e.tensor_scalar(uBC[:, :, :], uBC[:, :, :], 0.0,
                                        None, OP.min),
              reads=["l0a_uBC"], writes=["l0a_uBC"], tag=f"l0a_m0b{c}")
        fw.op("scalar",
              lambda e: e.activation(EG[:, :], xF[:, 2048:4096], AF.Exp),
              reads=["l0a_uBC"], writes=["l0a_EG"], tag=f"l0a_m0e{c}")
        fw.op("vector",
              lambda e: e.tensor_tensor(Mm[:, :], EG[:, :], nKKu[:, :],
                                        OP.mult),
              reads=["l0a_EG", "l0a_nKKu"], writes=["l0a_M"],
              tag=f"l0a_m0{c}")
        fw.op("gpsimd",
              lambda e, c=c, csl=csl: [e.dma_start(
                  out=cgBC, in_=cgstage[:, csl].unsqueeze(0)
                  .to_broadcast([128, 16, C_]))],
              reads=["cgst"], writes=["l0a_cgBC"], dma=True,
              tag=f"l0a_ldbc2{c}")
        fw.op("vector",
              lambda e: e.scalar_tensor_tensor(
                  cgBC[:, :, :],
                  cgT[:, :].unsqueeze(2).to_broadcast([128, 16, C_]),
                  -1.0, cgBC[:, :, :], OP.mult, OP.add),
              reads=["l0a_cgBC", "l0a_cgT"], writes=["l0a_cgBC"],
              tag=f"l0a_sca{c}")
        fw.op("vector",
              lambda e: e.tensor_scalar(cgBC[:, :, :], cgBC[:, :, :], 0.0,
                                        None, OP.min),
              reads=["l0a_cgBC"], writes=["l0a_cgBC"], tag=f"l0a_scb{c}")
        fw.op("scalar",
              lambda e: e.activation(EG[:, :], xF[:, 0:2048], AF.Exp),
              reads=["l0a_cgBC"], writes=["l0a_EG"], tag=f"l0a_sce{c}")
        fw.op("vector",
              lambda e: e.tensor_tensor(SCt[:, :], EG[:, :], QKm[:, :],
                                        OP.mult),
              reads=["l0a_EG", "l0a_QKm"], writes=["l0a_SCt"],
              tag=f"l0a_sc{c}")
        fw.op("vector",
              lambda e: e.scalar_tensor_tensor(
                  Pm[:, :].rearrange("p (h i) -> p h i", h=16),
                  identb[:, :].unsqueeze(1).to_broadcast([128, 16, 128]),
                  1.0,
                  Mm[:, :].rearrange("p (h i) -> p h i", h=16),
                  OP.mult, OP.add),
              reads=["l0a_M", "c_identb"], writes=["l0a_P"],
              tag=f"l0a_p0{c}")
        for it in range(6):
            def sq(e):
                last = None
                for h in range(16):
                    hs = slice(h * 128, (h + 1) * 128)
                    e.matmul(psA[:, hs], Mm[:, hs], Gm[:, hs],
                             start=True, stop=True)
                    last = e.matmul(psB[:, hs], Gm[:, hs], Mm[:, hs],
                                    start=True, stop=True)
                return last
            fw.op("tensor", sq, reads=["l0a_G", "l0a_M"],
                  writes=_akeys() + _bkeys(), tag=f"l0a_sq{c}_{it}")
            fw.op("vector", lambda e: e.tensor_copy(Gm[:, :], psA[:, :]),
                  reads=_akeys(), writes=["l0a_G"], tag=f"l0a_ge{c}_{it}")
            fw.op("vector", lambda e: e.tensor_copy(Mm[:, :], psB[:, :]),
                  reads=_bkeys(), writes=["l0a_M"], tag=f"l0a_me{c}_{it}")

            def pup(e):
                last = None
                for h in range(16):
                    hs = slice(h * 128, (h + 1) * 128)
                    e.matmul(psA[:, hs], identb[:, :], Pm[:, hs],
                             start=True, stop=False)
                    last = e.matmul(psA[:, hs], Gm[:, hs], Pm[:, hs],
                                    start=False, stop=True)
                return last
            fw.op("tensor", pup, reads=["l0a_G", "l0a_P", "c_identb"],
                  writes=_akeys(), tag=f"l0a_pu{c}_{it}")
            fw.op("vector", lambda e: e.tensor_copy(Pm[:, :], psA[:, :]),
                  reads=_akeys(), writes=["l0a_P"], tag=f"l0a_pe{c}_{it}")
        def vtr(e):
            last = None
            pab = psA.bitcast(BF16)
            for h in range(16):
                last = e.transpose(pab[:, h * 64:(h + 1) * 64],
                                   vc[:, h, :], identb[0:64, 0:64])
            return last
        fw.op("tensor", vtr, reads=["l0a_vc", "c_identb"],
              writes=["A0", "A1"], tag=f"l0a_vt{c}")
        fw.op("vector",
              lambda e: e.tensor_copy(Vtok2[:, :],
                                      psA.bitcast(BF16)[:, 0:1024]),
              reads=["A0", "A1"], writes=["l0a_Vtok"], tag=f"l0a_vtc{c}")

        def ktr(e):
            last = None
            pab = psA.bitcast(BF16)
            for h in range(16):
                last = e.transpose(pab[:, 1024 + h * 64:
                                       1024 + (h + 1) * 64],
                                   kc[:, h, :], identb[0:64, 0:64])
            return last
        fw.op("tensor", ktr, reads=["l0a_kc", "c_identb"],
              writes=["A2", "A3"], tag=f"l0a_kt{c}")
        fw.op("vector",
              lambda e: e.tensor_copy(Ktok2[:, :],
                                      psA.bitcast(BF16)[:, 1024:2048]),
              reads=["A2", "A3"], writes=["l0a_Ktok"], tag=f"l0a_ktc{c}")
        fw.op("vector",
              lambda e: e.tensor_tensor(kc[:, :, :], kc[:, :, :],
                                        euBC[:, :, :], OP.mult),
              reads=["l0a_kc", "l0a_euBC"], writes=["l0a_kc"],
              tag=f"l0a_kbg{c}")
        fw.op("vector",
              lambda e: e.tensor_tensor(qc[:, :, :], qc[:, :, :],
                                        gaBC[:, :, :], OP.mult),
              reads=["l0a_qc", "l0a_gaBC"], writes=["l0a_qc"],
              tag=f"l0a_qga{c}")
        fw.op("vector",
              lambda e: e.tensor_tensor(
                  Ktok2[:, :].rearrange("p (h v) -> p h v", h=16),
                  Ktok2[:, :].rearrange("p (h v) -> p h v", h=16),
                  deT[:, :].unsqueeze(2).to_broadcast([128, 16, 64]),
                  OP.mult),
              reads=["l0a_Ktok", "l0a_deT"], writes=["l0a_Ktok"],
              tag=f"l0a_kd{c}")
        fw.op("vector",
              lambda e: e.tensor_tensor(
                  EG[:, :].rearrange("p (h i) -> p h i", h=16),
                  Pm[:, :].rearrange("p (h i) -> p h i", h=16),
                  beT[:, :].unsqueeze(2).to_broadcast([128, 16, 128]),
                  OP.mult),
              reads=["l0a_P", "l0a_beT"], writes=["l0a_EG"],
              tag=f"l0a_pb{c}")
        def a2mm(e):
            last = None
            for h in range(16):
                last = e.matmul(psB[:, h * 64:(h + 1) * 64],
                                kc[:, h, :],
                                S0b[:, h * 64:(h + 1) * 64],
                                start=True, stop=True)
            return last
        fw.op("tensor", a2mm, reads=["l0a_kc", "l0a_S0b"],
              writes=_bkeys(), tag=f"l0a_a2{c}")
        fw.op("scalar",
              lambda e: e.activation(t2n[:, :], psB[:, 0:1024],
                                     AF.Identity, scale=-1.0),
              reads=_bkeys(), writes=["l0a_t2n"], tag=f"l0a_t2n{c}")
        def wmm(e):
            last = None
            for h in range(16):
                hs = slice(h * 128, (h + 1) * 128)
                vs = slice(h * 64, (h + 1) * 64)
                e.matmul(psB[:, vs], EG[:, hs], Vtok2[:, vs],
                         start=True, stop=False)
                last = e.matmul(psB[:, vs], Pm[:, hs], t2n[:, vs],
                                start=False, stop=True)
            return last
        fw.op("tensor", wmm,
              reads=["l0a_EG", "l0a_P", "l0a_Vtok", "l0a_t2n"],
              writes=_bkeys(), tag=f"l0a_w{c}")
        fw.op("vector", lambda e: e.tensor_copy(Wtok[:, :], psB[:, 0:1024]),
              reads=_bkeys(), writes=["l0a_Wtok"], tag=f"l0a_we{c}")
        def omm(e):
            last = None
            for h in range(16):
                hs = slice(h * 128, (h + 1) * 128)
                vs = slice(h * 64, (h + 1) * 64)
                e.matmul(psA[:, vs], qc[:, h, :], S0b[:, vs],
                         start=True, stop=False)
                last = e.matmul(psA[:, vs], SCt[:, hs], Wtok[:, vs],
                                start=False, stop=True)
            return last
        fw.op("tensor", omm,
              reads=["l0a_qc", "l0a_S0b", "l0a_SCt", "l0a_Wtok"],
              writes=_akeys(), tag=f"l0a_o{c}")
        fw.op("vector", lambda e: e.tensor_copy(Oe[:, :], psA[:, 0:1024]),
              reads=_akeys(), writes=["l0a_Oe"], tag=f"l0a_oe{c}")
        def smm(e):
            last = None
            for h in range(16):
                vs = slice(h * 64, (h + 1) * 64)
                last = e.matmul(psB[0:64, vs], Ktok2[:, vs], Wtok[:, vs],
                                start=True, stop=True)
            return last
        fw.op("tensor", smm, reads=["l0a_Ktok", "l0a_Wtok"],
              writes=_bkeys(), tag=f"l0a_s{c}")
        fw.op("vector",
              lambda e: e.tensor_tensor(
                  S0[:, :].rearrange("p (h v) -> p h v", h=16),
                  S0[:, :].rearrange("p (h v) -> p h v", h=16),
                  gaBC[:, :, C_ - 1:C_].to_broadcast([64, 16, 64]),
                  OP.mult),
              reads=["l0a_S0", "l0a_gaBC"], writes=["l0a_S0"],
              tag=f"l0a_su1{c}")
        fw.op("vector",
              lambda e: e.tensor_tensor(S0[:, :], S0[:, :],
                                        psB[0:64, 0:1024], OP.add),
              reads=["l0a_S0"] + _bkeys(), writes=["l0a_S0"],
              tag=f"l0a_su2{c}")
        fw.op("vector", lambda e: e.tensor_copy(S0b[:, :], S0[:, :]),
              reads=["l0a_S0"], writes=["l0a_S0b"], tag=f"l0a_sb{c}")
        fw.op("scalar", lambda e: e.activation(sq2[:, :], Oe[:, :],
                                               AF.Square),
              reads=["l0a_Oe"], writes=["l0a_sq2"], tag=f"l0a_osq{c}")
        fw.op("vector",
              lambda e: e.tensor_reduce(
                  rr[:, :], sq2[:, :].rearrange("p (h v) -> p h v", h=16),
                  AX.X, OP.add),
              reads=["l0a_sq2"], writes=["l0a_rr"], tag=f"l0a_ors{c}")
        fw.op("vector",
              lambda e: e.tensor_scalar(rr[:, :], rr[:, :], 1.0 / 64.0,
                                        1e-6, OP.mult, OP.add),
              reads=["l0a_rr"], writes=["l0a_rr"], tag=f"l0a_orm{c}")
        fw.op("scalar", lambda e: e.activation(rr[:, :], rr[:, :],
                                               AF.Ln),
              reads=["l0a_rr"], writes=["l0a_rr"], tag=f"l0a_orq{c}")
        fw.op("scalar", lambda e: e.activation(rr[:, :], rr[:, :],
                                               AF.Exp, scale=-0.5),
              reads=["l0a_rr"], writes=["l0a_rr"], tag=f"l0a_orr{c}")
        fw.op("vector",
              lambda e: e.tensor_tensor(
                  Oe[:, :].rearrange("p (h v) -> p h v", h=16),
                  Oe[:, :].rearrange("p (h v) -> p h v", h=16),
                  rr[:, :].unsqueeze(2).to_broadcast([128, 16, 64]),
                  OP.mult),
              reads=["l0a_Oe", "l0a_rr"], writes=["l0a_Oe"],
              tag=f"l0a_on{c}")
        fw.op("vector",
              lambda e: e.tensor_tensor(
                  Oe[:, :].rearrange("p (h v) -> p h v", h=16),
                  Oe[:, :].rearrange("p (h v) -> p h v", h=16),
                  nwbc[:, :].unsqueeze(1).to_broadcast([128, 16, 64]),
                  OP.mult),
              reads=["l0a_Oe", "l0a_nwbc"], writes=["l0a_Oe"],
              tag=f"l0a_ow{c}")
        zcl = xlnB[0:64, 14336:16384].rearrange("p (h t) -> p h t", h=16)
        fw.op("gpsimd",
              lambda e, csl=csl: [e.dma_start(
                  out=zcl[:], in_=zstage[:, :, csl]
                  .rearrange("h d t -> d h t"))],
              reads=[f"zst:{o}" for o in range(8)],
              writes=["l0a_zcl"], dma=True, tag=f"l0a_ldzc{c}")

        def ztr(e):
            last = None
            pab = psA.bitcast(BF16)
            for h in range(16):
                last = e.transpose(pab[:, h * 64:(h + 1) * 64],
                                   zcl[:, h, :], identb[0:64, 0:64])
            return last
        fw.op("tensor", ztr, reads=["l0a_zcl", "c_identb"],
              writes=["A0", "A1"], tag=f"l0a_zt{c}")
        fw.op("scalar",
              lambda e: e.activation(ztk[:, :],
                                     psA.bitcast(BF16)[:, 0:1024],
                                     AF.Silu),
              reads=["A0", "A1"], writes=["l0a_ztk"], tag=f"l0a_zs{c}")
        fw.op("vector",
              lambda e: e.tensor_tensor(Oe[:, :], Oe[:, :], ztk[:, :],
                                        OP.mult),
              reads=["l0a_Oe", "l0a_ztk"], writes=["l0a_Oe"],
              tag=f"l0a_og{c}")
        def ytr(e):
            last = None
            pab = psB.bitcast(BF16)
            for j in range(8):
                last = e.transpose(pab[:, j * 128:(j + 1) * 128],
                                   Oe[:, j * 128:(j + 1) * 128],
                                   identb[:, :])
            return last
        fw.op("tensor", ytr, reads=["l0a_Oe", "c_identb"],
              writes=_bkeys(), tag=f"l0a_yt{c}")
        fw.op("vector",
              lambda e: e.tensor_copy(yF[:, :],
                                      psB.bitcast(BF16)[:, 0:1024]),
              reads=_bkeys(), writes=["l0a_yF"], tag=f"l0a_ytc{c}")
        def prmm(e):
            last = None
            for o in range(KT):
                osl = slice(o * 128, (o + 1) * 128)
                for kv in range(KT):
                    last = e.matmul(psA[:, osl],
                                    wo[:, kv, o * 128:(o + 1) * 128],
                                    yF[:, :].rearrange(
                                        "p (kk t) -> p kk t", kk=8)
                                    [:, kv, :],
                                    start=(kv == 0), stop=(kv == KT - 1))
            return last
        fw.op("tensor", prmm, reads=["w_l0a_wo", "l0a_yF"],
              writes=_akeys(), tag=f"l0a_pr{c}")
        fw.op("sync",
              lambda e, c=c: [e.dma_start(
                  out=xrow[:], in_=xi[:, :, c * C_:(c + 1) * C_])],
              writes=["l0a_xr"], dma=True, tag=f"l0a_ldx{c}")
        fw.op("vector",
              lambda e: e.tensor_tensor(
                  xrow[:].rearrange("p k t -> p (k t)"),
                  xrow[:].rearrange("p k t -> p (k t)"),
                  psA[:, 0:1024], OP.add),
              reads=["l0a_xr"] + _akeys(), writes=["l0a_xr"],
              tag=f"l0a_xa{c}")
        fw.op("sync",
              lambda e, c=c: [e.dma_start(
                  out=xo[:, :, c * C_:(c + 1) * C_], in_=xrow[:])],
              reads=["l0a_xr"], dma=True, tag=f"l0a_stx{c}")


def build_l0attn_block():
    p = Prog()
    p.load_consts()
    p.load_x()
    p.set_xio("x_in", "x_out")
    emit_ln(p, "ln0a")
    emit_l0attn(p)
    return p.finish()


def build_l1ffn_block():
    p = Prog()
    p.load_consts()
    p.load_x()
    emit_ln(p, "ln1f")
    emit_l1ffn(p)
    p.store_x()
    return p.finish()


def build_l2ffn_block():
    p = Prog()
    p.load_consts()
    p.load_x()
    emit_ln(p, "ln2f")
    emit_l2ffn(p)
    p.store_x()
    return p.finish()


def build_l1attn_block():
    p = Prog()
    p.load_consts()
    p.load_x()
    emit_ln(p, "ln1a")
    emit_l1attn(p)
    p.store_x()
    return p.finish()


def build_l0ffn_block():
    p = Prog()
    p.load_consts()
    p.load_x()
    emit_ln(p, "ln0f")
    emit_l0ffn(p)
    p.store_x()
    return p.finish()


HOST_PREP = {"l1ffn": host_prep_l1ffn, "l2ffn": host_prep_l2ffn,
             "l0ffn": host_prep_l0ffn, "l1attn": host_prep_l1attn,
             "l3ffn": host_prep_l3ffn, "l2attn": host_prep_l2attn,
             "l3attn": host_prep_l3attn, "l0attn": host_prep_l0attn}
BUILDERS = {"l1ffn": build_l1ffn_block, "l2ffn": build_l2ffn_block,
            "l0ffn": build_l0ffn_block, "l1attn": build_l1attn_block,
            "l3ffn": build_l3ffn_block, "l2attn": build_l2attn_block,
            "l3attn": build_l3attn_block, "l0attn": build_l0attn_block}


def run_block(block, inp, x_fm, sim=False):
    consts = consts_host()
    weights = HOST_PREP[block](inp)
    nc = BUILDERS[block]()
    in_maps = []
    for core in range(NCORES):
        m = dict(consts)
        m.update(weights)
        m["x_in"] = _f(x_fm[core])
        in_maps.append(m)
    if sim:
        from simrun import run_sim
        s = run_sim(nc, in_maps[0])
        return np.asarray(s.tensor("x_out"))[None]
    res = run_bass_kernel_spmd(nc, in_maps, core_ids=list(range(NCORES)))
    return np.stack([res.results[c]["x_out"] for c in range(NCORES)])


BLOCK_ORDER = ["l0attn", "l0ffn", "l1attn", "l1ffn", "l2attn", "l2ffn",
               "l3attn", "l3ffn"]


def build_mega():
    p = Prog()
    nc = p.nc
    rsem = p.ctx.enter_context(nc.semaphore("rdma_r"))
    lsem = p.ctx.enter_context(nc.semaphore("rdma_l"))
    p.load_consts()
    p.load_x()
    for nm, dn, shape in (
            ("l1f_dbf", "w_l1f_d", [KT, 128, 36, 128]),
            ("l2f_dbf", "w_l2f_d", [KT, 128, 72, 128]),
            ("l3a_wobf", "w_l3a_wout", None)):
        dst = p.dscratch(nm, shape or [128, 16, 1024], BF16)
        srcap = p.din(dn, shape or [128, 16, 1024])
        p.fw.op("gpsimd",
                lambda e, dst=dst, srcap=srcap: [e.dma_start(out=dst,
                                                             in_=srcap)],
                writes=[nm], dma=True, tag="stg_" + nm)

    p.begin_block("B0_l0attn")
    p.set_xio("x_in", "xh0")
    emit_ln(p, "ln0a")
    emit_l0attn(p)
    p.end_block()
    p.reload_x_from(
        p.dscratch("xh0", [D, T]).rearrange("(k p) t -> p k t", p=128))

    p.begin_block("B1_l0ffn")
    emit_ln(p, "ln0f")
    emit_l0ffn(p)
    p.end_block()

    p.begin_block("B2_l1attn")
    emit_ln(p, "ln1a")
    emit_l1attn(p)
    p.end_block()

    p.begin_block("B3_l1ffn")
    emit_ln(p, "ln1f")
    emit_l1ffn(p)
    p.end_block()
    p.store_x_to("xh1")

    p.begin_block("B4_l2attn")
    p.set_xio("xh1", "xh2")
    p.load("w_l2a_gam", [128, KT], dt=F32)
    p.load("w_l2a_bet", [128, KT], dt=F32)
    emit_ln(p, "ln2a", gamma="w_l2a_gam", beta="w_l2a_bet")
    emit_l2attn(p)
    p.end_block()
    p.reload_x_from(
        p.dscratch("xh2", [D, T]).rearrange("(k p) t -> p k t", p=128))

    p.begin_block("B5_l2ffn")
    emit_ln(p, "ln2f")
    emit_l2ffn(p)
    p.end_block()
    p.store_x_to("xh3i")

    a2a_in = p.dscratch("a2a_in", [8, D, 256])
    a2a_out = p.dscratch("a2a_out", [8, D, 256])
    p.begin_block("B6_l3attn")
    p.set_xio("xh3i",
              a2a_in.rearrange("s (k p) c -> p k s c", p=128))
    emit_ln(p, "ln3a")
    emit_l3attn(p)
    p.end_block()
    p.fw.raw_op(
        "gpsimd",
        lambda e: e.collective_compute(
            "AllToAll", mybir.AluOpType.bypass,
            ins=[a2a_in], outs=[a2a_out],
            replica_groups=[[0, 1, 2, 3, 4, 5, 6, 7]],
        ).then_inc(rsem, 1),
        tag="a2a")
    lane_view = a2a_out.rearrange("r (k p) c -> p k r c", p=128)
    p.reload_x_from(lane_view, extra_waits=[(rsem, 1)])

    p.begin_block("B7_l3ffn")
    p.set_xio(lane_view, "x_out")
    emit_ln(p, "ln3f")
    emit_l3ffn(p)
    p.end_block()
    return p.finish()


def kernel(**inputs):
    inp = {k: np.asarray(v) for k, v in inputs.items()}
    consts = consts_host()
    weights = {}
    for blk in BLOCK_ORDER:
        weights.update(HOST_PREP[blk](inp))
    x_fm = np.ascontiguousarray(
        np.asarray(inp["L0_attn_x"], dtype=np.float32).transpose(0, 2, 1))
    nc = build_mega()
    in_maps = []
    for core in range(NCORES):
        m = dict(consts)
        m.update(weights)
        m["x_in"] = _f(x_fm[core])
        in_maps.append(m)
    res = run_bass_kernel_spmd(nc, in_maps, core_ids=list(range(NCORES)))
    x_fm = np.stack([np.asarray(res.results[c]["x_out"])
                     for c in range(NCORES)])
    out = np.empty((8, T, D), dtype=np.float32)
    for i in range(NCORES):
        y = x_fm[i].reshape(D, 8, 256)
        for b_ in range(8):
            out[b_, 256 * i:256 * (i + 1), :] = y[:, b_, :].T
    return out
